# version 01
# baseline (speedup 1.0000x reference)
import numpy as np
import ml_dtypes

import concourse.bass as bass
import concourse.mybir as mybir
from concourse.bass_utils import run_bass_kernel_spmd

F32 = mybir.dt.float32
BF16 = mybir.dt.bfloat16
F16 = mybir.dt.float16
I32 = mybir.dt.int32
AF = mybir.ActivationFunctionType
ALU = mybir.AluOpType
AX = mybir.AxisListType

NCORES = 8
D = 4096
KC = 32
T = 2048
NTT = T // 128
EPS_LN = 1e-5

bfdt = ml_dtypes.bfloat16
AP = bass.AP



def _blk_fp32(w):
    O, I = w.shape
    M, K = O // 128, I // 128
    return np.ascontiguousarray(w.reshape(M, 128, K, 128).transpose(0, 3, 2, 1))


def _blk_l0q(qw):
    return np.ascontiguousarray(qw.reshape(32, 16, 32, 128).transpose(2, 1, 0, 3))


def _sblk(s):
    G, N = s.shape
    M = N // 128
    return np.ascontiguousarray(
        s.reshape(G, M, 128).transpose(1, 0, 2).reshape(M, G * 128))


def shard_x(x):
    out = []
    for cid in range(NCORES):
        b, q = cid // 4, cid % 4
        out.append(np.ascontiguousarray(x[b, q * T:(q + 1) * T, :]))
    return out


def _decays():
    HEADS, BS_, LI, NL = 32, 256, 1, 32
    base = 1.0 / 2.0 ** (8.0 / HEADS)
    exponent = np.arange(HEADS, dtype=np.float64) + 1.0
    factor = 1.0 - LI / (NL - 1 + 1e-05) + 1e-05
    slope = base ** exponent * factor
    r = np.arange(BS_, dtype=np.float64) + 1.0
    q_decay = np.exp(-slope[:, None] * r[None, :])
    k_decay = np.exp(-slope[:, None] * (BS_ - r)[None, :])
    diff = r[:, None] - r[None, :]
    diag = slope[:, None, None] * diff[None, :, :]
    dd = np.where(diag >= 0, np.exp(-diag), 0.0)
    bd = np.exp(-slope * BS_)
    return q_decay, k_decay, dd, bd


def host_prepare(inputs):
    c = {}
    c["ident"] = np.eye(128, dtype=np.float32)
    c["w0q_blk"] = _blk_l0q(inputs["L0_ffn_qweight"])
    c["qz0"] = np.ascontiguousarray(inputs["L0_ffn_qzeros"])
    c["s0_nat"] = np.ascontiguousarray(inputs["L0_ffn_scales"])
    c["s0_blk"] = _sblk(inputs["L0_ffn_scales"])
    c["bias0_col"] = np.ascontiguousarray(inputs["L0_ffn_bias"].reshape(32, 128).T)
    c["shift_l0"] = (4 * (np.arange(128) % 8)).astype(np.int32).reshape(128, 1)
    c["shift_l0z"] = np.ascontiguousarray(np.broadcast_to(
        (4 * np.arange(8, dtype=np.int32))[None, :], (32, 8)))
    c["qkv_blk"] = _blk_fp32(inputs["L0_attn_qkv_w"])
    c["gate_blk"] = _blk_fp32(inputs["L0_attn_gate_w"])
    c["out_blk"] = _blk_fp32(inputs["L0_attn_out_w"])
    c["normw_col"] = np.ascontiguousarray(
        inputs["L0_attn_norm_w"].reshape(32, 128).T.astype(np.float32))
    qd, kd, dd, bd = _decays()
    c["qd_bc"] = np.ascontiguousarray(
        np.broadcast_to(qd[:, None, :], (32, 128, 256)).astype(np.float32))
    c["kd_bc"] = np.ascontiguousarray(
        np.broadcast_to(kd[:, None, :], (32, 128, 256)).astype(np.float32))
    ddT = dd.transpose(0, 2, 1)
    c["dd_T"] = np.ascontiguousarray(
        ddT.reshape(32, 2, 128, 256).transpose(0, 2, 1, 3).astype(np.float32))
    pw = bd[:, None] ** np.arange(9)[None, :]
    c["bd_pows"] = np.ascontiguousarray(
        np.broadcast_to(pw.reshape(1, -1), (128, 32 * 9)).astype(np.float32))
    c["cw_percore"] = []
    for cid in range(NCORES):
        pos = cid % 4
        W = np.zeros((32, 4), np.float64)
        for cp in range(4):
            if cp < pos:
                W[:, cp] = bd ** (8 * (pos - 1 - cp))
        c["cw_percore"].append(np.ascontiguousarray(
            np.broadcast_to(W.reshape(1, -1), (128, 128)).astype(np.float32)))
    c["wb_blk"] = _blk_fp32(inputs["L1_ffn_W_base"])
    c["bb_col"] = np.ascontiguousarray(
        inputs["L1_ffn_b_base"].reshape(32, 128).T)
    A = inputs["L1_ffn_A"]
    c["A_blk2"] = np.ascontiguousarray(
        A.reshape(128, 32, 128).transpose(2, 1, 0))
    Wg = inputs["L1_ffn_W_gate"]
    c["Wg_blk2"] = np.ascontiguousarray(
        Wg.reshape(8, 32, 128).transpose(2, 1, 0))
    B = inputs["L1_ffn_B"]
    c["B_blk"] = np.ascontiguousarray(
        B.transpose(0, 2, 1).reshape(128, 32, 128).transpose(1, 0, 2))
    def _blk_l1q(qw, kch, mch):
        return np.ascontiguousarray(
            qw.reshape(kch, 128, mch, 16).transpose(2, 1, 0, 3))
    c["q1_blk"] = _blk_l1q(np.ascontiguousarray(inputs["L1_attn_qkv_qw"][:, :512]), 32, 32)
    c["o1_blk"] = _blk_l1q(inputs["L1_attn_o_qw"], 32, 32)
    c["gu_blk"] = _blk_l1q(inputs["L1_attn_gu_qw"], 32, 172)
    c["dn_blk"] = _blk_l1q(inputs["L1_attn_dn_qw"], 86, 32)
    c["q1_sblk"] = _sblk(np.ascontiguousarray(inputs["L1_attn_qkv_sc"][:, :4096]))
    c["o1_sblk"] = _sblk(inputs["L1_attn_o_sc"])
    c["gu_sblk"] = _sblk(inputs["L1_attn_gu_sc"])
    c["dn_sblk"] = _sblk(inputs["L1_attn_dn_sc"])
    c["q1_qz"] = np.ascontiguousarray(inputs["L1_attn_qkv_qz"][:, :512])
    c["o1_qz"] = np.ascontiguousarray(inputs["L1_attn_o_qz"])
    c["gu_qz"] = np.ascontiguousarray(inputs["L1_attn_gu_qz"])
    c["dn_qz"] = np.ascontiguousarray(inputs["L1_attn_dn_qz"])
    c["q1_sc"] = np.ascontiguousarray(inputs["L1_attn_qkv_sc"][:, :4096])
    c["o1_sc"] = np.ascontiguousarray(inputs["L1_attn_o_sc"])
    c["gu_sc"] = np.ascontiguousarray(inputs["L1_attn_gu_sc"])
    c["dn_sc"] = np.ascontiguousarray(inputs["L1_attn_dn_sc"])
    ORDER = np.array([0, 4, 1, 5, 2, 6, 3, 7], dtype=np.int32) * 4
    c["shift_l1"] = np.ascontiguousarray(np.broadcast_to(ORDER[None, :], (128, 8)))
    c["shift_l1z"] = np.ascontiguousarray(np.broadcast_to(ORDER[None, :], (86, 8)))
    c["ln1_col"] = np.ascontiguousarray(
        inputs["L1_attn_ln1_w"].reshape(32, 128).T)
    c["ln2_col"] = np.ascontiguousarray(
        inputs["L1_attn_ln2_w"].reshape(32, 128).T)
    return c



class Ctx:
    def __init__(self):
        self.nc = bass.Bass("TRN2", target_bir_lowering=False)
        self.stack = []
        self.semhs = []

    def cleanup(self, block):
        pass

    def sbuf(self, name, shape, dtype):
        g = self.nc.sbuf_tensor(name, shape, dtype)
        t = g.__enter__()
        self.stack.append((name, g))
        return t

    def psum(self, name, shape, dtype=F32):
        g = self.nc.psum_tensor(name, shape, dtype)
        t = g.__enter__()
        self.stack.append((name, g))
        return t

    def sem(self, name):
        g = self.nc.semaphore(name)
        t = g.__enter__()
        self.stack.append((name, g))
        self.semhs.append(t)
        return t

    def free_top(self, upto_name):
        while self.stack:
            name, g = self.stack.pop()
            g.__exit__(None, None, None)
            if name == upto_name:
                return

    def close(self):
        while self.stack:
            _, g = self.stack.pop()
            g.__exit__(None, None, None)



def build_init():
    ctx = Ctx()
    nc = ctx.nc
    x_tm = nc.dram_tensor("x_tm", [T, D], F32, kind="ExternalInput")
    ident = nc.dram_tensor("ident", [128, 128], F32, kind="ExternalInput")
    qz0 = nc.dram_tensor("qz0", [32, 512], I32, kind="ExternalInput")
    s0n = nc.dram_tensor("s0_nat", [32, 4096], F16, kind="ExternalInput")
    shlz = nc.dram_tensor("shift_l0z", [32, 8], I32, kind="ExternalInput")
    x_fm = nc.dram_tensor("x_fm", [KC, 128, T], F32, kind="ExternalOutput")
    zs0 = nc.dram_tensor("zs0", [32, 4096], F32, kind="ExternalOutput")

    xin = ctx.sbuf("xin", [128, 2 * D], F32)
    stage = [ctx.sbuf(f"stg{i}", [128, 2 * 128], F32) for i in range(4)]
    sid = ctx.sbuf("sid", [128, 128], F32)
    zrow = ctx.sbuf("zrow", [32, 512], I32)
    zint = ctx.sbuf("zint", [32, 4096], I32)
    zf = ctx.sbuf("zf", [32, 4096], F32)
    snat = ctx.sbuf("snat", [32, 4096], F32)
    shz = ctx.sbuf("shz", [32, 8], I32)
    pt = [ctx.psum(f"pt{i}", [128, 512], F32) for i in range(8)]
    ld = ctx.sem("ld")
    ldz = ctx.sem("ldz")
    vz = ctx.sem("vz")
    mm = ctx.sem("mm")
    ev = ctx.sem("ev")
    stm = ctx.sem("stm")

    ngrp = NTT * 8

    with nc.Block() as block:
        @block.gpsimd
        def _(g):
            g.dma_start(out=zrow[:], in_=qz0[:, :]).then_inc(ldz, 16)
            g.dma_start(out=snat[:], in_=s0n[:, :]).then_inc(ldz, 16)
            g.dma_start(out=shz[:], in_=shlz[:, :]).then_inc(ldz, 16)
            for G in range(ngrp):
                tt, gg = G // 8, G % 8
                g.wait_ge(ev, G + 1)
                for j in range(4):
                    k = gg * 4 + j
                    g.dma_start(out=x_fm[k, :, tt * 128:(tt + 1) * 128],
                                in_=stage[j][:, (G % 2) * 128:(G % 2 + 1) * 128]
                                ).then_inc(stm, 16)
            g.wait_ge(stm, 16 * 4 * ngrp)
            g.wait_ge(ldz, 64)

        @block.vector
        def _(v):
            v.wait_ge(ldz, 48)
            v.tensor_tensor(
                AP(zint, 0, [[4096, 32], [8, 512], [1, 8]]),
                AP(zrow, 0, [[512, 32], [1, 512], [0, 8]]),
                AP(shz, 0, [[8, 32], [0, 512], [1, 8]]),
                ALU.logical_shift_right).then_inc(vz, 1)
            v.wait_ge(vz, 1)
            v.tensor_scalar(zint[:, :], zint[:, :], 15, None,
                            ALU.bitwise_and).then_inc(vz, 1)
            v.wait_ge(vz, 2)
            v.tensor_scalar(zint[:, :], zint[:, :], 1, None, ALU.add).then_inc(vz, 1)
            v.wait_ge(vz, 3)
            v.tensor_tensor(zf[:, :], zint[:, :], snat[:, :], ALU.mult).then_inc(vz, 1)
            for G in range(ngrp):
                v.wait_ge(mm, G + 1)
                if G >= 2:
                    v.wait_ge(stm, 16 * 4 * (G - 1))
                bs = (G % 2) * 4
                for j in range(4):
                    ins = v.tensor_copy(stage[j][:, (G % 2) * 128:(G % 2 + 1) * 128],
                                        pt[bs + j][:, :128])
                ins.then_inc(ev, 1)

        @block.sync
        def _(s):
            s.dma_start(out=sid[:], in_=ident[:, :]).then_inc(ld, 16)
            s.wait_ge(vz, 4)
            s.dma_start(out=zs0[:, :], in_=zf[:]).then_inc(ldz, 16)
            for tt in range(NTT):
                if tt >= 2:
                    s.wait_ge(mm, (tt - 1) * 8)
                s.dma_start(out=xin[:, (tt % 2) * D:(tt % 2 + 1) * D],
                            in_=x_tm[tt * 128:(tt + 1) * 128, :]).then_inc(ld, 16)

        @block.tensor
        def _(te):
            for tt in range(NTT):
                te.wait_ge(ld, 16 * (tt + 2))
                for g in range(8):
                    G = tt * 8 + g
                    if G >= 2:
                        te.wait_ge(ev, G - 1)
                    bs = (G % 2) * 4
                    for j in range(4):
                        k = g * 4 + j
                        ins = te.transpose(
                            pt[bs + j][:, :128],
                            xin[:, (tt % 2) * D + k * 128:(tt % 2) * D + (k + 1) * 128],
                            sid[:])
                    ins.then_inc(mm, 1)

        ctx.cleanup(block)

    ctx.close()
    return nc



def build_fin():
    ctx = Ctx()
    nc = ctx.nc
    x_fm = nc.dram_tensor("x_fm", [KC, 128, T], F32, kind="ExternalInput")
    ident = nc.dram_tensor("ident", [128, 128], F32, kind="ExternalInput")
    x_tm = nc.dram_tensor("x_tm", [T, D], F32, kind="ExternalOutput")

    xin = ctx.sbuf("xin", [128, 2 * T], F32)
    stage = [ctx.sbuf(f"stg{i}", [128, 2 * 128], F32) for i in range(4)]
    sid = ctx.sbuf("sid", [128, 128], F32)
    pt = [ctx.psum(f"pt{i}", [128, 512], F32) for i in range(8)]
    ld = ctx.sem("ld")
    mm = ctx.sem("mm")
    ev = ctx.sem("ev")
    stm = ctx.sem("stm")

    ngrp = KC * 4

    with nc.Block() as block:
        @block.sync
        def _(s):
            s.dma_start(out=sid[:], in_=ident[:, :]).then_inc(ld, 16)
            for k in range(KC):
                if k >= 2:
                    s.wait_ge(mm, (k - 1) * 4)
                s.dma_start(out=xin[:, (k % 2) * T:(k % 2 + 1) * T],
                            in_=x_fm[k, :, :]).then_inc(ld, 16)

        @block.tensor
        def _(te):
            for k in range(KC):
                te.wait_ge(ld, 16 * (k + 2))
                for g in range(4):
                    G = k * 4 + g
                    if G >= 2:
                        te.wait_ge(ev, G - 1)
                    bs = (G % 2) * 4
                    for j in range(4):
                        tt = g * 4 + j
                        ins = te.transpose(
                            pt[bs + j][:, :128],
                            xin[:, (k % 2) * T + tt * 128:(k % 2) * T + (tt + 1) * 128],
                            sid[:])
                    ins.then_inc(mm, 1)

        @block.vector
        def _(v):
            for G in range(ngrp):
                v.wait_ge(mm, G + 1)
                if G >= 2:
                    v.wait_ge(stm, 16 * 4 * (G - 1))
                bs = (G % 2) * 4
                for j in range(4):
                    ins = v.tensor_copy(stage[j][:, (G % 2) * 128:(G % 2 + 1) * 128],
                                        pt[bs + j][:, :128])
                ins.then_inc(ev, 1)

        @block.gpsimd
        def _(g):
            for G in range(ngrp):
                k, gg = G // 4, G % 4
                g.wait_ge(ev, G + 1)
                for j in range(4):
                    tt = gg * 4 + j
                    g.dma_start(
                        out=x_tm[tt * 128:(tt + 1) * 128, k * 128:(k + 1) * 128],
                        in_=stage[j][:, (G % 2) * 128:(G % 2 + 1) * 128]
                        ).then_inc(stm, 16)
            g.wait_ge(stm, 16 * 4 * ngrp)

        ctx.cleanup(block)

    ctx.close()
    return nc



def build_ffn0(dequant=True):
    ctx = Ctx()
    nc = ctx.nc
    x_in = nc.dram_tensor("x_fm", [KC, 128, T], F32, kind="ExternalInput")
    x_out = nc.dram_tensor("x_fm_o", [KC, 128, T], F32, kind="ExternalOutput")
    biasc = nc.dram_tensor("bias0_col", [128, 32], F16, kind="ExternalInput")
    if dequant:
        w0q = nc.dram_tensor("w0q_blk", [32, 16, 32, 128], I32, kind="ExternalInput")
        s0b = nc.dram_tensor("s0_blk", [32, 4096], F16, kind="ExternalInput")
        zs0 = nc.dram_tensor("zs0", [32, 4096], F32, kind="ExternalInput")
        shl0 = nc.dram_tensor("shift_l0", [128, 1], I32, kind="ExternalInput")
        w0d = nc.dram_tensor("w0deq", [32, 128, 4096], BF16, kind="ExternalOutput")
    else:
        w0d = nc.dram_tensor("w0deq", [32, 128, 4096], BF16, kind="ExternalInput")

    H = ctx.sbuf("H", [128, KC * T], BF16)
    xf = ctx.sbuf("xf", [128, 2 * T], F32)
    onesb = ctx.sbuf("onesb", [128, 1], BF16)
    onesbr = ctx.sbuf("onesbr", [1, 128], BF16)
    onesfr = ctx.sbuf("onesfr", [1, 128], F32)
    bcol = ctx.sbuf("bcol", [128, 32], F32)
    hbcol = ctx.sbuf("hbcol", [128, 32], F32)
    epsc = ctx.sbuf("epsc", [128, 1], F32)
    shc = ctx.sbuf("shc", [128, 1], I32) if dequant else None
    sq = ctx.sbuf("sq", [128, T], BF16)
    r_bc = ctx.sbuf("r_bc", [128, T], F32)
    mr_bc = ctx.sbuf("mr_bc", [128, T], F32)
    sra = ctx.sbuf("sra", [1, T], F32)
    srb = ctx.sbuf("srb", [1, T], F32)
    rows_r = ctx.sbuf("rows_r", [1, T], F32)
    rows_mr = ctx.sbuf("rows_mr", [1, T], F32)
    t1 = ctx.sbuf("t1", [128, T], F32)
    ctx.free_top("sq")
    ws = ctx.sbuf("ws", [128, 2 * 4096], BF16)
    xn = ctx.sbuf("xn", [128, T], F32)
    if dequant:
        sw = ctx.sbuf("sw", [128, 2048], I32)
        qint = ctx.sbuf("qint", [128, 2048], I32)
        srow = ctx.sbuf("srow", [1, 4096], BF16)
        zsrow = ctx.sbuf("zsrow", [1, 4096], BF16)

    ps = [ctx.psum(f"ps{i}", [128, 512], F32) for i in range(8)]

    gini = ctx.sem("gini")
    lda = ctx.sem("lda")
    ld = ctx.sem("ld")
    acts = ctx.sem("acts")
    pes = ctx.sem("pes")
    vstat = ctx.sem("vstat")
    aa = ctx.sem("aa")
    bb = ctx.sem("bb")
    vb = ctx.sem("vb")
    hb = ctx.sem("hb")
    ldw = ctx.sem("ldw")
    mmb = ctx.sem("mmb")
    vd = ctx.sem("vd")
    mmg = ctx.sem("mmg")
    evu = ctx.sem("evu")
    stm = ctx.sem("stm")
    stw = ctx.sem("stw")
    vv = ctx.sem("vv")

    NW = 18 if dequant else 1

    with nc.Block() as block:
        @block.sync
        def _(s):
            for k in range(KC):
                if k >= 2:
                    s.wait_ge(acts, k - 1)
                s.dma_start(out=xf[:, (k % 2) * T:(k % 2 + 1) * T],
                            in_=x_in[k, :, :]).then_inc(ld, 16)
            for k in range(KC):
                if k == 0:
                    s.wait_ge(acts, KC - 1)
                elif k == 1:
                    s.wait_ge(acts, KC)
                else:
                    s.wait_ge(hb, k - 1)
                s.dma_start(out=xf[:, (k % 2) * T:(k % 2 + 1) * T],
                            in_=x_in[k, :, :]).then_inc(ld, 16)
            s.wait_ge(hb, KC - 1)
            s.dma_start(out=xf[:, 0:T], in_=x_in[0, :, :]).then_inc(ld, 16)
            s.wait_ge(hb, KC)
            s.dma_start(out=xf[:, T:2 * T], in_=x_in[1, :, :]).then_inc(ld, 16)
            for m in range(KC):
                s.wait_ge(evu, m + 1)
                s.dma_start(out=x_out[m, :, :], in_=xn[:]).then_inc(stm, 16)
                if dequant:
                    s.wait_ge(vd, 8 * (m + 1))
                    s.dma_start(out=w0d[m, :, :],
                                in_=ws[:, (m % 2) * 4096:(m % 2 + 1) * 4096]
                                ).then_inc(stw, 16)
                if m + 2 < KC:
                    s.dma_start(out=xf[:, ((m + 2) % 2) * T:((m + 2) % 2 + 1) * T],
                                in_=x_in[m + 2, :, :]).then_inc(ld, 16)
            s.wait_ge(stm, 16 * KC)
            if dequant:
                s.wait_ge(stw, 16 * KC)

        @block.gpsimd
        def _(g):
            g.memset(onesb[:], 1.0)
            g.memset(onesbr[:], 1.0)
            g.memset(onesfr[:], 1.0)
            g.memset(epsc[:], EPS_LN)
            g.engine_nop().then_inc(gini, 1)
            g.dma_start(out=bcol[:], in_=biasc[:, :]).then_inc(lda, 16)
            if dequant:
                g.dma_start(out=shc[:], in_=shl0[:, :]).then_inc(lda, 16)
            for k in range(KC):
                g.wait_ge(vb, k + 1)
                g.tensor_tensor(H[:, k * T:(k + 1) * T], t1[:], mr_bc[:],
                                ALU.subtract).then_inc(hb, 1)
            for m in range(KC):
                if dequant:
                    if m == 0:
                        g.wait_ge(hb, KC)
                    else:
                        g.wait_ge(vd, 8 * m)
                    for j in range(8):
                        dst = AP(sw, j * 2048, [[8 * 2048, 16], [1, 2048]])
                        g.dma_start(out=dst, in_=w0q[m, :, 0:16, :]).then_inc(ldw, 16)
                    if m >= 1:
                        g.wait_ge(mmb, 8 * m)
                    g.dma_start(out=srow[:], in_=s0b[m:m + 1, :]).then_inc(ldw, 16)
                    zsl = AP(zs0, 128 * m, [[0, 1], [4096, 32], [1, 128]])
                    g.dma_start(out=zsrow[:].rearrange("a (k n) -> a k n", k=32),
                                in_=zsl).then_inc(ldw, 16)
                    g.wait_ge(vd, 8 * m + 4)
                    for j in range(8):
                        dst = AP(sw, j * 2048, [[8 * 2048, 16], [1, 2048]])
                        g.dma_start(out=dst, in_=w0q[m, :, 16:32, :]).then_inc(ldw, 16)
                else:
                    if m == 0:
                        g.wait_ge(hb, KC)
                    elif m >= 2:
                        g.wait_ge(mmg, m - 1)
                    g.dma_start(out=ws[:, (m % 2) * 4096:(m % 2 + 1) * 4096],
                                in_=w0d[m, :, :]).then_inc(ldw, 16)

        @block.scalar
        def _(a):
            for k in range(KC):
                a.wait_ge(ld, 16 * (k + 1))
                if k >= 1:
                    a.wait_ge(pes, k)
                a.activation(H[:, k * T:(k + 1) * T],
                             xf[:, (k % 2) * T:(k % 2 + 1) * T], AF.Copy)
                a.activation(sq[:, :], xf[:, (k % 2) * T:(k % 2 + 1) * T],
                             AF.Square).then_inc(acts, 1)
            a.wait_ge(vstat, 1)
            a.wait_ge(gini, 1)
            a.activation(rows_r[0:1, :], srb[0:1, :], AF.Sqrt,
                         bias=epsc[0:1, 0:1]).then_inc(aa, 4)
            a.wait_ge(lda, 16)
            a.mul(hbcol[:], bcol[:], 0.5).then_inc(aa, 1)

        @block.tensor
        def _(te):
            te.wait_ge(gini, 1)
            for k in range(KC):
                te.wait_ge(acts, k + 1)
                st, sp = (k == 0), (k == KC - 1)
                for tg in range(4):
                    te.matmul(ps[tg][0:1, :], onesb[:],
                              H[:, k * T + tg * 512:k * T + (tg + 1) * 512],
                              start=st, stop=sp)
                for tg in range(4):
                    ins = te.matmul(
                        ps[4 + tg][0:1, :], onesb[:],
                        sq[:, tg * 512:(tg + 1) * 512],
                        start=st, stop=sp)
                ins.then_inc(pes, 1)
            te.wait_ge(vstat, 2)
            for tg in range(4):
                te.matmul(ps[tg][:, :], onesfr[:], rows_r[0:1, tg * 512:(tg + 1) * 512],
                          start=True, stop=True)
                te.matmul(ps[4 + tg][:, :], onesfr[:],
                          rows_mr[0:1, tg * 512:(tg + 1) * 512],
                          start=True, stop=True).then_inc(bb, 1)

            def bcasts(m):
                te.wait_ge(ldw, 16 * (NW * m + 10))
                for i in range(8):
                    S = m * 8 + i
                    if S == 0:
                        te.wait_ge(vstat, 3)
                    if S >= 2:
                        te.wait_ge(vd, S - 1)
                    te.matmul(ps[4 + S % 2][:, :], onesbr[:],
                              srow[0:1, i * 512:(i + 1) * 512], start=True, stop=True)
                    te.matmul(ps[6 + S % 2][:, :], onesbr[:],
                              zsrow[0:1, i * 512:(i + 1) * 512], start=True,
                              stop=True).then_inc(mmb, 1)

            if dequant:
                bcasts(0)
            for m in range(KC):
                if dequant and m + 1 < KC:
                    bcasts(m + 1)
                if dequant:
                    te.wait_ge(vd, 8 * (m + 1))
                else:
                    te.wait_ge(ldw, 16 * (m + 1))
                if m == 0:
                    te.wait_ge(vstat, 3)
                else:
                    te.wait_ge(evu, m)
                for k in range(KC):
                    st, sp = (k == 0), (k == KC - 1)
                    for tg in range(4):
                        ins = te.matmul(
                            ps[tg][:, :],
                            ws[:, (m % 2) * 4096 + k * 128:(m % 2) * 4096 + (k + 1) * 128],
                            H[:, k * T + tg * 512:k * T + (tg + 1) * 512],
                            start=st, stop=sp)
                ins.then_inc(mmg, 1)

        @block.vector
        def _(v):
            nvv = [0]

            def chain(ins):
                nvv[0] += 1
                ins.then_inc(vv, 1)
                v.wait_ge(vv, nvv[0])

            v.wait_ge(pes, KC)
            for tg in range(4):
                ins = v.tensor_copy(sra[0:1, tg * 512:(tg + 1) * 512], ps[tg][0:1, :])
            for tg in range(4):
                ins = v.tensor_copy(srb[0:1, tg * 512:(tg + 1) * 512],
                                    ps[4 + tg][0:1, :])
            chain(ins)
            ins = v.tensor_scalar(sra[0:1, :], sra[0:1, :], 1.0 / D, None, ALU.mult)
            ins = v.tensor_scalar(srb[0:1, :], srb[0:1, :], 1.0 / D, None, ALU.mult)
            chain(ins)
            ins = v.tensor_tensor(rows_mr[0:1, :], sra[0:1, :], sra[0:1, :], ALU.mult)
            chain(ins)
            v.tensor_tensor(srb[0:1, :], srb[0:1, :], rows_mr[0:1, :],
                            ALU.subtract).then_inc(vstat, 1)
            v.wait_ge(aa, 4)
            ins = v.reciprocal(srb[0:1, :], rows_r[0:1, :])
            chain(ins)
            ins = v.tensor_tensor(rows_mr[0:1, :], sra[0:1, :], srb[0:1, :], ALU.mult)
            ins = v.tensor_copy(rows_r[0:1, :], srb[0:1, :])
            ins.then_inc(vstat, 1)
            for tg in range(4):
                v.wait_ge(bb, tg + 1)
                v.tensor_copy(r_bc[:, tg * 512:(tg + 1) * 512], ps[tg][:, :])
                ins = v.tensor_copy(mr_bc[:, tg * 512:(tg + 1) * 512], ps[4 + tg][:, :])
            ins.then_inc(vstat, 1)
            v.wait_ge(vstat, 3)
            for k in range(KC):
                v.wait_ge(ld, 16 * (KC + k + 1))
                if k >= 1:
                    v.wait_ge(hb, k)
                v.tensor_tensor(t1[:], xf[:, (k % 2) * T:(k % 2 + 1) * T],
                                r_bc[:], ALU.mult).then_inc(vb, 1)

            def dequant_strip(m):
                if m >= 2:
                    v.wait_ge(mmg, m - 1)
                    v.wait_ge(stw, 16 * (m - 1))
                for h in range(2):
                    v.wait_ge(ldw, 16 * (NW * m + (8 if h == 0 else 18)))
                    if h == 1:
                        v.wait_ge(vd, 8 * m + 4)
                    ins = v.tensor_tensor(
                        qint[:, :],
                        AP(sw, 0, [[2048, 128], [1, 2048]]),
                        AP(shc, 0, [[1, 128], [0, 2048]]),
                        ALU.logical_shift_right)
                    chain(ins)
                    ins = v.tensor_scalar(qint[:, :], qint[:, :], 15, None,
                                          ALU.bitwise_and)
                    chain(ins)
                    for i in range(4 * h, 4 * h + 4):
                        S = m * 8 + i
                        v.wait_ge(mmb, S + 1)
                        off = (m % 2) * 4096 + i * 512
                        ins = v.tensor_tensor(
                            ws[:, off:off + 512],
                            qint[:, (i - 4 * h) * 512:(i - 4 * h + 1) * 512],
                            ps[4 + S % 2][:, :], ALU.mult)
                        chain(ins)
                        v.tensor_tensor(ws[:, off:off + 512], ws[:, off:off + 512],
                                        ps[6 + S % 2][:, :],
                                        ALU.subtract).then_inc(vd, 1)

            def update(m):
                v.wait_ge(mmg, m + 1)
                v.wait_ge(ld, 16 * (2 * KC + m + 1))
                if m >= 1:
                    v.wait_ge(stm, 16 * m)
                v.wait_ge(aa, 5)
                for tg in range(4):
                    ins = v.scalar_tensor_tensor(
                        xn[:, tg * 512:(tg + 1) * 512], ps[tg][:, :], 0.5,
                        xf[:, (m % 2) * T + tg * 512:(m % 2) * T + (tg + 1) * 512],
                        ALU.mult, ALU.add)
                chain(ins)
                v.tensor_scalar(xn[:, :], xn[:, :], hbcol[:, m:m + 1], None,
                                ALU.add).then_inc(evu, 1)

            if dequant:
                dequant_strip(0)
            for m in range(KC):
                if dequant and m + 1 < KC:
                    dequant_strip(m + 1)
                update(m)

        ctx.cleanup(block)

    ctx.close()
    return nc




def build_molora():
    ctx = Ctx()
    nc = ctx.nc
    x_in = nc.dram_tensor("x_fm", [KC, 128, T], F32, kind="ExternalInput")
    x_out = nc.dram_tensor("x_fm_o", [KC, 128, T], F32, kind="ExternalOutput")
    wb = nc.dram_tensor("wb_blk", [32, 128, 32, 128], F32, kind="ExternalInput")
    bbc = nc.dram_tensor("bb_col", [128, 32], F32, kind="ExternalInput")
    Abk = nc.dram_tensor("A_blk2", [128, 32, 128], F32, kind="ExternalInput")
    Wgb = nc.dram_tensor("Wg_blk2", [128, 32, 8], F32, kind="ExternalInput")
    Bbk = nc.dram_tensor("B_blk", [32, 128, 128], F32, kind="ExternalInput")
    ident = nc.dram_tensor("ident", [128, 128], F32, kind="ExternalInput")

    BIG = 30000.0

    H = ctx.sbuf("H", [128, KC * T], BF16)
    xf = ctx.sbuf("xf", [128, 2 * T], F32)
    onesb = ctx.sbuf("onesb", [128, 1], BF16)
    onesfr = ctx.sbuf("onesfr", [1, 128], F32)
    bcol = ctx.sbuf("bcol", [128, 32], F32)
    hbcol = ctx.sbuf("hbcol", [128, 32], F32)
    epsc = ctx.sbuf("epsc", [128, 1], F32)
    sq = ctx.sbuf("sq", [128, T], BF16)
    r_bc = ctx.sbuf("r_bc", [128, T], F32)
    mr_bc = ctx.sbuf("mr_bc", [128, T], F32)
    sra = ctx.sbuf("sra", [1, T], F32)
    srb = ctx.sbuf("srb", [1, T], F32)
    rows_r = ctx.sbuf("rows_r", [1, T], F32)
    rows_mr = ctx.sbuf("rows_mr", [1, T], F32)
    t1 = ctx.sbuf("t1", [128, T], F32)
    ctx.free_top("sq")
    ws = ctx.sbuf("ws", [128, 2 * 4096], BF16)
    xn = ctx.sbuf("xn", [128, T], F32)
    bs = ctx.sbuf("bs", [128, 2 * 128], BF16)
    sid = ctx.sbuf("sid", [128, 128], F32)
    Alh = ctx.sbuf("Alh", [128, 32 * 128], BF16)
    Wgl = ctx.sbuf("Wgl", [128, 32 * 8], BF16)
    hA = ctx.sbuf("hA", [128, T], BF16)
    hw = ctx.sbuf("hw", [128, T], BF16)
    rexp = ctx.sbuf("rexp", [128, T], BF16)
    lg_fm = ctx.sbuf("lg_fm", [8, T], F32)
    lg_tm = ctx.sbuf("lg_tm", [128, NTT * 8], F32)
    rt_tm = ctx.sbuf("rt_tm", [128, NTT * 8], F32)
    rt_fm = ctx.sbuf("rt_fm", [8, T], BF16)
    rcols = ctx.sbuf("rcols", [128, 8], F32)

    ps = [ctx.psum(f"ps{i}", [128, 512], F32) for i in range(8)]

    gini = ctx.sem("gini")
    lda = ctx.sem("lda")
    ld = ctx.sem("ld")
    acts = ctx.sem("acts")
    pes = ctx.sem("pes")
    vstat = ctx.sem("vstat")
    aa = ctx.sem("aa")
    bb = ctx.sem("bb")
    vb = ctx.sem("vb")
    hb = ctx.sem("hb")
    ldw = ctx.sem("ldw")
    ldp = ctx.sem("ldp")
    mmp = ctx.sem("mmp")
    vp = ctx.sem("vp")
    ab = ctx.sem("ab")
    vlg = ctx.sem("vlg")
    vsel = ctx.sem("vsel")
    vrt = ctx.sem("vrt")
    vre = ctx.sem("vre")
    gp = ctx.sem("gp")
    vhw = ctx.sem("vhw")
    mmg = ctx.sem("mmg")
    evu = ctx.sem("evu")
    stm = ctx.sem("stm")
    vv = ctx.sem("vv")

    with nc.Block() as block:
        @block.sync
        def _(s):
            for k in range(KC):
                if k >= 2:
                    s.wait_ge(acts, k - 1)
                s.dma_start(out=xf[:, (k % 2) * T:(k % 2 + 1) * T],
                            in_=x_in[k, :, :]).then_inc(ld, 16)
            for k in range(KC):
                if k == 0:
                    s.wait_ge(acts, KC - 1)
                elif k == 1:
                    s.wait_ge(acts, KC)
                else:
                    s.wait_ge(hb, k - 1)
                s.dma_start(out=xf[:, (k % 2) * T:(k % 2 + 1) * T],
                            in_=x_in[k, :, :]).then_inc(ld, 16)
            s.wait_ge(hb, KC - 1)
            s.dma_start(out=xf[:, 0:T], in_=x_in[0, :, :]).then_inc(ld, 16)
            s.wait_ge(hb, KC)
            s.dma_start(out=xf[:, T:2 * T], in_=x_in[1, :, :]).then_inc(ld, 16)
            for m in range(KC):
                s.wait_ge(evu, m + 1)
                s.dma_start(out=x_out[m, :, :], in_=xn[:]).then_inc(stm, 16)
                if m + 2 < KC:
                    s.dma_start(out=xf[:, ((m + 2) % 2) * T:((m + 2) % 2 + 1) * T],
                                in_=x_in[m + 2, :, :]).then_inc(ld, 16)
            s.wait_ge(stm, 16 * KC)

        @block.gpsimd
        def _(g):
            g.memset(onesb[:], 1.0)
            g.memset(onesfr[:], 1.0)
            g.memset(epsc[:], EPS_LN)
            g.engine_nop().then_inc(gini, 1)
            g.dma_start(out=bcol[:], in_=bbc[:, :]).then_inc(lda, 16)
            for k in range(KC):
                g.wait_ge(vb, k + 1)
                g.tensor_tensor(H[:, k * T:(k + 1) * T], t1[:], mr_bc[:],
                                ALU.subtract).then_inc(hb, 1)
            g.dma_start(out=sid[:], in_=ident[:, :]).then_inc(ldp, 16)
            g.dma_start(out=Alh[:].rearrange("p (k e) -> p k e", k=32),
                        in_=Abk[:, :, :]).then_inc(ldp, 16)
            g.dma_start(out=Wgl[:].rearrange("p (k e) -> p k e", k=32),
                        in_=Wgb[:, :, :]).then_inc(ldp, 16)
            g.wait_ge(vre, NTT)
            for r in range(16):
                dst = AP(rexp, r * T, [[16 * T, 8], [1, T]])
                g.dma_start(out=dst, in_=rt_fm[:, :]).then_inc(gp, 16)
            for m in range(KC):
                if m == 0:
                    g.wait_ge(hb, KC)
                elif m >= 2:
                    g.wait_ge(mmg, m - 1)
                g.dma_start(out=ws[:, (m % 2) * 4096:(m % 2 + 1) * 4096],
                            in_=wb[m, :, :, :]).then_inc(ldw, 16)
                g.dma_start(out=bs[:, (m % 2) * 128:(m % 2 + 1) * 128],
                            in_=Bbk[m, :, :]).then_inc(ldw, 16)

        @block.scalar
        def _(a):
            for k in range(KC):
                a.wait_ge(ld, 16 * (k + 1))
                if k >= 1:
                    a.wait_ge(pes, k)
                a.activation(H[:, k * T:(k + 1) * T],
                             xf[:, (k % 2) * T:(k % 2 + 1) * T], AF.Copy)
                a.activation(sq[:, :], xf[:, (k % 2) * T:(k % 2 + 1) * T],
                             AF.Square).then_inc(acts, 1)
            a.wait_ge(vstat, 1)
            a.wait_ge(gini, 1)
            a.activation(rows_r[0:1, :], srb[0:1, :], AF.Sqrt,
                         bias=epsc[0:1, 0:1]).then_inc(aa, 4)
            a.wait_ge(lda, 16)
            a.mul(hbcol[:], bcol[:], 0.5).then_inc(aa, 1)
            for tt in range(NTT):
                a.wait_ge(vsel, tt + 1)
                a.activation(rt_tm[:, tt * 8:(tt + 1) * 8],
                             lg_tm[:, tt * 8:(tt + 1) * 8], AF.Exp,
                             bias=rcols[:, 7:8]).then_inc(ab, 1)

        @block.tensor
        def _(te):
            te.wait_ge(gini, 1)
            for k in range(KC):
                te.wait_ge(acts, k + 1)
                st, sp = (k == 0), (k == KC - 1)
                for tg in range(4):
                    te.matmul(ps[tg][0:1, :], onesb[:],
                              H[:, k * T + tg * 512:k * T + (tg + 1) * 512],
                              start=st, stop=sp)
                for tg in range(4):
                    ins = te.matmul(ps[4 + tg][0:1, :], onesb[:],
                                    sq[:, tg * 512:(tg + 1) * 512],
                                    start=st, stop=sp)
                ins.then_inc(pes, 1)
            te.wait_ge(vstat, 2)
            for tg in range(4):
                te.matmul(ps[tg][:, :], onesfr[:], rows_r[0:1, tg * 512:(tg + 1) * 512],
                          start=True, stop=True)
                te.matmul(ps[4 + tg][:, :], onesfr[:],
                          rows_mr[0:1, tg * 512:(tg + 1) * 512],
                          start=True, stop=True).then_inc(bb, 1)
            te.wait_ge(vstat, 3)
            te.wait_ge(ldp, 48)
            for k in range(KC):
                te.wait_ge(hb, k + 1)
                st, sp = (k == 0), (k == KC - 1)
                for tg in range(4):
                    te.matmul(ps[tg][:, :], Alh[:, k * 128:(k + 1) * 128],
                              H[:, k * T + tg * 512:k * T + (tg + 1) * 512],
                              start=st, stop=sp)
                for tg in range(4):
                    ins = te.matmul(ps[4 + tg][0:8, :], Wgl[:, k * 8:(k + 1) * 8],
                                    H[:, k * T + tg * 512:k * T + (tg + 1) * 512],
                                    start=st, stop=sp)
            ins.then_inc(mmp, 1)
            for tt in range(NTT):
                te.wait_ge(vp, 1)
                if tt >= 2:
                    te.wait_ge(vlg, tt - 1)
                te.transpose(ps[4 + tt % 2][:, 0:8],
                             lg_fm[:, tt * 128:(tt + 1) * 128],
                             sid[0:8, 0:8]).then_inc(mmp, 1)
            for tt in range(NTT):
                te.wait_ge(vrt, tt + 1)
                if tt >= 2:
                    te.wait_ge(vre, tt - 1)
                te.transpose(ps[6 + tt % 2][0:8, 0:128],
                             rt_tm[:, tt * 8:(tt + 1) * 8],
                             sid[:]).then_inc(mmp, 1)
            for m in range(KC):
                te.wait_ge(ldw, 32 * (m + 1))
                if m == 0:
                    te.wait_ge(vp, 1)
                    te.wait_ge(vhw, 1)
                else:
                    te.wait_ge(evu, m)
                for k in range(KC):
                    st = (k == 0)
                    for tg in range(4):
                        te.matmul(
                            ps[tg][:, :],
                            ws[:, (m % 2) * 4096 + k * 128:(m % 2) * 4096 + (k + 1) * 128],
                            H[:, k * T + tg * 512:k * T + (tg + 1) * 512],
                            start=st, stop=False)
                for tg in range(4):
                    ins = te.matmul(ps[tg][:, :],
                                    bs[:, (m % 2) * 128:(m % 2 + 1) * 128],
                                    hw[:, tg * 512:(tg + 1) * 512],
                                    start=False, stop=True)
                ins.then_inc(mmg, 1)

        @block.vector
        def _(v):
            nvv = [0]

            def chain(ins):
                nvv[0] += 1
                ins.then_inc(vv, 1)
                v.wait_ge(vv, nvv[0])

            v.wait_ge(pes, KC)
            for tg in range(4):
                ins = v.tensor_copy(sra[0:1, tg * 512:(tg + 1) * 512], ps[tg][0:1, :])
            for tg in range(4):
                ins = v.tensor_copy(srb[0:1, tg * 512:(tg + 1) * 512],
                                    ps[4 + tg][0:1, :])
            chain(ins)
            ins = v.tensor_scalar(sra[0:1, :], sra[0:1, :], 1.0 / D, None, ALU.mult)
            ins = v.tensor_scalar(srb[0:1, :], srb[0:1, :], 1.0 / D, None, ALU.mult)
            chain(ins)
            ins = v.tensor_tensor(rows_mr[0:1, :], sra[0:1, :], sra[0:1, :], ALU.mult)
            chain(ins)
            v.tensor_tensor(srb[0:1, :], srb[0:1, :], rows_mr[0:1, :],
                            ALU.subtract).then_inc(vstat, 1)
            v.wait_ge(aa, 4)
            ins = v.reciprocal(srb[0:1, :], rows_r[0:1, :])
            chain(ins)
            ins = v.tensor_tensor(rows_mr[0:1, :], sra[0:1, :], srb[0:1, :], ALU.mult)
            ins = v.tensor_copy(rows_r[0:1, :], srb[0:1, :])
            ins.then_inc(vstat, 1)
            for tg in range(4):
                v.wait_ge(bb, tg + 1)
                v.tensor_copy(r_bc[:, tg * 512:(tg + 1) * 512], ps[tg][:, :])
                ins = v.tensor_copy(mr_bc[:, tg * 512:(tg + 1) * 512], ps[4 + tg][:, :])
            ins.then_inc(vstat, 1)
            v.wait_ge(vstat, 3)
            for k in range(KC):
                v.wait_ge(ld, 16 * (KC + k + 1))
                if k >= 1:
                    v.wait_ge(hb, k)
                v.tensor_tensor(t1[:], xf[:, (k % 2) * T:(k % 2 + 1) * T],
                                r_bc[:], ALU.mult).then_inc(vb, 1)

            v.wait_ge(mmp, 1)
            for tg in range(4):
                ins = v.tensor_copy(hA[:, tg * 512:(tg + 1) * 512], ps[tg][:, :])
            for tg in range(4):
                ins = v.tensor_copy(lg_fm[0:8, tg * 512:(tg + 1) * 512],
                                    ps[4 + tg][0:8, :])
            ins.then_inc(vp, 1)
            for tt in range(NTT):
                v.wait_ge(mmp, 2 + tt)
                v.tensor_copy(lg_tm[:, tt * 8:(tt + 1) * 8],
                              ps[4 + tt % 2][:, 0:8]).then_inc(vlg, 1)
            for tt in range(NTT):
                lg = lg_tm[:, tt * 8:(tt + 1) * 8]
                rt = rt_tm[:, tt * 8:(tt + 1) * 8]
                if tt == 0:
                    v.wait_ge(vlg, 1)
                ins = v.reduce_max(rcols[:, 0:1], lg, axis=AX.X)
                chain(ins)
                ins = v.tensor_scalar(rcols[:, 7:8], rcols[:, 0:1], -1.0, -BIG,
                                      ALU.mult, ALU.add)
                ins = v.tensor_scalar(rt, lg, rcols[:, 0:1], None, ALU.is_ge)
                chain(ins)
                ins = v.scalar_tensor_tensor(rt, rt, -BIG, lg, ALU.mult, ALU.add)
                chain(ins)
                ins = v.reduce_max(rcols[:, 2:3], rt, axis=AX.X)
                chain(ins)
                ins = v.tensor_scalar(rt, lg, rcols[:, 2:3], None, ALU.is_ge)
                chain(ins)
                ins = v.scalar_tensor_tensor(rt, rt, BIG, lg, ALU.mult, ALU.add)
                chain(ins)
                v.tensor_copy(lg_tm[:, tt * 8:(tt + 1) * 8], rt).then_inc(vsel, 1)
                v.wait_ge(ab, tt + 1)
                ins = v.reduce_sum(rcols[:, 3:4], rt, axis=AX.X)
                chain(ins)
                ins = v.reciprocal(rcols[:, 4:5], rcols[:, 3:4])
                chain(ins)
                v.tensor_scalar(rt, rt, rcols[:, 4:5], None,
                                ALU.mult).then_inc(vrt, 1)
                if tt >= 2:
                    v.wait_ge(mmp, 18 + tt - 2)
                    v.tensor_copy(rt_fm[0:8, (tt - 2) * 128:(tt - 1) * 128],
                                  ps[6 + tt % 2][0:8, 0:128]).then_inc(vre, 1)
            for tt in (NTT - 2, NTT - 1):
                v.wait_ge(mmp, 18 + tt)
                v.tensor_copy(rt_fm[0:8, tt * 128:(tt + 1) * 128],
                              ps[6 + tt % 2][0:8, 0:128]).then_inc(vre, 1)
            v.wait_ge(gp, 16 * 16)
            v.tensor_tensor(hw[:, :], hA[:, :], rexp[:, :],
                            ALU.mult).then_inc(vhw, 1)

            def update(m):
                v.wait_ge(mmg, m + 1)
                v.wait_ge(ld, 16 * (2 * KC + m + 1))
                if m >= 1:
                    v.wait_ge(stm, 16 * m)
                v.wait_ge(aa, 5)
                for tg in range(4):
                    ins = v.scalar_tensor_tensor(
                        xn[:, tg * 512:(tg + 1) * 512], ps[tg][:, :], 0.5,
                        xf[:, (m % 2) * T + tg * 512:(m % 2) * T + (tg + 1) * 512],
                        ALU.mult, ALU.add)
                chain(ins)
                v.tensor_scalar(xn[:, :], xn[:, :], hbcol[:, m:m + 1], None,
                                ALU.add).then_inc(evu, 1)

            for m in range(KC):
                update(m)

        ctx.cleanup(block)

    ctx.close()
    return nc




def build_llama(dbg=False):
    KIND = "ExternalOutput" if dbg else "Internal"
    ctx = Ctx()
    nc = ctx.nc
    x_in = nc.dram_tensor("x_fm", [KC, 128, T], F32, kind="ExternalInput")
    x_out = nc.dram_tensor("x_fm_o", [KC, 128, T], F32, kind="ExternalOutput")
    q1b = nc.dram_tensor("q1_blk", [32, 128, 32, 16], I32, kind="ExternalInput")
    o1b = nc.dram_tensor("o1_blk", [32, 128, 32, 16], I32, kind="ExternalInput")
    gub = nc.dram_tensor("gu_blk", [172, 128, 32, 16], I32, kind="ExternalInput")
    dnb = nc.dram_tensor("dn_blk", [32, 128, 86, 16], I32, kind="ExternalInput")
    q1s = nc.dram_tensor("q1_sblk", [32, 4096], F32, kind="ExternalInput")
    o1s = nc.dram_tensor("o1_sblk", [32, 4096], F32, kind="ExternalInput")
    gus = nc.dram_tensor("gu_sblk", [172, 4096], F32, kind="ExternalInput")
    dns = nc.dram_tensor("dn_sblk", [32, 11008], F32, kind="ExternalInput")
    zq1 = nc.dram_tensor("zs_q1", [32, 4096], F32, kind="ExternalInput")
    zo1 = nc.dram_tensor("zs_o1", [32, 4096], F32, kind="ExternalInput")
    zgu = nc.dram_tensor("zs_gu", [32, 22016], F32, kind="ExternalInput")
    zdn = nc.dram_tensor("zs_dn", [86, 4096], F32, kind="ExternalInput")
    shl1 = nc.dram_tensor("shift_l1", [128, 8], I32, kind="ExternalInput")
    ln1c = nc.dram_tensor("ln1_col", [128, 32], F32, kind="ExternalInput")
    ln2c = nc.dram_tensor("ln2_col", [128, 32], F32, kind="ExternalInput")
    w1deq = nc.dram_tensor("w1deq", [32, 128, 4096], BF16, kind=KIND)
    wodeq = nc.dram_tensor("wodeq", [32, 128, 4096], BF16, kind=KIND)
    gudeq = nc.dram_tensor("gudeq", [172, 128, 4096], BF16, kind=KIND)
    dndeq = nc.dram_tensor("dndeq", [32, 128, 11008], BF16, kind=KIND)
    traw = nc.dram_tensor("traw", [32, 128, T], BF16, kind=KIND)

    EPS2 = 1e-6
    NG, GT = 4, 512

    onesb = ctx.sbuf("onesb", [128, 1], BF16)
    onesbr = ctx.sbuf("onesbr", [1, 128], BF16)
    onesfr = ctx.sbuf("onesfr", [1, 128], F32)
    epsc = ctx.sbuf("epsc", [128, 1], F32)
    eps2c = ctx.sbuf("eps2c", [128, 1], F32)
    shc = ctx.sbuf("shc", [128, 8], I32)
    l1col = ctx.sbuf("l1col", [128, 32], F32)
    l2col = ctx.sbuf("l2col", [128, 32], F32)
    r_bc = ctx.sbuf("r_bc", [128, T], BF16)
    mr_bc = ctx.sbuf("mr_bc", [128, T], BF16)
    r2_bc = ctx.sbuf("r2_bc", [128, T], BF16)
    r3bc = ctx.sbuf("r3bc", [128, GT], F32)
    r3row = ctx.sbuf("r3row", [1, GT], F32)
    H = ctx.sbuf("H", [128, KC * T], BF16)
    xf = ctx.sbuf("xf", [128, 2 * T], F32)
    sq = ctx.sbuf("sq", [128, T], BF16)
    sra = ctx.sbuf("sra", [1, T], F32)
    srb = ctx.sbuf("srb", [1, T], F32)
    rows_r = ctx.sbuf("rows_r", [1, T], F32)
    rows_mr = ctx.sbuf("rows_mr", [1, T], F32)
    t1 = ctx.sbuf("t1", [128, T], F32)
    ctx.free_top("xf")
    swq = ctx.sbuf("swq", [128, 2 * 1376], I32)
    qint = ctx.sbuf("qint", [128, 2048], I32)
    srow = ctx.sbuf("srow", [1, 4096], BF16)
    zsrow = ctx.sbuf("zsrow", [1, 4096], BF16)
    wsB = ctx.sbuf("wsB", [128, 2 * 4096], BF16)
    ctx.free_top("swq")
    wsA = ctx.sbuf("wsA", [128, 4096], BF16)
    tstag = ctx.sbuf("tstag", [128, T], BF16)
    ctx.free_top("wsA")
    ctx.free_top("H")
    xf2 = ctx.sbuf("xf2", [128, 6 * GT], F32)
    h2g = ctx.sbuf("h2g", [128, KC * GT], BF16)
    H2s = ctx.sbuf("H2s", [128, KC * GT], BF16)
    ws = ctx.sbuf("ws", [128, 2 * 4096], BF16)
    sg = ctx.sbuf("sg", [128, GT], BF16)
    sqg = ctx.sbuf("sqg", [128, GT], BF16)
    xnst = ctx.sbuf("xnst", [128, 2 * GT], F32)
    gsc = ctx.sbuf("gsc", [128, 2 * GT], F32)
    trg = ctx.sbuf("trg", [128, KC * GT], BF16)
    ctx.free_top("trg")
    gbuf = ctx.sbuf("gbuf", [128, 86 * GT], BF16)

    ps = [ctx.psum(f"ps{i}", [128, 512], F32) for i in range(8)]

    gini = ctx.sem("gini")
    lda = ctx.sem("lda")
    ld = ctx.sem("ld")
    acts = ctx.sem("acts")
    pes = ctx.sem("pes")
    vstat = ctx.sem("vstat")
    aa = ctx.sem("aa")
    bb = ctx.sem("bb")
    vbe = ctx.sem("vbe")
    vb = ctx.sem("vb")
    hb = ctx.sem("hb")
    ldq = ctx.sem("ldq")
    mmb = ctx.sem("mmb")
    vd = ctx.sem("vd")
    stw = ctx.sem("stw")
    ldw = ctx.sem("ldw")
    mmg = ctx.sem("mmg")
    vte = ctx.sem("vte")
    vev = ctx.sem("vev")
    ldt = ctx.sem("ldt")
    ldx = ctx.sem("ldx")
    aag = ctx.sem("aag")
    pst = ctx.sem("pst")
    vr3 = ctx.sem("vr3")
    vgu = ctx.sem("vgu")
    vh2 = ctx.sem("vh2")
    stm = ctx.sem("stm")
    stt2 = ctx.sem("stt2")
    vv = ctx.sem("vv")

    DQ = [
        ("q1", q1b, q1s, zq1, 4096, w1deq, 32, [(0, 32)], 32),
        ("o1", o1b, o1s, zo1, 4096, wodeq, 32, [(0, 32)], 32),
        ("gu", gub, gus, zgu, 22016, gudeq, 172, [(0, 32)], 32),
        ("dn", dnb, dns, zdn, 4096, dndeq, 32, [(0, 32), (32, 64), (64, 86)], 86),
    ]

    def slices_of(k0, k1):
        cols = (k1 - k0) * 128
        out = []
        c = 0
        while c < cols:
            w = min(512, cols - c)
            out.append((c, w))
            c += w
        return out

    PIECES = []
    for (nm, qb, sb, zt, zN, dq, ns, pcs, G) in DQ:
        for m in range(ns):
            for (k0, k1) in pcs:
                PIECES.append(dict(nm=nm, qb=qb, sb=sb, zt=zt, zN=zN, dq=dq,
                                   m=m, k0=k0, k1=k1, G=G,
                                   sl=slices_of(k0, k1)))
    SLB = [0]
    for p in PIECES:
        SLB.append(SLB[-1] + len(p["sl"]))
    NPC = len(PIECES)

    def Jo(g, m):
        return 32 + 300 * g + m + 1

    def Jgu(g, i, up):
        return 32 + 300 * g + 32 + 2 * i + (2 if up else 1)

    def Jdn(g, mp, pi):
        return 32 + 300 * g + 204 + 3 * mp + pi + 1

    DNP = [(0, 4096), (4096, 8192), (8192, 11008)]

    with nc.Block() as block:
        @block.sync
        def _(s):
            for k in range(KC):
                if k >= 2:
                    s.wait_ge(acts, k - 1)
                s.dma_start(out=xf[:, (k % 2) * T:(k % 2 + 1) * T],
                            in_=x_in[k, :, :]).then_inc(ld, 16)
            for k in range(KC):
                if k == 0:
                    s.wait_ge(acts, KC - 1)
                elif k == 1:
                    s.wait_ge(acts, KC)
                else:
                    s.wait_ge(vb, k - 1)
                s.dma_start(out=xf[:, (k % 2) * T:(k % 2 + 1) * T],
                            in_=x_in[k, :, :]).then_inc(ld, 16)
            for pc, p in enumerate(PIECES):
                s.wait_ge(vd, SLB[pc + 1])
                s.dma_start(
                    out=p["dq"][p["m"], :, p["k0"] * 128:p["k1"] * 128],
                    in_=wsB[:, (pc % 2) * 4096:(pc % 2) * 4096 + (p["k1"] - p["k0"]) * 128]
                    ).then_inc(stw, 16)
            for m in range(KC):
                s.wait_ge(vte, m + 1)
                s.dma_start(out=traw[m, :, :], in_=tstag[:]).then_inc(stt2, 16)
            s.wait_ge(stt2, 16 * KC)
            for g in range(NG):
                if g >= 1:
                    s.wait_ge(mmg, 32 + 300 * g)
                for k in range(KC):
                    s.dma_start(out=trg[:, k * GT:(k + 1) * GT],
                                in_=traw[k, :, g * GT:(g + 1) * GT]).then_inc(ldt, 16)
                for m in range(KC):
                    c = 64 * g + m
                    if c >= 6:
                        s.wait_ge(vev, c - 6 + 1)
                    s.dma_start(out=xf2[:, (c % 6) * GT:(c % 6 + 1) * GT],
                                in_=x_in[m, :, g * GT:(g + 1) * GT]).then_inc(ldx, 16)
                for m in range(KC):
                    c = 64 * g + 32 + m
                    if c >= 6:
                        s.wait_ge(vev, c - 6 + 1)
                    s.dma_start(out=xf2[:, (c % 6) * GT:(c % 6 + 1) * GT],
                                in_=x_in[m, :, g * GT:(g + 1) * GT]).then_inc(ldx, 16)
                    s.wait_ge(vev, c + 1)
                    s.dma_start(out=x_out[m, :, g * GT:(g + 1) * GT],
                                in_=xnst[:, (m % 2) * GT:(m % 2 + 1) * GT]
                                ).then_inc(stm, 16)
            s.wait_ge(stm, 16 * NG * KC)
            s.wait_ge(stw, 16 * NPC)

        @block.gpsimd
        def _(g_):
            g_.memset(onesb[:], 1.0)
            g_.memset(onesbr[:], 1.0)
            g_.memset(onesfr[:], 1.0)
            g_.memset(epsc[:], EPS_LN)
            g_.memset(eps2c[:], EPS2)
            g_.engine_nop().then_inc(gini, 1)
            g_.dma_start(out=shc[:], in_=shl1[:, :]).then_inc(lda, 16)
            g_.dma_start(out=l1col[:], in_=ln1c[:, :]).then_inc(lda, 16)
            g_.dma_start(out=l2col[:], in_=ln2c[:, :]).then_inc(lda, 16)
            for k in range(KC):
                g_.wait_ge(vb, k + 1)
                g_.wait_ge(lda, 32)
                g_.tensor_tensor(H[:, k * T:(k + 1) * T], t1[:], mr_bc[:],
                                 ALU.subtract).then_inc(hb, 1)
                g_.wait_ge(hb, k + 1)
                g_.tensor_scalar(H[:, k * T:(k + 1) * T], H[:, k * T:(k + 1) * T],
                                 l1col[:, k:k + 1], None, ALU.mult).then_inc(hb, 1)
            for pc, p in enumerate(PIECES):
                if pc == 0:
                    g_.wait_ge(hb, 2 * KC)
                if pc >= 2:
                    g_.wait_ge(vd, SLB[pc - 1])
                nw = (p["k1"] - p["k0"]) * 16
                g_.dma_start(out=swq[:, (pc % 2) * 1376:(pc % 2) * 1376 + nw],
                             in_=p["qb"][p["m"], :, p["k0"]:p["k1"], :]
                             ).then_inc(ldq, 16)
                if pc >= 1:
                    g_.wait_ge(mmb, SLB[pc])
                nc_ = (p["k1"] - p["k0"]) * 128
                g_.dma_start(out=srow[0:1, 0:nc_],
                             in_=p["sb"][p["m"]:p["m"] + 1, p["k0"] * 128:p["k1"] * 128]
                             ).then_inc(ldq, 16)
                zsl = AP(p["zt"], p["k0"] * p["zN"] + 128 * p["m"],
                         [[0, 1], [p["zN"], p["k1"] - p["k0"]], [1, 128]])
                g_.dma_start(out=zsrow[0:1, 0:nc_].rearrange("a (k n) -> a k n", n=128),
                             in_=zsl).then_inc(ldq, 16)
            for m in range(KC):
                if m == 0:
                    g_.wait_ge(stw, 16 * NPC)
                g_.wait_ge(stw, 16 * (m + 1))
                if m >= 1:
                    g_.wait_ge(mmg, m)
                g_.dma_start(out=wsA[:, :],
                             in_=w1deq[m, :, :]).then_inc(ldw, 16)
            for g in range(NG):
                for m in range(KC):
                    J = Jo(g, m)
                    if J >= 35:
                        g_.wait_ge(mmg, J - 2)
                    else:
                        g_.wait_ge(mmg, KC)
                    if g == 0:
                        g_.wait_ge(stw, 16 * (32 + m + 1))
                    g_.dma_start(out=ws[:, (J % 2) * 4096:(J % 2 + 1) * 4096],
                                 in_=wodeq[m, :, :]).then_inc(ldw, 16)
                for i in range(86):
                    for up in (False, True):
                        J = Jgu(g, i, up)
                        mi = (i + 86) if up else i
                        g_.wait_ge(mmg, J - 2)
                        if g == 0:
                            g_.wait_ge(stw, 16 * (64 + mi + 1))
                        g_.dma_start(out=ws[:, (J % 2) * 4096:(J % 2 + 1) * 4096],
                                     in_=gudeq[mi, :, :]).then_inc(ldw, 16)
                for mp in range(KC):
                    for pi, (c0, c1) in enumerate(DNP):
                        J = Jdn(g, mp, pi)
                        g_.wait_ge(mmg, J - 2)
                        if g == 0:
                            g_.wait_ge(stw, 16 * (96 + 3 * mp + pi + 1))
                        g_.dma_start(out=ws[:, (J % 2) * 4096:(J % 2) * 4096 + (c1 - c0)],
                                     in_=dndeq[mp, :, c0:c1]).then_inc(ldw, 16)

        @block.scalar
        def _(a):
            for k in range(KC):
                a.wait_ge(ld, 16 * (k + 1))
                if k >= 1:
                    a.wait_ge(pes, k)
                a.activation(H[:, k * T:(k + 1) * T],
                             xf[:, (k % 2) * T:(k % 2 + 1) * T], AF.Copy)
                a.activation(sq[:, :], xf[:, (k % 2) * T:(k % 2 + 1) * T],
                             AF.Square).then_inc(acts, 1)
            a.wait_ge(vstat, 1)
            a.wait_ge(gini, 1)
            a.activation(rows_r[0:1, :], srb[0:1, :], AF.Sqrt,
                         bias=epsc[0:1, 0:1]).then_inc(aa, 4)
            a.wait_ge(vv, 1000)
            a.activation(sra[0:1, :], srb[0:1, :], AF.Sqrt,
                         bias=eps2c[0:1, 0:1]).then_inc(aa, 1)
            for g in range(NG):
                for m in range(KC):
                    a.wait_ge(vev, 64 * g + m + 1)
                    if g + m > 0:
                        a.wait_ge(pst, 33 * g + m)
                    a.activation(sqg[:, :], h2g[:, m * GT:(m + 1) * GT],
                                 AF.Square).then_inc(aag, 1)
                a.wait_ge(vr3, 3 * g + 1)
                a.activation(ps[4][64:65, :], ps[4][32:33, :], AF.Sqrt,
                             bias=eps2c[0:1, 0:1]).then_inc(aag, 1)
                for i in range(86):
                    a.wait_ge(mmg, Jgu(g, i, False))
                    if i >= 1:
                        a.wait_ge(vgu, 86 * g + i)
                    a.activation(sg[:, :], ps[i % 2][:, :], AF.Silu).then_inc(aag, 1)

        @block.tensor
        def _(te):
            te.wait_ge(gini, 1)
            for k in range(KC):
                te.wait_ge(acts, k + 1)
                st, sp = (k == 0), (k == KC - 1)
                for tg in range(4):
                    te.matmul(ps[tg][0:1, :], onesb[:],
                              H[:, k * T + tg * 512:k * T + (tg + 1) * 512],
                              start=st, stop=sp)
                for tg in range(4):
                    ins = te.matmul(ps[4 + tg][0:1, :], onesb[:],
                                    sq[:, tg * 512:(tg + 1) * 512],
                                    start=st, stop=sp)
                ins.then_inc(pes, 1)
            te.wait_ge(vstat, 2)
            for tg in range(4):
                te.matmul(ps[tg][:, :], onesfr[:], rows_r[0:1, tg * 512:(tg + 1) * 512],
                          start=True, stop=True)
                te.matmul(ps[4 + tg][:, :], onesfr[:],
                          rows_mr[0:1, tg * 512:(tg + 1) * 512],
                          start=True, stop=True).then_inc(bb, 1)
            te.wait_ge(vstat, 3)
            for tg in range(4):
                te.wait_ge(vbe, tg + 1)
                te.matmul(ps[tg][:, :], onesfr[:], srb[0:1, tg * 512:(tg + 1) * 512],
                          start=True, stop=True).then_inc(bb, 1)
            for pc, p in enumerate(PIECES):
                te.wait_ge(ldq, 48 * (pc + 1))
                for si, (c0, w) in enumerate(p["sl"]):
                    S = SLB[pc] + si
                    if S == 0:
                        te.wait_ge(vbe, 8)
                    if S >= 2:
                        te.wait_ge(vd, S - 1)
                    te.matmul(ps[4 + S % 2][:, 0:w], onesbr[:],
                              srow[0:1, c0:c0 + w], start=True, stop=True)
                    te.matmul(ps[6 + S % 2][:, 0:w], onesbr[:],
                              zsrow[0:1, c0:c0 + w], start=True,
                              stop=True).then_inc(mmb, 1)
            for m in range(KC):
                te.wait_ge(ldw, 16 * (m + 1))
                if m == 0:
                    te.wait_ge(vbe, 8)
                if m >= 1:
                    te.wait_ge(vte, m)
                for k in range(KC):
                    st, sp = (k == 0), (k == KC - 1)
                    for tg in range(4):
                        ins = te.matmul(
                            ps[tg][:, :],
                            wsA[:, k * 128:(k + 1) * 128],
                            H[:, k * T + tg * 512:k * T + (tg + 1) * 512],
                            start=st, stop=sp)
                ins.then_inc(mmg, 1)
            for g in range(NG):
                te.wait_ge(ldt, 16 * 32 * (g + 1))
                for m in range(KC):
                    J = Jo(g, m)
                    te.wait_ge(ldw, 16 * J)
                    c = 64 * g + m
                    if c >= 2:
                        te.wait_ge(vev, c - 1)
                    for k in range(KC):
                        ins = te.matmul(
                            ps[2 + m % 2][:, :],
                            ws[:, (J % 2) * 4096 + k * 128:(J % 2) * 4096 + (k + 1) * 128],
                            trg[:, k * GT:(k + 1) * GT],
                            start=(k == 0), stop=(k == KC - 1))
                    ins.then_inc(mmg, 1)
                    if m >= 1:
                        te.wait_ge(aag, 119 * g + m)
                        te.matmul(ps[4][0:1, :], onesb[:], sqg[:, :],
                                  start=(m == 1), stop=(m == KC - 1)
                                  ).then_inc(pst, 1)
                te.wait_ge(aag, 119 * g + 32)
                te.matmul(ps[4][0:1, :], onesb[:], sqg[:, :],
                          start=False, stop=True).then_inc(pst, 1)
                te.wait_ge(vr3, 3 * g + 2)
                te.matmul(ps[5][:, :], onesfr[:], r3row[0:1, :],
                          start=True, stop=True).then_inc(pst, 1)
                te.wait_ge(vh2, 32 * (g + 1))
                for i in range(86):
                    for up in (False, True):
                        J = Jgu(g, i, up)
                        te.wait_ge(ldw, 16 * J)
                        bank = (2 if up else 0) + i % 2
                        if up:
                            if i >= 2:
                                te.wait_ge(vgu, 86 * g + i - 1)
                        else:
                            if i >= 2:
                                te.wait_ge(aag, 119 * g + 33 + i - 1)
                        for k in range(KC):
                            ins = te.matmul(
                                ps[bank][:, :],
                                ws[:, (J % 2) * 4096 + k * 128:(J % 2) * 4096 + (k + 1) * 128],
                                H2s[:, k * GT:(k + 1) * GT],
                                start=(k == 0), stop=(k == KC - 1))
                        ins.then_inc(mmg, 1)
                for mp in range(KC):
                    if mp <= 1:
                        te.wait_ge(aag, 119 * (g + 1))
                    c = 64 * g + 32 + mp
                    if mp >= 2:
                        te.wait_ge(vev, c - 1)
                    kk = 0
                    for pi, (c0, c1) in enumerate(DNP):
                        J = Jdn(g, mp, pi)
                        te.wait_ge(ldw, 16 * J)
                        nk = (c1 - c0) // 128
                        for k in range(nk):
                            ins = te.matmul(
                                ps[mp % 2][:, :],
                                ws[:, (J % 2) * 4096 + k * 128:(J % 2) * 4096 + (k + 1) * 128],
                                gbuf[:, (kk + k) * GT:(kk + k + 1) * GT],
                                start=(kk + k == 0), stop=(kk + k == 85))
                        kk += nk
                        ins.then_inc(mmg, 1)

        @block.vector
        def _(v):
            nvv = [0]

            def chain(ins):
                nvv[0] += 1
                ins.then_inc(vv, 1)
                v.wait_ge(vv, nvv[0])

            v.wait_ge(pes, KC)
            for tg in range(4):
                ins = v.tensor_copy(sra[0:1, tg * 512:(tg + 1) * 512], ps[tg][0:1, :])
            for tg in range(4):
                ins = v.tensor_copy(srb[0:1, tg * 512:(tg + 1) * 512],
                                    ps[4 + tg][0:1, :])
            chain(ins)
            ins = v.tensor_scalar(sra[0:1, :], sra[0:1, :], 1.0 / D, None, ALU.mult)
            ins = v.tensor_scalar(srb[0:1, :], srb[0:1, :], 1.0 / D, None, ALU.mult)
            chain(ins)
            ins = v.tensor_tensor(rows_mr[0:1, :], sra[0:1, :], sra[0:1, :], ALU.mult)
            chain(ins)
            v.tensor_tensor(srb[0:1, :], srb[0:1, :], rows_mr[0:1, :],
                            ALU.subtract).then_inc(vstat, 1)
            v.wait_ge(aa, 4)
            ins = v.reciprocal(rows_mr[0:1, :], rows_r[0:1, :])
            chain(ins)
            ins = v.tensor_tensor(srb[0:1, :], srb[0:1, :], rows_mr[0:1, :], ALU.mult)
            chain(ins)
            ins = v.tensor_tensor(srb[0:1, :], srb[0:1, :], rows_mr[0:1, :], ALU.mult)
            chain(ins)
            ins = v.tensor_tensor(sra[0:1, :], sra[0:1, :], rows_mr[0:1, :], ALU.mult)
            chain(ins)
            ins = v.tensor_copy(rows_r[0:1, :], rows_mr[0:1, :])
            ins = v.tensor_copy(rows_mr[0:1, :], sra[0:1, :])
            ins.then_inc(vstat, 1)
            nvv[0] += 1
            ins2 = v.engine_nop()
            ins2.then_inc(vv, 1000 - nvv[0] + 1)
            nvv[0] = 1000
            v.wait_ge(vv, 1000)
            v.wait_ge(aa, 5)
            ins = v.reciprocal(srb[0:1, :], sra[0:1, :])
            nvv[0] += 1
            ins.then_inc(vv, 1)
            v.wait_ge(vv, nvv[0])
            v.engine_nop().then_inc(vstat, 1)
            for tg in range(4):
                v.wait_ge(bb, tg + 1)
                v.tensor_copy(r_bc[:, tg * 512:(tg + 1) * 512], ps[tg][:, :])
                v.tensor_copy(mr_bc[:, tg * 512:(tg + 1) * 512],
                              ps[4 + tg][:, :]).then_inc(vbe, 1)
            for tg in range(4):
                v.wait_ge(bb, 4 + tg + 1)
                v.tensor_copy(r2_bc[:, tg * 512:(tg + 1) * 512],
                              ps[tg][:, :]).then_inc(vbe, 1)
            for k in range(KC):
                v.wait_ge(ld, 16 * (KC + k + 1))
                if k >= 1:
                    v.wait_ge(hb, 2 * k)
                v.tensor_tensor(t1[:], xf[:, (k % 2) * T:(k % 2 + 1) * T],
                                r_bc[:], ALU.mult).then_inc(vb, 1)
            for pc, p in enumerate(PIECES):
                if pc >= 2:
                    v.wait_ge(stw, 16 * (pc - 1))
                ncol = (p["k1"] - p["k0"]) * 128
                nhalf = (ncol + 2047) // 2048
                for h in range(nhalf):
                    h0 = h * 2048
                    h1 = min(h0 + 2048, ncol)
                    k0h = p["k0"] + h0 // 128
                    nk = (h1 - h0) // 128
                    v.wait_ge(ldq, 48 * pc + 16)
                    ins = v.tensor_tensor(
                        AP(qint, 0, [[2048, 128], [128, nk], [8, 16], [1, 8]]),
                        AP(swq, (pc % 2) * 1376 + (h0 // 128 * 16),
                           [[2 * 1376, 128], [16, nk], [1, 16], [0, 8]]),
                        AP(shc, 0, [[8, 128], [0, nk], [0, 16], [1, 8]]),
                        ALU.logical_shift_right)
                    chain(ins)
                    ins = v.tensor_scalar(qint[:, 0:h1 - h0], qint[:, 0:h1 - h0], 15,
                                          None, ALU.bitwise_and)
                    chain(ins)
                    for si, (c0, w) in enumerate(p["sl"]):
                        if not (h0 <= c0 < h1):
                            continue
                        S = SLB[pc] + si
                        v.wait_ge(mmb, S + 1)
                        off = (pc % 2) * 4096 + c0
                        ins = v.tensor_tensor(wsB[:, off:off + w],
                                              qint[:, c0 - h0:c0 - h0 + w],
                                              ps[4 + S % 2][:, 0:w], ALU.mult)
                        chain(ins)
                        v.tensor_tensor(wsB[:, off:off + w], wsB[:, off:off + w],
                                        ps[6 + S % 2][:, 0:w],
                                        ALU.subtract).then_inc(vd, 1)
            for m in range(KC):
                v.wait_ge(mmg, m + 1)
                if m >= 1:
                    v.wait_ge(stt2, 16 * m)
                for tg in range(4):
                    ins = v.tensor_copy(tstag[:, tg * 512:(tg + 1) * 512], ps[tg][:, :])
                ins.then_inc(vte, 1)
            for g in range(NG):
                for m in range(KC):
                    c = 64 * g + m
                    v.wait_ge(mmg, Jo(g, m))
                    v.wait_ge(ldx, 16 * (c + 1))
                    gs = slice(g * GT, (g + 1) * GT)
                    ins = v.tensor_tensor(gsc[:, 0:GT],
                                          xf2[:, (c % 6) * GT:(c % 6 + 1) * GT],
                                          r_bc[:, gs], ALU.mult)
                    chain(ins)
                    ins = v.tensor_tensor(gsc[:, 0:GT], gsc[:, 0:GT], mr_bc[:, gs],
                                          ALU.subtract)
                    chain(ins)
                    ins = v.tensor_tensor(gsc[:, GT:2 * GT], ps[2 + m % 2][:, :],
                                          r2_bc[:, gs], ALU.mult)
                    chain(ins)
                    v.tensor_tensor(h2g[:, m * GT:(m + 1) * GT], gsc[:, 0:GT],
                                    gsc[:, GT:2 * GT], ALU.add).then_inc(vev, 1)
                v.wait_ge(pst, 33 * g + 32)
                ins = v.tensor_scalar(ps[4][32:33, :], ps[4][0:1, :], 1.0 / D, None,
                                      ALU.mult)
                ins.then_inc(vr3, 1)
                v.wait_ge(aag, 119 * g + 33)
                ins = v.reciprocal(r3row[0:1, :], ps[4][64:65, :])
                ins.then_inc(vr3, 1)
                v.wait_ge(pst, 33 * g + 33)
                ins = v.tensor_copy(r3bc[:, :], ps[5][:, :])
                ins.then_inc(vr3, 1)
                nvv[0] += 0
                v.wait_ge(vr3, 3 * g + 3)
                for k in range(KC):
                    ins = v.tensor_tensor(gsc[:, 0:GT], h2g[:, k * GT:(k + 1) * GT],
                                          r3bc[:, :], ALU.mult)
                    chain(ins)
                    v.tensor_scalar(H2s[:, k * GT:(k + 1) * GT], gsc[:, 0:GT],
                                    l2col[:, k:k + 1], None,
                                    ALU.mult).then_inc(vh2, 1)
                for i in range(86):
                    v.wait_ge(aag, 119 * g + 33 + i + 1)
                    v.wait_ge(mmg, Jgu(g, i, True))
                    v.tensor_tensor(gbuf[:, i * GT:(i + 1) * GT], sg[:, :],
                                    ps[2 + i % 2][:, :], ALU.mult).then_inc(vgu, 1)
                for mp in range(KC):
                    c = 64 * g + 32 + mp
                    cst = 32 * g + mp
                    v.wait_ge(mmg, Jdn(g, mp, 2))
                    v.wait_ge(ldx, 16 * (c + 1))
                    if cst >= 2:
                        v.wait_ge(stm, 16 * (cst - 1))
                    ins = v.tensor_tensor(gsc[:, 0:GT], ps[mp % 2][:, :],
                                          h2g[:, mp * GT:(mp + 1) * GT], ALU.add)
                    chain(ins)
                    v.tensor_tensor(xnst[:, (mp % 2) * GT:(mp % 2 + 1) * GT],
                                    gsc[:, 0:GT],
                                    xf2[:, (c % 6) * GT:(c % 6 + 1) * GT],
                                    ALU.add).then_inc(vev, 1)

        ctx.cleanup(block)

    ctx.close()
    return nc




def build_attn(stop_after='full', sub=9, dump=False):
    PH = {'p1': 1, 'p2': 2, 'p3': 3, 'full': 4}[stop_after]
    ctx = Ctx()
    nc = ctx.nc
    x_in = nc.dram_tensor("x_fm", [KC, 128, T], F32, kind="ExternalInput")
    x_out = nc.dram_tensor("x_fm_o", [KC, 128, T], F32, kind="ExternalOutput")
    qkvb = nc.dram_tensor("qkv_blk", [96, 128, 32, 128], F32, kind="ExternalInput")
    gateb = nc.dram_tensor("gate_blk", [32, 128, 32, 128], F32, kind="ExternalInput")
    outb = nc.dram_tensor("out_blk", [32, 128, 32, 128], F32, kind="ExternalInput")
    nwc = nc.dram_tensor("normw_col", [128, 32], F32, kind="ExternalInput")
    qdb = nc.dram_tensor("qd_bc", [32, 128, 256], F32, kind="ExternalInput")
    kdb = nc.dram_tensor("kd_bc", [32, 128, 256], F32, kind="ExternalInput")
    ddp = nc.dram_tensor("dd_T", [32, 128, 2, 256], F32, kind="ExternalInput")
    cwd = nc.dram_tensor("cw_d", [128, 128], F32, kind="ExternalInput")
    bdp = nc.dram_tensor("bd_pows", [128, 32 * 9], F32, kind="ExternalInput")
    gate_d = nc.dram_tensor("gate_d", [KC, 128, T], BF16, kind="Internal")
    attn_d = nc.dram_tensor("attn_d", [KC, 128, T], F32, kind="Internal")
    q_d = nc.dram_tensor("q_d", [32, 128, T], BF16, kind="Internal")
    sc_d = nc.dram_tensor("sc_d", [32 * 128, 128], F32, kind="Internal")
    scg_d = nc.dram_tensor("scg_d", [4 * 32 * 128, 128], F32, kind="Internal")
    if dump:
        scgo = nc.dram_tensor("scg_o", [4 * 32 * 128, 128], F32, kind="ExternalOutput")
        attno = nc.dram_tensor("attn_o", [KC, 128, T], F32, kind="ExternalOutput")

    EPSA = 1e-5
    NH, NB, BS = 32, 8, 256

    onesb = ctx.sbuf("onesb", [128, 1], BF16)
    onesfr = ctx.sbuf("onesfr", [1, 128], F32)
    epsc = ctx.sbuf("epsc", [128, 1], F32)
    identb = ctx.sbuf("identb", [128, 128], BF16)
    nwcol = ctx.sbuf("nwcol", [128, 32], F32)
    cw = ctx.sbuf("cw", [128, 128], F32)
    bdp_s = ctx.sbuf("bdp_s", [128, 32 * 9], F32)
    H = ctx.sbuf("H", [128, KC * T], BF16)
    xf = ctx.sbuf("xf", [128, 2 * T], F32)
    sq = ctx.sbuf("sq", [128, T], BF16)
    r_bc = ctx.sbuf("r_bc", [128, T], F32)
    mr_bc = ctx.sbuf("mr_bc", [128, T], F32)
    sra = ctx.sbuf("sra", [1, T], F32)
    srb = ctx.sbuf("srb", [1, T], F32)
    rows_r = ctx.sbuf("rows_r", [1, T], F32)
    rows_mr = ctx.sbuf("rows_mr", [1, T], F32)
    t1 = ctx.sbuf("t1", [128, T], F32)
    ctx.free_top("xf")
    ws = ctx.sbuf("ws", [128, 2 * 4096], BF16)
    qh = ctx.sbuf("qh", [128, T], BF16)
    kh = ctx.sbuf("kh", [128, T], BF16)
    vh = ctx.sbuf("vh", [128, T], BF16)
    qsc = ctx.sbuf("qsc", [128, T], BF16)
    ksc = ctx.sbuf("ksc", [128, T], BF16)
    kT = ctx.sbuf("kT", [128, T], BF16)
    vT = ctx.sbuf("vT", [128, T], BF16)
    sdec = ctx.sbuf("sdec", [128, 512], BF16)
    astage = ctx.sbuf("astage", [128, T], F32)
    kv = ctx.sbuf("kv", [128, 128], F32)
    kvbf = ctx.sbuf("kvbf", [128, 128], BF16)
    kvin = ctx.sbuf("kvin", [128, 128], F32)
    kvinb = ctx.sbuf("kvinb", [128, 128], BF16)
    ddh = ctx.sbuf("ddh", [128, 512], BF16)
    qdbc = ctx.sbuf("qdbc", [128, 256], BF16)
    kdbc = ctx.sbuf("kdbc", [128, 256], BF16)
    sgb = ctx.sbuf("sgb", [128, 512], F32)
    qs2 = ctx.sbuf("qs2", [128, T], BF16)
    ctx.free_top("ws")
    rstd_bc = ctx.sbuf("rstd_bc", [128, T], BF16)
    sqb = ctx.sbuf("sqb", [128, T], BF16)
    gb = ctx.sbuf("gb", [128, T], BF16)
    af = ctx.sbuf("af", [128, T], F32)
    ws2 = ctx.sbuf("ws2", [128, 2 * 4096], BF16)
    xn = ctx.sbuf("xn", [128, T], F32)
    xfp = ctx.sbuf("xfp", [128, 2 * T], F32)
    srowA = ctx.sbuf("srowA", [1, T], F32)

    ps = [ctx.psum(f"ps{i}", [128, 512], F32) for i in range(7)]
    p7b = ctx.psum("p7b", [128, 1024], BF16)

    gini = ctx.sem("gini")
    lda = ctx.sem("lda")
    ld = ctx.sem("ld")
    acts = ctx.sem("acts")
    pes = ctx.sem("pes")
    vstat = ctx.sem("vstat")
    aa = ctx.sem("aa")
    bb = ctx.sem("bb")
    vb = ctx.sem("vb")
    hb = ctx.sem("hb")
    ldw = ctx.sem("ldw")
    mmg = ctx.sem("mmg")
    ea = ctx.sem("ea")
    gstm = ctx.sem("gstm")
    qstm = ctx.sem("qstm")
    ldh = ctx.sem("ldh")
    vq = ctx.sem("vq")
    mmt = ctx.sem("mmt")
    vt = ctx.sem("vt")
    ps1 = ctx.sem("ps1")
    vsd = ctx.sem("vsd")
    pi = ctx.sem("pi")
    pu = ctx.sem("pu")
    vkv = ctx.sem("vkv")
    ek = ctx.sem("ek")
    eb = ctx.sem("eb")
    astm = ctx.sem("astm")
    scst = ctx.sem("scst")
    cc = ctx.sem("cc")
    ldsg = ctx.sem("ldsg")
    vci = ctx.sem("vci")
    eki = ctx.sem("eki")
    vq2 = ctx.sem("vq2")
    mmc = ctx.sem("mmc")
    vce = ctx.sem("vce")
    vbe = ctx.sem("vbe")
    dmp = ctx.sem("dmp")
    ldz = ctx.sem("ldz")
    acz = ctx.sem("acz")
    pz = ctx.sem("pz")
    vz = ctx.sem("vz")
    evu = ctx.sem("evu")
    stm = ctx.sem("stm")
    vv = ctx.sem("vv")

    with nc.Block() as block:
        @block.sync
        def _(s):
            for k in range(KC):
                if k >= 2:
                    s.wait_ge(acts, k - 1)
                s.dma_start(out=xf[:, (k % 2) * T:(k % 2 + 1) * T],
                            in_=x_in[k, :, :]).then_inc(ld, 16)
            for k in range(KC):
                if k == 0:
                    s.wait_ge(acts, KC - 1)
                elif k == 1:
                    s.wait_ge(acts, KC)
                else:
                    s.wait_ge(vb, k - 1)
                s.dma_start(out=xf[:, (k % 2) * T:(k % 2 + 1) * T],
                            in_=x_in[k, :, :]).then_inc(ld, 16)
            for m in range(KC):
                s.wait_ge(ea, m + 1)
                s.dma_start(out=gate_d[m, :, :], in_=qh[:]).then_inc(gstm, 16)
            for h in range(NH if PH >= 2 else 0):
                s.wait_ge(ea, KC + 3 * h + 1)
                s.dma_start(out=q_d[h, :, :], in_=qh[:]).then_inc(qstm, 16)
                if sub >= 3:
                    s.wait_ge(eb, NB * (h + 1))
                    s.dma_start(out=attn_d[h, :, :], in_=astage[:]).then_inc(astm, 16)
                    s.wait_ge(vkv, NB * (h + 1))
                    s.dma_start(out=sc_d[128 * h:128 * (h + 1), :], in_=kv[:]
                                ).then_inc(scst, 16)
            if PH >= 3:
                s.wait_ge(cc, 1)
                s.wait_ge(qstm, 16 * NH)
            for h in range(NH if PH >= 3 else 0):
                if h >= 1:
                    s.wait_ge(vci, h)
                for cp in range(4):
                    s.dma_start(out=sgb[:, cp * 128:(cp + 1) * 128],
                                in_=scg_d[cp * 4096 + 128 * h:cp * 4096 + 128 * (h + 1), :]
                                ).then_inc(ldsg, 16)
                if h >= 1:
                    s.wait_ge(vq2, h)
                s.dma_start(out=qh[:], in_=q_d[h, :, :]).then_inc(ldsg, 16)
            if PH >= 4:
                s.wait_ge(astm, 32 * NH)
            for k in range(KC if PH >= 4 else 0):
                if k >= 1:
                    s.wait_ge(acz, k)
                s.dma_start(out=af[:, :],
                            in_=attn_d[k, :, :]).then_inc(ldz, 16)
            for k in range(KC if (PH >= 4 and sub >= 6) else 0):
                if k == 0:
                    s.wait_ge(acz, KC)
                if k >= 1:
                    s.wait_ge(vz, k + 2)
                s.dma_start(out=af[:, :],
                            in_=attn_d[k, :, :]).then_inc(ldz, 16)
                s.dma_start(out=gb[:], in_=gate_d[k, :, :]).then_inc(ldz, 16)
            for m in range(KC if (PH >= 4 and sub >= 6) else 0):
                if m >= 2:
                    s.wait_ge(evu, m - 1)
                s.dma_start(out=xfp[:, (m % 2) * T:(m % 2 + 1) * T],
                            in_=x_in[m, :, :]).then_inc(ldz, 16)
                s.wait_ge(evu, m + 1)
                s.dma_start(out=x_out[m, :, :], in_=xn[:]).then_inc(stm, 16)
            if PH >= 4 and sub == 4.2:
                s.wait_ge(acz, KC)
            elif PH >= 4 and sub == 4.5:
                s.wait_ge(vz, 1)
            elif PH >= 4 and sub == 4.7:
                s.wait_ge(vz, 2)
            elif PH >= 4 and sub == 4.8:
                s.wait_ge(pz, KC + 4)
            elif PH >= 4 and sub == 5:
                s.wait_ge(acz, KC + 1)
            elif PH >= 4:
                s.wait_ge(stm, 16 * KC)
                if dump:
                    s.dma_start(out=scgo[:, :], in_=scg_d[:, :]).then_inc(dmp, 16)
                    for k in range(KC):
                        s.dma_start(out=attno[k, :, :],
                                    in_=attn_d[k, :, :]).then_inc(dmp, 16)
                    s.wait_ge(dmp, 16 * (KC + 1))
            elif PH >= 2:
                if sub >= 3:
                    s.wait_ge(astm, 16 * NH)
                    s.wait_ge(scst, 16 * NH)
                else:
                    s.wait_ge(qstm, 16 * NH)
            else:
                s.wait_ge(gstm, 16 * KC)

        @block.gpsimd
        def _(g):
            g.memset(onesb[:], 1.0)
            g.memset(onesfr[:], 1.0)
            g.memset(epsc[:], EPSA)
            g.iota(identb[:], [[1, 128]], channel_multiplier=-1,
                   allow_small_or_imprecise_dtypes=True)
            g.tensor_single_scalar(identb[:], identb[:], 0.0, ALU.is_equal)
            g.engine_nop().then_inc(gini, 1)
            g.dma_start(out=nwcol[:], in_=nwc[:, :]).then_inc(lda, 16)
            g.dma_start(out=cw[:], in_=cwd[:, :]).then_inc(lda, 16)
            g.dma_start(out=bdp_s[:], in_=bdp[:, :]).then_inc(lda, 16)
            for k in range(KC):
                g.wait_ge(vb, k + 1)
                g.tensor_tensor(H[:, k * T:(k + 1) * T], t1[:], mr_bc[:],
                                ALU.subtract).then_inc(hb, 1)
            def head_consts(h):
                if h >= 1 and sub >= 2:
                    g.wait_ge(vq, 2 * h)
                    if sub >= 3:
                        g.wait_ge(vsd, NB * h)
                g.dma_start(out=qdbc[:], in_=qdb[h, :, :]).then_inc(ldh, 16)
                g.dma_start(out=kdbc[:], in_=kdb[h, :, :]).then_inc(ldh, 16)
                g.dma_start(out=ddh[:].rearrange("p (a n) -> p a n", a=2),
                            in_=ddp[h, :, :, :]).then_inc(ldh, 16)
                if sub >= 3:
                    g.wait_ge(scst, 16 * h)
                g.memset(kv[:], 0.0)
                g.memset(kvbf[:], 0.0)
                g.engine_nop().then_inc(ldh, 1)

            for sidx in range(KC + 96 if PH >= 2 else KC):
                if sidx == 0:
                    g.wait_ge(hb, KC)
                elif sidx >= 2:
                    g.wait_ge(mmg, sidx - 1)
                src = gateb[sidx, :, :, :] if sidx < KC else qkvb[sidx - KC, :, :, :]
                g.dma_start(out=ws[:, (sidx % 2) * 4096:(sidx % 2 + 1) * 4096],
                            in_=src).then_inc(ldw, 16)
                if sidx >= KC and (sidx - KC) % 3 == 0:
                    head_consts((sidx - KC) // 3)
            if PH >= 3:
                g.wait_ge(scst, 16 * NH)
            if PH >= 3:
                g.collective_compute(
                "AllGather", ALU.bypass,
                    replica_groups=[[0, 1, 2, 3], [4, 5, 6, 7]],
                    ins=[sc_d.ap().opt()], outs=[scg_d.ap().opt()]).then_inc(cc, 1)
            for h in range(NH if PH >= 3 else 0):
                if h == 0:
                    g.wait_ge(vq, 2 * NH)
                else:
                    g.wait_ge(vq2, h)
                g.dma_start(out=qdbc[:], in_=qdb[h, :, :]).then_inc(ldh, 16)
                g.wait_ge(vce, h + 1)
                g.dma_start(out=attn_d[h, :, :], in_=astage[:],
                            accum_op=ALU.add).then_inc(astm, 16)
            for m in range(KC if (PH >= 4 and sub >= 6) else 0):
                if m == 0:
                    g.wait_ge(astm, 32 * NH)
                if m >= 2:
                    g.wait_ge(mmg, KC + 96 + m - 1)
                g.dma_start(out=ws2[:, (m % 2) * 4096:(m % 2 + 1) * 4096],
                            in_=outb[m, :, :, :]).then_inc(ldw, 16)

        @block.scalar
        def _(a):
            for k in range(KC):
                a.wait_ge(ld, 16 * (k + 1))
                if k >= 1:
                    a.wait_ge(pes, k)
                a.activation(H[:, k * T:(k + 1) * T],
                             xf[:, (k % 2) * T:(k % 2 + 1) * T], AF.Copy)
                a.activation(sq[:, :], xf[:, (k % 2) * T:(k % 2 + 1) * T],
                             AF.Square).then_inc(acts, 1)
            a.wait_ge(vstat, 1)
            a.wait_ge(gini, 1)
            a.activation(rows_r[0:1, :], srb[0:1, :], AF.Sqrt,
                         bias=epsc[0:1, 0:1]).then_inc(aa, 4)
            for m in range(KC):
                a.wait_ge(mmg, m + 1)
                if m >= 1:
                    a.wait_ge(gstm, 16 * m)
                for tg in range(4):
                    ins = a.activation(qh[:, tg * 512:(tg + 1) * 512], ps[tg][:, :],
                                       AF.Sigmoid)
                ins.then_inc(ea, 1)
            for h in range(NH if PH >= 2 else 0):
                for ci in range(3):
                    m2 = 3 * h + ci
                    a.wait_ge(mmg, KC + m2 + 1)
                    if ci == 0:
                        if h == 0:
                            a.wait_ge(gstm, 16 * KC)
                        a.wait_ge(qstm, 16 * h)
                    dst = (qh, kh, vh)[ci]
                    for tg in range(4):
                        ins = a.activation(dst[:, tg * 512:(tg + 1) * 512],
                                           ps[tg][:, :], AF.Silu)
                    ins.then_inc(ea, 1)
                for b_ in range(NB if sub >= 3 else 0):
                    BA = NB * h + b_
                    a.wait_ge(pi, BA + 1)
                    if b_ == 0:
                        a.wait_ge(astm, 16 * h)
                    a.activation(astage[:, b_ * 256:(b_ + 1) * 256],
                                 ps[5][:, 0:256], AF.Copy).then_inc(eb, 1)
                    a.wait_ge(vkv, BA + 1)
                    a.activation(kvbf[:], kv[:], AF.Copy).then_inc(ek, 1)
            for h in range(NH if PH >= 3 else 0):
                a.wait_ge(vci, h + 1)
                if h >= 1:
                    a.wait_ge(mmc, h)
                a.activation(kvinb[:], kvin[:], AF.Copy).then_inc(eki, 1)
                a.wait_ge(mmc, h + 1)
                a.wait_ge(astm, 16 * NH + 16 * h)
                for tg in range(4):
                    ins = a.activation(astage[:, tg * 512:(tg + 1) * 512],
                                       ps[tg][:, :], AF.Copy)
                ins.then_inc(vce, 1)
            for k in range(KC if PH >= 4 else 0):
                a.wait_ge(ldz, 16 * (k + 1))
                if k >= 1 and sub >= 4.5:
                    a.wait_ge(pz, k)
                a.activation(sqb[:, :], af[:, :],
                             AF.Square).then_inc(acz, 1)
            if PH < 4 or sub < 4.7:
                return
            a.wait_ge(vz, 1)
            a.activation(xn[0:1, :], srowA[0:1, :], AF.Sqrt,
                         bias=epsc[0:1, 0:1]).then_inc(acz, 1)

        @block.tensor
        def _(te):
            te.wait_ge(gini, 1)
            for k in range(KC):
                te.wait_ge(acts, k + 1)
                st, sp = (k == 0), (k == KC - 1)
                for tg in range(4):
                    te.matmul(ps[tg][0:1, :], onesb[:],
                              H[:, k * T + tg * 512:k * T + (tg + 1) * 512],
                              start=st, stop=sp)
                for tg in range(4):
                    dst = ps[4 + tg][0:1, :] if tg < 3 else ps[6][32:33, :]
                    st_ = st and tg != 3
                    sp_ = sp and tg != 2
                    ins = te.matmul(dst, onesb[:], sq[:, tg * 512:(tg + 1) * 512],
                                    start=st_, stop=sp_)
                ins.then_inc(pes, 1)
            te.wait_ge(vstat, 2)
            for tg in range(4):
                te.matmul(ps[tg][:, :], onesfr[:], rows_r[0:1, tg * 512:(tg + 1) * 512],
                          start=True, stop=True)
                if tg == 3:
                    te.wait_ge(vbe, 1)
                te.matmul(ps[4 + tg][:, :] if tg < 3 else ps[6][:, :], onesfr[:],
                          rows_mr[0:1, tg * 512:(tg + 1) * 512],
                          start=True, stop=True).then_inc(bb, 1)
            def gemm(sidx, wbuf):
                te.wait_ge(ldw, 16 * (sidx + 1))
                if sidx == 0:
                    te.wait_ge(vstat, 3)
                else:
                    te.wait_ge(ea, sidx)
                for k in range(KC):
                    st, sp = (k == 0), (k == KC - 1)
                    for tg in range(4):
                        ins = te.matmul(
                            ps[tg][:, :],
                            wbuf[:, (sidx % 2) * 4096 + k * 128:(sidx % 2) * 4096 + (k + 1) * 128],
                            H[:, k * T + tg * 512:k * T + (tg + 1) * 512],
                            start=st, stop=sp)
                ins.then_inc(mmg, 1)

            for m in range(KC):
                gemm(m, ws)
            for h in range(NH if PH >= 2 else 0):
                for ci in range(3):
                    gemm(KC + 3 * h + ci, ws)
                if sub >= 2.5:
                    te.wait_ge(vq, 2 * h + 2)
                    te.wait_ge(ea, KC + 3 * h + 3)
                for tt in range(32 if sub >= 2.5 else 0):
                    src = ksc if tt < 16 else vh
                    t0 = tt % 16
                    TA = 32 * h + tt
                    if TA >= 1:
                        te.wait_ge(vt, TA)
                    te.transpose(p7b[:, (tt % 8) * 128:(tt % 8 + 1) * 128],
                                 src[:, t0 * 128:(t0 + 1) * 128],
                                 identb[:]).then_inc(mmt, 1)
                if sub >= 3:
                    te.wait_ge(vt, 32 * (h + 1))
                for b_ in range(NB if sub >= 3 else 0):
                    BA = NB * h + b_
                    if BA >= 1:
                        te.wait_ge(vsd, BA)
                    for jm in range(2):
                        ins = te.matmul(ps[4][:, jm * 256:(jm + 1) * 256],
                                        kh[:, b_ * 256 + jm * 128:b_ * 256 + (jm + 1) * 128],
                                        qh[:, b_ * 256:(b_ + 1) * 256],
                                        start=True, stop=True)
                    ins.then_inc(ps1, 1)
                    if sub == 3:
                        continue
                    te.wait_ge(vsd, BA + 1)
                    if BA >= 1:
                        te.wait_ge(eb, BA)
                    if b_ == 0:
                        te.wait_ge(ldh, 49 * h + 49)
                    for jm in range(2):
                        te.matmul(ps[5][:, 0:256],
                                  vT[:, (b_ * 2 + jm) * 128:(b_ * 2 + jm + 1) * 128],
                                  sdec[:, jm * 256:(jm + 1) * 256],
                                  start=(jm == 0), stop=False)
                    te.wait_ge(ek, BA) if b_ > 0 else None
                    te.matmul(ps[5][:, 0:256], kvbf[:],
                              qsc[:, b_ * 256:(b_ + 1) * 256],
                              start=False, stop=True).then_inc(pi, 1)
                    if BA >= 1:
                        te.wait_ge(vkv, BA)
                    for jm in range(2):
                        ins = te.matmul(ps[6][:, 0:128],
                                        kT[:, (b_ * 2 + jm) * 128:(b_ * 2 + jm + 1) * 128],
                                        vT[:, (b_ * 2 + jm) * 128:(b_ * 2 + jm + 1) * 128],
                                        start=(jm == 0), stop=(jm == 1))
                    ins.then_inc(pu, 1)
            for h in range(NH if PH >= 3 else 0):
                te.wait_ge(eki, h + 1)
                te.wait_ge(vq2, h + 1)
                if h >= 1:
                    te.wait_ge(vce, h)
                for tg in range(4):
                    ins = te.matmul(ps[tg][:, :], kvinb[:],
                                    qs2[:, tg * 512:(tg + 1) * 512],
                                    start=True, stop=True)
                ins.then_inc(mmc, 1)
            for k in range(KC if (PH >= 4 and sub >= 4.5) else 0):
                te.wait_ge(acz, k + 1)
                st, sp = (k == 0), (k == KC - 1)
                for tg in range(4):
                    dst = ps[4 + tg][0:1, :] if tg < 3 else ps[6][32:33, :]
                    st_ = st and tg != 3
                    sp_ = sp and tg != 2
                    ins = te.matmul(dst, onesb[:], sqb[:, tg * 512:(tg + 1) * 512],
                                    start=st_, stop=sp_)
                ins.then_inc(pz, 1)
            if PH < 4 or sub < 4.8:
                return
            te.wait_ge(acz, KC + 1)
            te.wait_ge(vz, 2)
            for tg in range(4):
                te.matmul(ps[tg][:, :], onesfr[:],
                          srowA[0:1, tg * 512:(tg + 1) * 512],
                          start=True, stop=True).then_inc(pz, 1)
            for m in range(KC if sub >= 6 else 0):
                sidx = KC + 96 + m
                te.wait_ge(ldw, 16 * (sidx + 1))
                te.wait_ge(vz, KC + 2)
                if m >= 1:
                    te.wait_ge(evu, m)
                for k in range(KC):
                    st, sp = (k == 0), (k == KC - 1)
                    for tg in range(4):
                        ins = te.matmul(
                            ps[tg][:, :],
                            ws2[:, (m % 2) * 4096 + k * 128:(m % 2) * 4096 + (k + 1) * 128],
                            H[:, k * T + tg * 512:k * T + (tg + 1) * 512],
                            start=st, stop=sp)
                ins.then_inc(mmg, 1)

        @block.vector
        def _(v):
            nvv = [0]

            def chain(ins):
                nvv[0] += 1
                ins.then_inc(vv, 1)
                v.wait_ge(vv, nvv[0])

            v.wait_ge(pes, KC)
            for tg in range(4):
                ins = v.tensor_copy(sra[0:1, tg * 512:(tg + 1) * 512], ps[tg][0:1, :])
            for tg in range(4):
                ins = v.tensor_copy(srb[0:1, tg * 512:(tg + 1) * 512],
                                    ps[4 + tg][0:1, :] if tg < 3 else ps[6][32:33, :])
            chain(ins)
            ins = v.tensor_scalar(sra[0:1, :], sra[0:1, :], 1.0 / D, None, ALU.mult)
            ins = v.tensor_scalar(srb[0:1, :], srb[0:1, :], 1.0 / D, None, ALU.mult)
            chain(ins)
            ins = v.tensor_tensor(rows_mr[0:1, :], sra[0:1, :], sra[0:1, :], ALU.mult)
            chain(ins)
            v.tensor_tensor(srb[0:1, :], srb[0:1, :], rows_mr[0:1, :],
                            ALU.subtract).then_inc(vstat, 1)
            v.wait_ge(aa, 4)
            ins = v.reciprocal(rows_mr[0:1, :], rows_r[0:1, :])
            chain(ins)
            ins = v.tensor_tensor(sra[0:1, :], sra[0:1, :], rows_mr[0:1, :], ALU.mult)
            chain(ins)
            ins = v.tensor_copy(rows_r[0:1, :], rows_mr[0:1, :])
            ins = v.tensor_copy(rows_mr[0:1, :], sra[0:1, :])
            ins.then_inc(vstat, 1)
            for tg in range(4):
                v.wait_ge(bb, tg + 1)
                v.tensor_copy(r_bc[:, tg * 512:(tg + 1) * 512], ps[tg][:, :])
                ins = v.tensor_copy(mr_bc[:, tg * 512:(tg + 1) * 512],
                                    ps[4 + tg][:, :] if tg < 3 else ps[6][:, :])
                if tg == 2:
                    ins.then_inc(vbe, 1)
            ins.then_inc(vstat, 1)
            v.wait_ge(vstat, 3)
            for k in range(KC):
                v.wait_ge(ld, 16 * (KC + k + 1))
                if k >= 1:
                    v.wait_ge(hb, k)
                v.tensor_tensor(t1[:], xf[:, (k % 2) * T:(k % 2 + 1) * T],
                                r_bc[:], ALU.mult).then_inc(vb, 1)
            for h in range((NH if PH >= 2 else 0) if sub >= 2 else 0):
                v.wait_ge(ea, KC + 3 * h + 2)
                v.wait_ge(ldh, 49 * h + 48)
                if h >= 1:
                    if sub >= 3:
                        v.wait_ge(pi, NB * h)
                    if sub >= 2.5:
                        v.wait_ge(mmt, 32 * h)
                ins = v.tensor_tensor(
                    qsc[:, :], qh[:, :],
                    AP(qdbc, 0, [[256, 128], [0, 8], [1, 256]]),
                    ALU.mult)
                ins.then_inc(vq, 1)
                ins = v.tensor_tensor(
                    ksc[:, :], kh[:, :],
                    AP(kdbc, 0, [[256, 128], [0, 8], [1, 256]]),
                    ALU.mult)
                ins.then_inc(vq, 1)
                for tt in range(32 if sub >= 2.5 else 0):
                    TA = 32 * h + tt
                    v.wait_ge(mmt, TA + 1)
                    dst = kT if tt < 16 else vT
                    t0 = tt % 16
                    v.tensor_copy(dst[:, t0 * 128:(t0 + 1) * 128],
                                  p7b[:, (tt % 8) * 128:(tt % 8 + 1) * 128]
                                  ).then_inc(vt, 1)
                for b_ in range(NB if sub >= 3 else 0):
                    BA = NB * h + b_
                    v.wait_ge(ps1, BA + 1)
                    ins = v.tensor_tensor(sdec[:, 0:256], ps[4][:, 0:256],
                                          ddh[:, 0:256], ALU.mult)
                    ins = v.tensor_tensor(sdec[:, 256:512], ps[4][:, 256:512],
                                          ddh[:, 256:512], ALU.mult)
                    ins.then_inc(vsd, 1)
                    if sub == 3:
                        continue
                    v.wait_ge(pu, BA + 1)
                    v.scalar_tensor_tensor(
                        kv[:], kv[:], bdp_s[:, 9 * h + 1:9 * h + 2], ps[6][:, 0:128],
                        ALU.mult, ALU.add).then_inc(vkv, 1)
            for h in range(NH if PH >= 3 else 0):
                v.wait_ge(ldsg, 80 * h + 64)
                if h >= 1:
                    v.wait_ge(eki, h)
                ins = v.tensor_scalar(kvin[:], sgb[:, 0:128], cw[:, 4 * h:4 * h + 1],
                                      None, ALU.mult)
                chain(ins)
                for cp in range(1, 4):
                    ins = v.scalar_tensor_tensor(
                        kvin[:], sgb[:, cp * 128:(cp + 1) * 128],
                        cw[:, 4 * h + cp:4 * h + cp + 1], kvin[:],
                        ALU.mult, ALU.add)
                    if cp < 3:
                        chain(ins)
                nvv[0] += 1
                ins.then_inc(vv, 1)
                v.wait_ge(vv, nvv[0])
                v.engine_nop().then_inc(vci, 1)
                v.wait_ge(ldsg, 80 * h + 80)
                v.wait_ge(ldh, 49 * NH + 16 * (h + 1))
                if h >= 1:
                    v.wait_ge(mmc, h)
                for b_ in range(NB):
                    ins = v.scalar_tensor_tensor(
                        qs2[:, b_ * 256:(b_ + 1) * 256],
                        qh[:, b_ * 256:(b_ + 1) * 256],
                        bdp_s[:, 9 * h + b_:9 * h + b_ + 1],
                        AP(qdbc, 0, [[256, 128], [1, 256]]),
                        ALU.mult, ALU.mult)
                ins.then_inc(vq2, 1)
            if PH < 4 or sub < 4.5:
                return
            v.wait_ge(pz, KC)
            for tg in range(4):
                ins = v.tensor_copy(srowA[0:1, tg * 512:(tg + 1) * 512],
                                    ps[4 + tg][0:1, :] if tg < 3 else ps[6][32:33, :])
            chain(ins)
            v.tensor_scalar(srowA[0:1, :], srowA[0:1, :], 1.0 / D, None,
                            ALU.mult).then_inc(vz, 1)
            if sub < 4.7:
                return
            v.wait_ge(acz, KC + 1)
            ins = v.reciprocal(srowA[0:1, :], xn[0:1, :])
            chain(ins)
            v.engine_nop().then_inc(vz, 1)
            if sub < 5:
                return
            for tg in range(4):
                v.wait_ge(pz, KC + tg + 1)
                ins = v.tensor_copy(rstd_bc[:, tg * 512:(tg + 1) * 512],
                                    ps[tg][:, :])
            chain(ins)
            for k in range(KC if sub >= 6 else 0):
                v.wait_ge(ldz, 16 * KC + 32 * (k + 1))
                ins = v.tensor_tensor(xn[:],
                                      af[:, :],
                                      rstd_bc[:], ALU.mult)
                chain(ins)
                ins = v.tensor_scalar(xn[:], xn[:], nwcol[:, k:k + 1], None, ALU.mult)
                chain(ins)
                v.tensor_tensor(H[:, k * T:(k + 1) * T], xn[:], gb[:],
                                ALU.mult).then_inc(vz, 1)
            for m in range(KC if sub >= 6 else 0):
                v.wait_ge(mmg, KC + 96 + m + 1)
                v.wait_ge(ldz, 16 * KC + 32 * KC + 16 * (m + 1))
                if m >= 1:
                    v.wait_ge(stm, 16 * m)
                for tg in range(4):
                    ins = v.scalar_tensor_tensor(
                        xn[:, tg * 512:(tg + 1) * 512], ps[tg][:, :], 1.0,
                        xfp[:, (m % 2) * T + tg * 512:(m % 2) * T + (tg + 1) * 512],
                        ALU.mult, ALU.add)
                chain(ins)
                v.engine_nop().then_inc(evu, 1)

        ctx.cleanup(block)

    ctx.close()
    return nc



def build_prepl1():
    ctx = Ctx()
    nc = ctx.nc
    qzs = {
        "q1": (nc.dram_tensor("q1_qz", [32, 512], I32, kind="ExternalInput"),
               nc.dram_tensor("q1_sc", [32, 4096], F32, kind="ExternalInput"),
               nc.dram_tensor("zs_q1", [32, 4096], F32, kind="ExternalOutput"), 32, 4096),
        "o1": (nc.dram_tensor("o1_qz", [32, 512], I32, kind="ExternalInput"),
               nc.dram_tensor("o1_sc", [32, 4096], F32, kind="ExternalInput"),
               nc.dram_tensor("zs_o1", [32, 4096], F32, kind="ExternalOutput"), 32, 4096),
        "gu": (nc.dram_tensor("gu_qz", [32, 2752], I32, kind="ExternalInput"),
               nc.dram_tensor("gu_sc", [32, 22016], F32, kind="ExternalInput"),
               nc.dram_tensor("zs_gu", [32, 22016], F32, kind="ExternalOutput"), 32, 22016),
        "dn": (nc.dram_tensor("dn_qz", [86, 512], I32, kind="ExternalInput"),
               nc.dram_tensor("dn_sc", [86, 4096], F32, kind="ExternalInput"),
               nc.dram_tensor("zs_dn", [86, 4096], F32, kind="ExternalOutput"), 86, 4096),
    }
    sh1 = nc.dram_tensor("shift_l1z", [86, 8], I32, kind="ExternalInput")

    zrow = ctx.sbuf("zrow", [86, 2752], I32)
    zint = ctx.sbuf("zint", [86, 4096], I32)
    zf = ctx.sbuf("zf", [86, 4096], F32)
    snat = ctx.sbuf("snat", [86, 4096], F32)
    shz = ctx.sbuf("shz", [86, 8], I32)
    ld = ctx.sem("ld")
    vz = ctx.sem("vz")
    stm = ctx.sem("stm")

    with nc.Block() as block:
        @block.gpsimd
        def _(g):
            g.dma_start(out=shz[:], in_=sh1[:, :]).then_inc(ld, 16)
        nld = [16]
        nvz = [0]
        nst = [0]

        @block.gpsimd
        def _(g):
            for name, (qz, sc, zs, G, N) in qzs.items():
                for j in range((N + 4095) // 4096):
                    n0, n1 = j * 4096, min((j + 1) * 4096, N)
                    w0, w1 = n0 // 8, n1 // 8
                    g.wait_ge(vz, nvz[0])
                    g.dma_start(out=zrow[0:G, 0:w1 - w0], in_=qz[0:G, w0:w1]
                                ).then_inc(ld, 16)
                    g.dma_start(out=snat[0:G, 0:n1 - n0], in_=sc[0:G, n0:n1]
                                ).then_inc(ld, 16)
                    nld[0] += 32
                    nvz[0] += 3

        @block.vector
        def _(v):
            cld = 16
            cvz = 0
            for name, (qz, sc, zs, G, N) in qzs.items():
                for j in range((N + 4095) // 4096):
                    n0, n1 = j * 4096, min((j + 1) * 4096, N)
                    nw = (n1 - n0) // 8
                    cld += 32
                    v.wait_ge(ld, cld)
                    v.tensor_tensor(
                        AP(zint, 0, [[4096, G], [8, nw], [1, 8]]),
                        AP(zrow, 0, [[2752, G], [1, nw], [0, 8]]),
                        AP(shz, 0, [[8, G], [0, nw], [1, 8]]),
                        ALU.logical_shift_right).then_inc(vz, 1)
                    cvz += 1
                    v.wait_ge(vz, cvz)
                    v.tensor_scalar(zint[0:G, 0:n1 - n0], zint[0:G, 0:n1 - n0], 15,
                                    None, ALU.bitwise_and).then_inc(vz, 1)
                    cvz += 1
                    v.wait_ge(vz, cvz)
                    v.tensor_tensor(zf[0:G, 0:n1 - n0], zint[0:G, 0:n1 - n0],
                                    snat[0:G, 0:n1 - n0], ALU.mult).then_inc(vz, 1)
                    cvz += 1

        @block.sync
        def _(s):
            cvz = 0
            cst = 0
            for name, (qz, sc, zs, G, N) in qzs.items():
                for j in range((N + 4095) // 4096):
                    n0, n1 = j * 4096, min((j + 1) * 4096, N)
                    cvz += 3
                    s.wait_ge(vz, cvz)
                    s.dma_start(out=zs[0:G, n0:n1], in_=zf[0:G, 0:n1 - n0]
                                ).then_inc(stm, 16)
                    cst += 16
            s.wait_ge(stm, cst)

        ctx.cleanup(block)

    ctx.close()
    return nc



_CACHE = {}


def _get(name, fn, *args):
    key = (name,) + args
    if key not in _CACHE:
        _CACHE[key] = fn(*args)
    return _CACHE[key]


def kernel(**inputs):
    x = np.asarray(inputs["L0_attn_hidden_states"])
    c = host_prepare(inputs)
    xs = shard_x(x)
    cores = list(range(NCORES))

    nc = _get("init", build_init)
    res = run_bass_kernel_spmd(nc, [dict(
        x_tm=xs[i], ident=c["ident"], qz0=c["qz0"], s0_nat=c["s0_nat"],
        shift_l0z=c["shift_l0z"]) for i in cores], core_ids=cores)
    xfm = [r["x_fm"] for r in res.results]
    zs0 = [r["zs0"] for r in res.results]

    nc = _get("s1", build_ffn0, True)
    res = run_bass_kernel_spmd(nc, [dict(
        x_fm=xfm[i], bias0_col=c["bias0_col"], w0q_blk=c["w0q_blk"],
        s0_blk=c["s0_blk"], zs0=zs0[i], shift_l0=c["shift_l0"]) for i in cores],
        core_ids=cores)
    xfm = [r["x_fm_o"] for r in res.results]
    w0deq = [r["w0deq"] for r in res.results]

    nc = _get("s2", build_attn)
    res = run_bass_kernel_spmd(nc, [dict(
        x_fm=xfm[i], qkv_blk=c["qkv_blk"], gate_blk=c["gate_blk"],
        out_blk=c["out_blk"], normw_col=c["normw_col"], qd_bc=c["qd_bc"],
        kd_bc=c["kd_bc"], dd_T=c["dd_T"], cw_d=c["cw_percore"][i],
        bd_pows=c["bd_pows"]) for i in cores], core_ids=cores)
    xfm = [r["x_fm_o"] for r in res.results]

    nc = _get("s3", build_ffn0, False)
    res = run_bass_kernel_spmd(nc, [dict(
        x_fm=xfm[i], bias0_col=c["bias0_col"], w0deq=w0deq[i]) for i in cores],
        core_ids=cores)
    xfm = [r["x_fm_o"] for r in res.results]

    nc = _get("s4", build_molora)
    res = run_bass_kernel_spmd(nc, [dict(
        x_fm=xfm[i], wb_blk=c["wb_blk"], bb_col=c["bb_col"], A_blk2=c["A_blk2"],
        Wg_blk2=c["Wg_blk2"], B_blk=c["B_blk"], ident=c["ident"]) for i in cores],
        core_ids=cores)
    xfm = [r["x_fm_o"] for r in res.results]

    nc = _get("prepl1", build_prepl1)
    res = run_bass_kernel_spmd(nc, [dict(
        q1_qz=c["q1_qz"], q1_sc=c["q1_sc"], o1_qz=c["o1_qz"], o1_sc=c["o1_sc"],
        gu_qz=c["gu_qz"], gu_sc=c["gu_sc"], dn_qz=c["dn_qz"], dn_sc=c["dn_sc"],
        shift_l1z=c["shift_l1z"]) for i in cores], core_ids=cores)
    zsd = [{k: res.results[i][k] for k in ("zs_q1", "zs_o1", "zs_gu", "zs_dn")}
           for i in cores]

    nc = _get("s5", build_llama)
    res = run_bass_kernel_spmd(nc, [dict(
        x_fm=xfm[i], q1_blk=c["q1_blk"], o1_blk=c["o1_blk"], gu_blk=c["gu_blk"],
        dn_blk=c["dn_blk"], q1_sblk=c["q1_sblk"], o1_sblk=c["o1_sblk"],
        gu_sblk=c["gu_sblk"], dn_sblk=c["dn_sblk"], zs_q1=zsd[i]["zs_q1"],
        zs_o1=zsd[i]["zs_o1"], zs_gu=zsd[i]["zs_gu"], zs_dn=zsd[i]["zs_dn"],
        shift_l1=c["shift_l1"], ln1_col=c["ln1_col"], ln2_col=c["ln2_col"])
        for i in cores], core_ids=cores)
    xfm = [r["x_fm_o"] for r in res.results]

    nc = _get("s6", build_molora)
    res = run_bass_kernel_spmd(nc, [dict(
        x_fm=xfm[i], wb_blk=c["wb_blk"], bb_col=c["bb_col"], A_blk2=c["A_blk2"],
        Wg_blk2=c["Wg_blk2"], B_blk=c["B_blk"], ident=c["ident"]) for i in cores],
        core_ids=cores)
    xfm = [r["x_fm_o"] for r in res.results]

    nc = _get("fin", build_fin)
    res = run_bass_kernel_spmd(nc, [dict(
        x_fm=xfm[i], ident=c["ident"]) for i in cores], core_ids=cores)
    shards = [r["x_tm"] for r in res.results]

    out = np.empty((2, 8192, D), dtype=np.float32)
    for cid in range(NCORES):
        b, q = cid // 4, cid % 4
        out[b, q * T:(q + 1) * T, :] = shards[cid]
    return out


# version 05 (final)
# speedup vs baseline: 1.0088x; 1.0088x over previous
import numpy as np
import ml_dtypes

import concourse.bass as bass
import concourse.mybir as mybir
from concourse.bass_utils import run_bass_kernel_spmd

F32 = mybir.dt.float32
BF16 = mybir.dt.bfloat16
F16 = mybir.dt.float16
I32 = mybir.dt.int32
AF = mybir.ActivationFunctionType
ALU = mybir.AluOpType
AX = mybir.AxisListType

NCORES = 8
D = 4096
KC = 32
T = 2048
NTT = T // 128
EPS_LN = 1e-5

bfdt = ml_dtypes.bfloat16
AP = bass.AP



def _blk_fp32(w):
    O, I = w.shape
    M, K = O // 128, I // 128
    return np.ascontiguousarray(w.reshape(M, 128, K, 128).transpose(0, 3, 2, 1))


def _blk_l0q(qw):
    return np.ascontiguousarray(qw.reshape(32, 16, 32, 128).transpose(2, 1, 0, 3))


def _sblk(s):
    G, N = s.shape
    M = N // 128
    return np.ascontiguousarray(
        s.reshape(G, M, 128).transpose(1, 0, 2).reshape(M, G * 128))


def shard_x(x):
    out = []
    for cid in range(NCORES):
        b, q = cid // 4, cid % 4
        out.append(np.ascontiguousarray(x[b, q * T:(q + 1) * T, :]))
    return out


def _decays():
    HEADS, BS_, LI, NL = 32, 256, 1, 32
    base = 1.0 / 2.0 ** (8.0 / HEADS)
    exponent = np.arange(HEADS, dtype=np.float64) + 1.0
    factor = 1.0 - LI / (NL - 1 + 1e-05) + 1e-05
    slope = base ** exponent * factor
    r = np.arange(BS_, dtype=np.float64) + 1.0
    q_decay = np.exp(-slope[:, None] * r[None, :])
    k_decay = np.exp(-slope[:, None] * (BS_ - r)[None, :])
    diff = r[:, None] - r[None, :]
    diag = slope[:, None, None] * diff[None, :, :]
    dd = np.where(diag >= 0, np.exp(-diag), 0.0)
    bd = np.exp(-slope * BS_)
    return q_decay, k_decay, dd, bd


def host_prepare(inputs):
    c = {}
    c["ident"] = np.eye(128, dtype=np.float32)
    c["w0q_blk"] = _blk_l0q(inputs["L0_ffn_qweight"])
    c["qz0"] = np.ascontiguousarray(inputs["L0_ffn_qzeros"])
    c["s0_nat"] = np.ascontiguousarray(inputs["L0_ffn_scales"])
    c["s0_blk"] = _sblk(inputs["L0_ffn_scales"])
    c["bias0_col"] = np.ascontiguousarray(inputs["L0_ffn_bias"].reshape(32, 128).T)
    c["shift_l0"] = (4 * (np.arange(128) % 8)).astype(np.int32).reshape(128, 1)
    c["shift_l0z"] = np.ascontiguousarray(np.broadcast_to(
        (4 * np.arange(8, dtype=np.int32))[None, :], (32, 8)))
    c["qkv_blk"] = _blk_fp32(inputs["L0_attn_qkv_w"])
    c["gate_blk"] = _blk_fp32(inputs["L0_attn_gate_w"])
    c["out_blk"] = _blk_fp32(inputs["L0_attn_out_w"])
    c["normw_col"] = np.ascontiguousarray(
        inputs["L0_attn_norm_w"].reshape(32, 128).T.astype(np.float32))
    qd, kd, dd, bd = _decays()
    c["qd_bc"] = np.ascontiguousarray(
        np.broadcast_to(qd[:, None, :], (32, 128, 256)).astype(np.float32))
    c["kd_bc"] = np.ascontiguousarray(
        np.broadcast_to(kd[:, None, :], (32, 128, 256)).astype(np.float32))
    ddT = dd.transpose(0, 2, 1)
    c["dd_T"] = np.ascontiguousarray(
        ddT.reshape(32, 2, 128, 256).transpose(0, 2, 1, 3).astype(np.float32))
    pw = bd[:, None] ** np.arange(9)[None, :]
    c["bd_pows"] = np.ascontiguousarray(
        np.broadcast_to(pw.reshape(1, -1), (128, 32 * 9)).astype(np.float32))
    c["cw_percore"] = []
    for cid in range(NCORES):
        pos = cid % 4
        W = np.zeros((32, 4), np.float64)
        for cp in range(4):
            if cp < pos:
                W[:, cp] = bd ** (8 * (pos - 1 - cp))
        c["cw_percore"].append(np.ascontiguousarray(
            np.broadcast_to(W.reshape(1, -1), (128, 128)).astype(np.float32)))
    c["wb_blk"] = _blk_fp32(inputs["L1_ffn_W_base"])
    c["bb_col"] = np.ascontiguousarray(
        inputs["L1_ffn_b_base"].reshape(32, 128).T)
    A = inputs["L1_ffn_A"]
    c["A_blk2"] = np.ascontiguousarray(
        A.reshape(128, 32, 128).transpose(2, 1, 0))
    Wg = inputs["L1_ffn_W_gate"]
    c["Wg_blk2"] = np.ascontiguousarray(
        Wg.reshape(8, 32, 128).transpose(2, 1, 0))
    B = inputs["L1_ffn_B"]
    c["B_blk"] = np.ascontiguousarray(
        B.transpose(0, 2, 1).reshape(128, 32, 128).transpose(1, 0, 2))
    def _blk_l1q(qw, kch, mch):
        return np.ascontiguousarray(
            qw.reshape(kch, 128, mch, 16).transpose(2, 1, 0, 3))
    c["q1_blk"] = _blk_l1q(np.ascontiguousarray(inputs["L1_attn_qkv_qw"][:, :512]), 32, 32)
    c["o1_blk"] = _blk_l1q(inputs["L1_attn_o_qw"], 32, 32)
    c["gu_blk"] = _blk_l1q(inputs["L1_attn_gu_qw"], 32, 172)
    c["dn_blk"] = _blk_l1q(inputs["L1_attn_dn_qw"], 86, 32)
    c["q1_sblk"] = _sblk(np.ascontiguousarray(inputs["L1_attn_qkv_sc"][:, :4096]))
    c["o1_sblk"] = _sblk(inputs["L1_attn_o_sc"])
    c["gu_sblk"] = _sblk(inputs["L1_attn_gu_sc"])
    c["dn_sblk"] = _sblk(inputs["L1_attn_dn_sc"])
    c["q1_qz"] = np.ascontiguousarray(inputs["L1_attn_qkv_qz"][:, :512])
    c["o1_qz"] = np.ascontiguousarray(inputs["L1_attn_o_qz"])
    c["gu_qz"] = np.ascontiguousarray(inputs["L1_attn_gu_qz"])
    c["dn_qz"] = np.ascontiguousarray(inputs["L1_attn_dn_qz"])
    c["q1_sc"] = np.ascontiguousarray(inputs["L1_attn_qkv_sc"][:, :4096])
    c["o1_sc"] = np.ascontiguousarray(inputs["L1_attn_o_sc"])
    c["gu_sc"] = np.ascontiguousarray(inputs["L1_attn_gu_sc"])
    c["dn_sc"] = np.ascontiguousarray(inputs["L1_attn_dn_sc"])
    ORDER = np.array([0, 4, 1, 5, 2, 6, 3, 7], dtype=np.int32) * 4
    c["shift_l1"] = np.ascontiguousarray(np.broadcast_to(ORDER[None, :], (128, 8)))
    c["shift_l1z"] = np.ascontiguousarray(np.broadcast_to(ORDER[None, :], (86, 8)))
    c["ln1_col"] = np.ascontiguousarray(
        inputs["L1_attn_ln1_w"].reshape(32, 128).T)
    c["ln2_col"] = np.ascontiguousarray(
        inputs["L1_attn_ln2_w"].reshape(32, 128).T)
    return c



class SemRef:
    __slots__ = ("phys", "base")

    def __init__(self, phys, base=0):
        self.phys = phys
        self.base = base


class InsP:
    __slots__ = ("i",)

    def __init__(self, i):
        self.i = i

    def then_inc(self, sr, n):
        return InsP(self.i.then_inc(sr.phys, n))


class EngP:
    def __init__(self, e):
        object.__setattr__(self, "_e", e)

    def wait_ge(self, sr, n):
        self._e.wait_ge(sr.phys, sr.base + n)

    def __getattr__(self, k):
        f = getattr(self._e, k)
        if not callable(f):
            return f

        def w(*a, **kw):
            r = f(*a, **kw)
            return InsP(r) if hasattr(r, "then_inc") else r
        return w


class MockIns:
    __slots__ = ("cum",)

    def __init__(self, cum):
        self.cum = cum

    def then_inc(self, sr, n):
        self.cum[id(sr.phys)] = self.cum.get(id(sr.phys), 0) + n
        return self


class MockEng:
    def __init__(self, cum):
        self._cum = cum

    def wait_ge(self, sr, n):
        pass

    def __getattr__(self, k):
        cum = self._cum

        def w(*a, **kw):
            return MockIns(cum)
        return w


_ENGS = ("sync", "gpsimd", "scalar", "tensor", "vector")


class Ctx:
    def __init__(self):
        self.nc = bass.Bass("TRN2", target_bir_lowering=False)
        self.stack = []
        self._holds = []

    def sbuf(self, name, shape, dtype):
        g = self.nc.sbuf_tensor(name, shape, dtype)
        t = g.__enter__()
        self.stack.append((name, g))
        return t

    def psum(self, name, shape, dtype=F32):
        g = self.nc.psum_tensor(name, shape, dtype)
        t = g.__enter__()
        self.stack.append((name, g))
        return t

    def sem(self, name):
        g = self.nc.semaphore(name)
        t = g.__enter__()
        self._holds.append(g)
        return SemRef(t, 0)

    def dram(self, name, shape, dtype, kind="Internal"):
        return self.nc.dram_tensor(name, shape, dtype, kind=kind)

    def free_top(self, upto_name):
        while self.stack:
            name, g = self.stack.pop()
            g.__exit__(None, None, None)
            if name == upto_name:
                return

    def finish(self, name, secs, finals=None):
        nc = self.nc
        with nc.Block() as block:
            for eng in _ENGS:
                fn = secs.get(eng)
                if fn is None:
                    continue
                getattr(block, eng)(
                    (lambda fn: lambda e: fn(EngP(e)))(fn))
        while self.stack:
            _, g = self.stack.pop()
            g.__exit__(None, None, None)
        for g in self._holds:
            g.__exit__(None, None, None)
        return nc


class FusedCtx:

    _FMAP = {"x_fm": "xbuf", "x_fm_o": "xbuf"}
    _CROSS = {"zs0", "w0deq", "zs_q1", "zs_o1", "zs_gu", "zs_dn"}

    def __init__(self):
        self.nc = bass.Bass("TRN2", target_bir_lowering=False)
        self.stack = []
        self.pool = []
        self.phases = []
        self._cursems = []
        self._drams = {}
        self._holds = []
        self._pidx = 0

    def sbuf(self, name, shape, dtype):
        g = self.nc.sbuf_tensor(f"p{self._pidx}_{name}", shape, dtype)
        t = g.__enter__()
        self.stack.append((name, g))
        return t

    def psum(self, name, shape, dtype=F32):
        g = self.nc.psum_tensor(f"p{self._pidx}_{name}", shape, dtype)
        t = g.__enter__()
        self.stack.append((name, g))
        return t

    def sem(self, name):
        i = len(self._cursems)
        if i >= len(self.pool):
            g = self.nc.semaphore(f"s{i}")
            t = g.__enter__()
            self._holds.append(g)
            self.pool.append(t)
        sr = SemRef(self.pool[i], None)
        self._cursems.append(sr)
        return sr

    def dram(self, name, shape, dtype, kind="Internal"):
        if name in self._FMAP:
            key, k2 = self._FMAP[name], "Internal"
        elif name in self._CROSS:
            key, k2 = name, "Internal"
        elif kind == "ExternalOutput" and name == "x_tm":
            key, k2 = "x_out_tm", "ExternalOutput"
        elif kind == "ExternalInput":
            key, k2 = name, "ExternalInput"
        else:
            key, k2 = f"p{self._pidx}_{name}", "Internal"
        if key not in self._drams:
            self._drams[key] = self.nc.dram_tensor(key, shape, dtype, kind=k2)
        return self._drams[key]

    def free_top(self, upto_name):
        while self.stack:
            name, g = self.stack.pop()
            g.__exit__(None, None, None)
            if name == upto_name:
                return

    def finish(self, name, secs, finals=None):
        self.phases.append(dict(name=name, secs=secs, finals=finals or [],
                                semrefs=self._cursems))
        self._cursems = []
        while self.stack:
            _, g = self.stack.pop()
            g.__exit__(None, None, None)
        self._pidx += 1
        return None

    def build(self):
        cum = {}
        for ph in self.phases:
            for sr in ph["semrefs"]:
                sr.base = cum.get(id(sr.phys), 0)
            me = MockEng(cum)
            for eng in _ENGS:
                fn = ph["secs"].get(eng)
                if fn:
                    fn(me)
        nc = self.nc
        phases = self.phases
        with nc.Block() as block:
            def mk(eng):
                def f(e):
                    ep = EngP(e)
                    prev = None
                    for ph in phases:
                        if prev is not None:
                            for (sr, n) in prev["finals"]:
                                ep.wait_ge(sr, n)
                        fn = ph["secs"].get(eng)
                        if fn:
                            fn(ep)
                        prev = ph
                return f
            for eng in _ENGS:
                getattr(block, eng)(mk(eng))
        while self.stack:
            _, g = self.stack.pop()
            g.__exit__(None, None, None)
        for g in self._holds:
            g.__exit__(None, None, None)
        return nc



def build_init(fx=None):
    ctx = fx if fx is not None else Ctx()
    nc = ctx.nc
    x_tm = ctx.dram("x_tm", [T, D], F32, kind="ExternalInput")
    ident = ctx.dram("ident", [128, 128], F32, kind="ExternalInput")
    qz0 = ctx.dram("qz0", [32, 512], I32, kind="ExternalInput")
    s0n = ctx.dram("s0_nat", [32, 4096], F16, kind="ExternalInput")
    shlz = ctx.dram("shift_l0z", [32, 8], I32, kind="ExternalInput")
    x_fm = ctx.dram("x_fm", [KC, 128, T], F32, kind="ExternalOutput")
    zs0 = ctx.dram("zs0", [32, 4096], F32, kind="ExternalOutput")

    xin = ctx.sbuf("xin", [128, 2 * D], F32)
    stage = [ctx.sbuf(f"stg{i}", [128, 2 * 128], F32) for i in range(4)]
    sid = ctx.sbuf("sid", [128, 128], F32)
    zrow = ctx.sbuf("zrow", [32, 512], I32)
    zint = ctx.sbuf("zint", [32, 4096], I32)
    zf = ctx.sbuf("zf", [32, 4096], F32)
    snat = ctx.sbuf("snat", [32, 4096], F32)
    shz = ctx.sbuf("shz", [32, 8], I32)
    pt = [ctx.psum(f"pt{i}", [128, 512], F32) for i in range(8)]
    ld = ctx.sem("ld")
    ldz = ctx.sem("ldz")
    vz = ctx.sem("vz")
    mm = ctx.sem("mm")
    ev = ctx.sem("ev")
    stm = ctx.sem("stm")

    ngrp = NTT * 8

    def SEC_gpsimd(g):
        g.dma_start(out=zrow[:], in_=qz0[:, :]).then_inc(ldz, 16)
        g.dma_start(out=snat[:], in_=s0n[:, :]).then_inc(ldz, 16)
        g.dma_start(out=shz[:], in_=shlz[:, :]).then_inc(ldz, 16)
        for G in range(ngrp):
            tt, gg = G // 8, G % 8
            g.wait_ge(ev, G + 1)
            for j in range(4):
                k = gg * 4 + j
                g.dma_start(out=x_fm[k, :, tt * 128:(tt + 1) * 128],
                            in_=stage[j][:, (G % 2) * 128:(G % 2 + 1) * 128]
                            ).then_inc(stm, 16)
        g.wait_ge(stm, 16 * 4 * ngrp)
        g.wait_ge(ldz, 64)

    def SEC_vector(v):
        v.wait_ge(ldz, 48)
        v.tensor_tensor(
            AP(zint, 0, [[4096, 32], [8, 512], [1, 8]]),
            AP(zrow, 0, [[512, 32], [1, 512], [0, 8]]),
            AP(shz, 0, [[8, 32], [0, 512], [1, 8]]),
            ALU.logical_shift_right).then_inc(vz, 1)
        v.wait_ge(vz, 1)
        v.tensor_scalar(zint[:, :], zint[:, :], 15, None,
                        ALU.bitwise_and).then_inc(vz, 1)
        v.wait_ge(vz, 2)
        v.tensor_scalar(zint[:, :], zint[:, :], 1, None, ALU.add).then_inc(vz, 1)
        v.wait_ge(vz, 3)
        v.tensor_tensor(zf[:, :], zint[:, :], snat[:, :], ALU.mult).then_inc(vz, 1)
        for G in range(ngrp):
            v.wait_ge(mm, G + 1)
            if G >= 2:
                v.wait_ge(stm, 16 * 4 * (G - 1))
            bs = (G % 2) * 4
            for j in range(4):
                ins = v.tensor_copy(stage[j][:, (G % 2) * 128:(G % 2 + 1) * 128],
                                    pt[bs + j][:, :128])
            ins.then_inc(ev, 1)

    def SEC_sync(s):
        s.dma_start(out=sid[:], in_=ident[:, :]).then_inc(ld, 16)
        s.wait_ge(vz, 4)
        s.dma_start(out=zs0[:, :], in_=zf[:]).then_inc(ldz, 16)
        for tt in range(NTT):
            if tt >= 2:
                s.wait_ge(mm, (tt - 1) * 8)
            s.dma_start(out=xin[:, (tt % 2) * D:(tt % 2 + 1) * D],
                        in_=x_tm[tt * 128:(tt + 1) * 128, :]).then_inc(ld, 16)

    def SEC_tensor(te):
        for tt in range(NTT):
            te.wait_ge(ld, 16 * (tt + 2))
            for g in range(8):
                G = tt * 8 + g
                if G >= 2:
                    te.wait_ge(ev, G - 1)
                bs = (G % 2) * 4
                for j in range(4):
                    k = g * 4 + j
                    ins = te.transpose(
                        pt[bs + j][:, :128],
                        xin[:, (tt % 2) * D + k * 128:(tt % 2) * D + (k + 1) * 128],
                        sid[:])
                ins.then_inc(mm, 1)


    return ctx.finish("init", dict(gpsimd=SEC_gpsimd, vector=SEC_vector, sync=SEC_sync, tensor=SEC_tensor),
                      finals=[(stm, 16 * 4 * ngrp), (ldz, 64)])



def build_fin(fx=None):
    ctx = fx if fx is not None else Ctx()
    nc = ctx.nc
    x_fm = ctx.dram("x_fm", [KC, 128, T], F32, kind="ExternalInput")
    ident = ctx.dram("ident", [128, 128], F32, kind="ExternalInput")
    x_tm = ctx.dram("x_tm", [T, D], F32, kind="ExternalOutput")

    xin = ctx.sbuf("xin", [128, 2 * T], F32)
    stage = [ctx.sbuf(f"stg{i}", [128, 2 * 128], F32) for i in range(4)]
    sid = ctx.sbuf("sid", [128, 128], F32)
    pt = [ctx.psum(f"pt{i}", [128, 512], F32) for i in range(8)]
    ld = ctx.sem("ld")
    mm = ctx.sem("mm")
    ev = ctx.sem("ev")
    stm = ctx.sem("stm")

    ngrp = KC * 4

    def SEC_sync(s):
        s.dma_start(out=sid[:], in_=ident[:, :]).then_inc(ld, 16)
        for k in range(KC):
            if k >= 2:
                s.wait_ge(mm, (k - 1) * 4)
            s.dma_start(out=xin[:, (k % 2) * T:(k % 2 + 1) * T],
                        in_=x_fm[k, :, :]).then_inc(ld, 16)

    def SEC_tensor(te):
        for k in range(KC):
            te.wait_ge(ld, 16 * (k + 2))
            for g in range(4):
                G = k * 4 + g
                if G >= 2:
                    te.wait_ge(ev, G - 1)
                bs = (G % 2) * 4
                for j in range(4):
                    tt = g * 4 + j
                    ins = te.transpose(
                        pt[bs + j][:, :128],
                        xin[:, (k % 2) * T + tt * 128:(k % 2) * T + (tt + 1) * 128],
                        sid[:])
                ins.then_inc(mm, 1)

    def SEC_vector(v):
        for G in range(ngrp):
            v.wait_ge(mm, G + 1)
            if G >= 2:
                v.wait_ge(stm, 16 * 4 * (G - 1))
            bs = (G % 2) * 4
            for j in range(4):
                ins = v.tensor_copy(stage[j][:, (G % 2) * 128:(G % 2 + 1) * 128],
                                    pt[bs + j][:, :128])
            ins.then_inc(ev, 1)

    def SEC_gpsimd(g):
        for G in range(ngrp):
            k, gg = G // 4, G % 4
            g.wait_ge(ev, G + 1)
            for j in range(4):
                tt = gg * 4 + j
                g.dma_start(
                    out=x_tm[tt * 128:(tt + 1) * 128, k * 128:(k + 1) * 128],
                    in_=stage[j][:, (G % 2) * 128:(G % 2 + 1) * 128]
                    ).then_inc(stm, 16)
        g.wait_ge(stm, 16 * 4 * ngrp)


    return ctx.finish("fin", dict(sync=SEC_sync, tensor=SEC_tensor, vector=SEC_vector, gpsimd=SEC_gpsimd),
                      finals=[(stm, 16 * 4 * ngrp)])



def build_ffn0(dequant=True, fx=None):
    ctx = fx if fx is not None else Ctx()
    nc = ctx.nc
    x_in = ctx.dram("x_fm", [KC, 128, T], F32, kind="ExternalInput")
    x_out = ctx.dram("x_fm_o", [KC, 128, T], F32, kind="ExternalOutput")
    biasc = ctx.dram("bias0_col", [128, 32], F16, kind="ExternalInput")
    if dequant:
        w0q = ctx.dram("w0q_blk", [32, 16, 32, 128], I32, kind="ExternalInput")
        s0b = ctx.dram("s0_blk", [32, 4096], F16, kind="ExternalInput")
        zs0 = ctx.dram("zs0", [32, 4096], F32, kind="ExternalInput")
        shl0 = ctx.dram("shift_l0", [128, 1], I32, kind="ExternalInput")
        w0d = ctx.dram("w0deq", [32, 128, 4096], BF16, kind="ExternalOutput")
    else:
        w0d = ctx.dram("w0deq", [32, 128, 4096], BF16, kind="ExternalInput")

    H = ctx.sbuf("H", [128, KC * T], BF16)
    xf = ctx.sbuf("xf", [128, 2 * T], F32)
    onesb = ctx.sbuf("onesb", [128, 1], BF16)
    onesbr = ctx.sbuf("onesbr", [1, 128], BF16)
    onesfr = ctx.sbuf("onesfr", [1, 128], F32)
    bcol = ctx.sbuf("bcol", [128, 32], F32)
    hbcol = ctx.sbuf("hbcol", [128, 32], F32)
    epsc = ctx.sbuf("epsc", [128, 1], F32)
    shc = ctx.sbuf("shc", [128, 1], I32) if dequant else None
    sq = ctx.sbuf("sq", [128, T], BF16)
    r_bc = ctx.sbuf("r_bc", [128, T], F32)
    mr_bc = ctx.sbuf("mr_bc", [128, T], F32)
    sra = ctx.sbuf("sra", [1, T], F32)
    srb = ctx.sbuf("srb", [1, T], F32)
    rows_r = ctx.sbuf("rows_r", [1, T], F32)
    rows_mr = ctx.sbuf("rows_mr", [1, T], F32)
    t1 = ctx.sbuf("t1", [128, T], F32)
    ctx.free_top("sq")
    ws = ctx.sbuf("ws", [128, 2 * 4096], BF16)
    xn = ctx.sbuf("xn", [128, T], F32)
    if dequant:
        sw = ctx.sbuf("sw", [128, 2048], I32)
        qint = ctx.sbuf("qint", [128, 2048], I32)
        srow = ctx.sbuf("srow", [1, 4096], BF16)
        zsrow = ctx.sbuf("zsrow", [1, 4096], BF16)

    ps = [ctx.psum(f"ps{i}", [128, 512], F32) for i in range(8)]

    gini = ctx.sem("gini")
    lda = ctx.sem("lda")
    ld = ctx.sem("ld")
    acts = ctx.sem("acts")
    pes = ctx.sem("pes")
    vstat = ctx.sem("vstat")
    aa = ctx.sem("aa")
    bb = ctx.sem("bb")
    vb = ctx.sem("vb")
    hb = ctx.sem("hb")
    ldw = ctx.sem("ldw")
    mmb = ctx.sem("mmb")
    vd = ctx.sem("vd")
    mmg = ctx.sem("mmg")
    evu = ctx.sem("evu")
    stm = ctx.sem("stm")
    stw = ctx.sem("stw")
    vv = ctx.sem("vv")

    NW = 18 if dequant else 1

    def SEC_sync(s):
        for k in range(KC):
            if k >= 2:
                s.wait_ge(acts, k - 1)
            s.dma_start(out=xf[:, (k % 2) * T:(k % 2 + 1) * T],
                        in_=x_in[k, :, :]).then_inc(ld, 16)
        for k in range(KC):
            if k == 0:
                s.wait_ge(acts, KC - 1)
            elif k == 1:
                s.wait_ge(acts, KC)
            else:
                s.wait_ge(hb, k - 1)
            s.dma_start(out=xf[:, (k % 2) * T:(k % 2 + 1) * T],
                        in_=x_in[k, :, :]).then_inc(ld, 16)
        s.wait_ge(hb, KC - 1)
        s.dma_start(out=xf[:, 0:T], in_=x_in[0, :, :]).then_inc(ld, 16)
        s.wait_ge(hb, KC)
        s.dma_start(out=xf[:, T:2 * T], in_=x_in[1, :, :]).then_inc(ld, 16)
        for m in range(KC):
            s.wait_ge(evu, m + 1)
            s.dma_start(out=x_out[m, :, :], in_=xn[:]).then_inc(stm, 16)
            if dequant:
                s.wait_ge(vd, 8 * (m + 1))
                s.dma_start(out=w0d[m, :, :],
                            in_=ws[:, (m % 2) * 4096:(m % 2 + 1) * 4096]
                            ).then_inc(stw, 16)
            if m + 2 < KC:
                s.dma_start(out=xf[:, ((m + 2) % 2) * T:((m + 2) % 2 + 1) * T],
                            in_=x_in[m + 2, :, :]).then_inc(ld, 16)
        s.wait_ge(stm, 16 * KC)
        if dequant:
            s.wait_ge(stw, 16 * KC)

    def SEC_gpsimd(g):
        g.memset(onesb[:], 1.0)
        g.memset(onesbr[:], 1.0)
        g.memset(onesfr[:], 1.0)
        g.memset(epsc[:], EPS_LN)
        g.engine_nop().then_inc(gini, 1)
        g.dma_start(out=bcol[:], in_=biasc[:, :]).then_inc(lda, 16)
        if dequant:
            g.dma_start(out=shc[:], in_=shl0[:, :]).then_inc(lda, 16)
        for k in range(KC):
            g.wait_ge(vb, k + 1)
            g.tensor_tensor(H[:, k * T:(k + 1) * T], t1[:], mr_bc[:],
                            ALU.subtract).then_inc(hb, 1)
        for m in range(KC):
            if dequant:
                if m == 0:
                    g.wait_ge(hb, KC)
                else:
                    g.wait_ge(vd, 8 * m)
                for j in range(8):
                    dst = AP(sw, j * 2048, [[8 * 2048, 16], [1, 2048]])
                    g.dma_start(out=dst, in_=w0q[m, :, 0:16, :]).then_inc(ldw, 16)
                if m >= 1:
                    g.wait_ge(mmb, 8 * m)
                g.dma_start(out=srow[:], in_=s0b[m:m + 1, :]).then_inc(ldw, 16)
                zsl = AP(zs0, 128 * m, [[0, 1], [4096, 32], [1, 128]])
                g.dma_start(out=zsrow[:].rearrange("a (k n) -> a k n", k=32),
                            in_=zsl).then_inc(ldw, 16)
                g.wait_ge(vd, 8 * m + 4)
                for j in range(8):
                    dst = AP(sw, j * 2048, [[8 * 2048, 16], [1, 2048]])
                    g.dma_start(out=dst, in_=w0q[m, :, 16:32, :]).then_inc(ldw, 16)
            else:
                if m == 0:
                    g.wait_ge(hb, KC)
                elif m >= 2:
                    g.wait_ge(mmg, m - 1)
                g.dma_start(out=ws[:, (m % 2) * 4096:(m % 2 + 1) * 4096],
                            in_=w0d[m, :, :]).then_inc(ldw, 16)

    def SEC_scalar(a):
        for k in range(KC):
            a.wait_ge(ld, 16 * (k + 1))
            if k >= 1:
                a.wait_ge(pes, k)
            a.activation(H[:, k * T:(k + 1) * T],
                         xf[:, (k % 2) * T:(k % 2 + 1) * T], AF.Copy)
            a.activation(sq[:, :], xf[:, (k % 2) * T:(k % 2 + 1) * T],
                         AF.Square).then_inc(acts, 1)
        a.wait_ge(vstat, 1)
        a.wait_ge(gini, 1)
        a.activation(rows_r[0:1, :], srb[0:1, :], AF.Sqrt,
                     bias=epsc[0:1, 0:1]).then_inc(aa, 4)
        a.wait_ge(lda, 16)
        a.mul(hbcol[:], bcol[:], 0.5).then_inc(aa, 1)

    def SEC_tensor(te):
        te.wait_ge(gini, 1)
        for k in range(KC):
            te.wait_ge(acts, k + 1)
            st, sp = (k == 0), (k == KC - 1)
            for tg in range(4):
                te.matmul(ps[tg][0:1, :], onesb[:],
                          H[:, k * T + tg * 512:k * T + (tg + 1) * 512],
                          start=st, stop=sp)
            for tg in range(4):
                ins = te.matmul(
                    ps[4 + tg][0:1, :], onesb[:],
                    sq[:, tg * 512:(tg + 1) * 512],
                    start=st, stop=sp)
            ins.then_inc(pes, 1)
        te.wait_ge(vstat, 2)
        for tg in range(4):
            te.matmul(ps[tg][:, :], onesfr[:], rows_r[0:1, tg * 512:(tg + 1) * 512],
                      start=True, stop=True)
            te.matmul(ps[4 + tg][:, :], onesfr[:],
                      rows_mr[0:1, tg * 512:(tg + 1) * 512],
                      start=True, stop=True).then_inc(bb, 1)

        def bcasts(m):
            te.wait_ge(ldw, 16 * (NW * m + 10))
            for i in range(8):
                S = m * 8 + i
                if S == 0:
                    te.wait_ge(vstat, 3)
                if S >= 2:
                    te.wait_ge(vd, S - 1)
                te.matmul(ps[4 + S % 2][:, :], onesbr[:],
                          srow[0:1, i * 512:(i + 1) * 512], start=True, stop=True)
                te.matmul(ps[6 + S % 2][:, :], onesbr[:],
                          zsrow[0:1, i * 512:(i + 1) * 512], start=True,
                          stop=True).then_inc(mmb, 1)

        if dequant:
            bcasts(0)
        for m in range(KC):
            if dequant and m + 1 < KC:
                bcasts(m + 1)
            if dequant:
                te.wait_ge(vd, 8 * (m + 1))
            else:
                te.wait_ge(ldw, 16 * (m + 1))
            if m == 0:
                te.wait_ge(vstat, 3)
            else:
                te.wait_ge(evu, m)
            for k in range(KC):
                st, sp = (k == 0), (k == KC - 1)
                for tg in range(4):
                    ins = te.matmul(
                        ps[tg][:, :],
                        ws[:, (m % 2) * 4096 + k * 128:(m % 2) * 4096 + (k + 1) * 128],
                        H[:, k * T + tg * 512:k * T + (tg + 1) * 512],
                        start=st, stop=sp)
            ins.then_inc(mmg, 1)

    def SEC_vector(v):
        nvv = [0]

        def chain(ins):
            nvv[0] += 1
            ins.then_inc(vv, 1)
            v.wait_ge(vv, nvv[0])

        v.wait_ge(pes, KC)
        for tg in range(4):
            ins = v.tensor_copy(sra[0:1, tg * 512:(tg + 1) * 512], ps[tg][0:1, :])
        for tg in range(4):
            ins = v.tensor_copy(srb[0:1, tg * 512:(tg + 1) * 512],
                                ps[4 + tg][0:1, :])
        chain(ins)
        ins = v.tensor_scalar(sra[0:1, :], sra[0:1, :], 1.0 / D, None, ALU.mult)
        ins = v.tensor_scalar(srb[0:1, :], srb[0:1, :], 1.0 / D, None, ALU.mult)
        chain(ins)
        ins = v.tensor_tensor(rows_mr[0:1, :], sra[0:1, :], sra[0:1, :], ALU.mult)
        chain(ins)
        v.tensor_tensor(srb[0:1, :], srb[0:1, :], rows_mr[0:1, :],
                        ALU.subtract).then_inc(vstat, 1)
        v.wait_ge(aa, 4)
        ins = v.reciprocal(srb[0:1, :], rows_r[0:1, :])
        chain(ins)
        ins = v.tensor_tensor(rows_mr[0:1, :], sra[0:1, :], srb[0:1, :], ALU.mult)
        ins = v.tensor_copy(rows_r[0:1, :], srb[0:1, :])
        ins.then_inc(vstat, 1)
        for tg in range(4):
            v.wait_ge(bb, tg + 1)
            v.tensor_copy(r_bc[:, tg * 512:(tg + 1) * 512], ps[tg][:, :])
            ins = v.tensor_copy(mr_bc[:, tg * 512:(tg + 1) * 512], ps[4 + tg][:, :])
        ins.then_inc(vstat, 1)
        v.wait_ge(vstat, 3)
        for k in range(KC):
            v.wait_ge(ld, 16 * (KC + k + 1))
            if k >= 1:
                v.wait_ge(hb, k)
            v.tensor_tensor(t1[:], xf[:, (k % 2) * T:(k % 2 + 1) * T],
                            r_bc[:], ALU.mult).then_inc(vb, 1)

        def dequant_strip(m):
            if m >= 2:
                v.wait_ge(mmg, m - 1)
                v.wait_ge(stw, 16 * (m - 1))
            for h in range(2):
                v.wait_ge(ldw, 16 * (NW * m + (8 if h == 0 else 18)))
                if h == 1:
                    v.wait_ge(vd, 8 * m + 4)
                ins = v.tensor_tensor(
                    qint[:, :],
                    AP(sw, 0, [[2048, 128], [1, 2048]]),
                    AP(shc, 0, [[1, 128], [0, 2048]]),
                    ALU.logical_shift_right)
                chain(ins)
                ins = v.tensor_scalar(qint[:, :], qint[:, :], 15, None,
                                      ALU.bitwise_and)
                chain(ins)
                for i in range(4 * h, 4 * h + 4):
                    S = m * 8 + i
                    v.wait_ge(mmb, S + 1)
                    off = (m % 2) * 4096 + i * 512
                    ins = v.tensor_tensor(
                        ws[:, off:off + 512],
                        qint[:, (i - 4 * h) * 512:(i - 4 * h + 1) * 512],
                        ps[4 + S % 2][:, :], ALU.mult)
                    chain(ins)
                    v.tensor_tensor(ws[:, off:off + 512], ws[:, off:off + 512],
                                    ps[6 + S % 2][:, :],
                                    ALU.subtract).then_inc(vd, 1)

        def update(m):
            v.wait_ge(mmg, m + 1)
            v.wait_ge(ld, 16 * (2 * KC + m + 1))
            if m >= 1:
                v.wait_ge(stm, 16 * m)
            v.wait_ge(aa, 5)
            for tg in range(4):
                ins = v.scalar_tensor_tensor(
                    xn[:, tg * 512:(tg + 1) * 512], ps[tg][:, :], 0.5,
                    xf[:, (m % 2) * T + tg * 512:(m % 2) * T + (tg + 1) * 512],
                    ALU.mult, ALU.add)
            chain(ins)
            v.tensor_scalar(xn[:, :], xn[:, :], hbcol[:, m:m + 1], None,
                            ALU.add).then_inc(evu, 1)

        if dequant:
            dequant_strip(0)
        for m in range(KC):
            if dequant and m + 1 < KC:
                dequant_strip(m + 1)
            update(m)


    return ctx.finish("ffn0", dict(sync=SEC_sync, gpsimd=SEC_gpsimd, scalar=SEC_scalar, tensor=SEC_tensor, vector=SEC_vector),
                      finals=[(stm, 16 * KC)] + ([(stw, 16 * KC)] if dequant else []))




def build_molora(fx=None):
    ctx = fx if fx is not None else Ctx()
    nc = ctx.nc
    x_in = ctx.dram("x_fm", [KC, 128, T], F32, kind="ExternalInput")
    x_out = ctx.dram("x_fm_o", [KC, 128, T], F32, kind="ExternalOutput")
    wb = ctx.dram("wb_blk", [32, 128, 32, 128], F32, kind="ExternalInput")
    bbc = ctx.dram("bb_col", [128, 32], F32, kind="ExternalInput")
    Abk = ctx.dram("A_blk2", [128, 32, 128], F32, kind="ExternalInput")
    Wgb = ctx.dram("Wg_blk2", [128, 32, 8], F32, kind="ExternalInput")
    Bbk = ctx.dram("B_blk", [32, 128, 128], F32, kind="ExternalInput")
    ident = ctx.dram("ident", [128, 128], F32, kind="ExternalInput")

    BIG = 30000.0

    H = ctx.sbuf("H", [128, KC * T], BF16)
    xf = ctx.sbuf("xf", [128, 2 * T], F32)
    onesb = ctx.sbuf("onesb", [128, 1], BF16)
    onesfr = ctx.sbuf("onesfr", [1, 128], F32)
    bcol = ctx.sbuf("bcol", [128, 32], F32)
    hbcol = ctx.sbuf("hbcol", [128, 32], F32)
    epsc = ctx.sbuf("epsc", [128, 1], F32)
    sq = ctx.sbuf("sq", [128, T], BF16)
    r_bc = ctx.sbuf("r_bc", [128, T], F32)
    mr_bc = ctx.sbuf("mr_bc", [128, T], F32)
    sra = ctx.sbuf("sra", [1, T], F32)
    srb = ctx.sbuf("srb", [1, T], F32)
    rows_r = ctx.sbuf("rows_r", [1, T], F32)
    rows_mr = ctx.sbuf("rows_mr", [1, T], F32)
    t1 = ctx.sbuf("t1", [128, T], F32)
    ctx.free_top("sq")
    ws = ctx.sbuf("ws", [128, 2 * 4096], BF16)
    xn = ctx.sbuf("xn", [128, T], F32)
    bs = ctx.sbuf("bs", [128, 2 * 128], BF16)
    sid = ctx.sbuf("sid", [128, 128], F32)
    Alh = ctx.sbuf("Alh", [128, 32 * 128], BF16)
    Wgl = ctx.sbuf("Wgl", [128, 32 * 8], BF16)
    hA = ctx.sbuf("hA", [128, T], BF16)
    hw = ctx.sbuf("hw", [128, T], BF16)
    rexp = ctx.sbuf("rexp", [128, T], BF16)
    lg_fm = ctx.sbuf("lg_fm", [8, T], F32)
    lg_tm = ctx.sbuf("lg_tm", [128, NTT * 8], F32)
    rt_tm = ctx.sbuf("rt_tm", [128, NTT * 8], F32)
    rt_fm = ctx.sbuf("rt_fm", [8, T], BF16)
    rcols = ctx.sbuf("rcols", [128, 8], F32)

    ps = [ctx.psum(f"ps{i}", [128, 512], F32) for i in range(8)]

    gini = ctx.sem("gini")
    lda = ctx.sem("lda")
    ld = ctx.sem("ld")
    acts = ctx.sem("acts")
    pes = ctx.sem("pes")
    vstat = ctx.sem("vstat")
    aa = ctx.sem("aa")
    bb = ctx.sem("bb")
    vb = ctx.sem("vb")
    hb = ctx.sem("hb")
    ldw = ctx.sem("ldw")
    ldp = ctx.sem("ldp")
    mmp = ctx.sem("mmp")
    vp = ctx.sem("vp")
    ab = ctx.sem("ab")
    vlg = ctx.sem("vlg")
    vsel = ctx.sem("vsel")
    vrt = ctx.sem("vrt")
    vre = ctx.sem("vre")
    gp = ctx.sem("gp")
    vhw = ctx.sem("vhw")
    mmg = ctx.sem("mmg")
    evu = ctx.sem("evu")
    stm = ctx.sem("stm")
    vv = ctx.sem("vv")

    def SEC_sync(s):
        for k in range(KC):
            if k >= 2:
                s.wait_ge(acts, k - 1)
            s.dma_start(out=xf[:, (k % 2) * T:(k % 2 + 1) * T],
                        in_=x_in[k, :, :]).then_inc(ld, 16)
        for k in range(KC):
            if k == 0:
                s.wait_ge(acts, KC - 1)
            elif k == 1:
                s.wait_ge(acts, KC)
            else:
                s.wait_ge(hb, k - 1)
            s.dma_start(out=xf[:, (k % 2) * T:(k % 2 + 1) * T],
                        in_=x_in[k, :, :]).then_inc(ld, 16)
        s.wait_ge(hb, KC - 1)
        s.dma_start(out=xf[:, 0:T], in_=x_in[0, :, :]).then_inc(ld, 16)
        s.wait_ge(hb, KC)
        s.dma_start(out=xf[:, T:2 * T], in_=x_in[1, :, :]).then_inc(ld, 16)
        for m in range(KC):
            s.wait_ge(evu, m + 1)
            s.dma_start(out=x_out[m, :, :], in_=xn[:]).then_inc(stm, 16)
            if m + 2 < KC:
                s.dma_start(out=xf[:, ((m + 2) % 2) * T:((m + 2) % 2 + 1) * T],
                            in_=x_in[m + 2, :, :]).then_inc(ld, 16)
        s.wait_ge(stm, 16 * KC)

    def SEC_gpsimd(g):
        g.memset(onesb[:], 1.0)
        g.memset(onesfr[:], 1.0)
        g.memset(epsc[:], EPS_LN)
        g.engine_nop().then_inc(gini, 1)
        g.dma_start(out=bcol[:], in_=bbc[:, :]).then_inc(lda, 16)
        for k in range(KC):
            g.wait_ge(vb, k + 1)
            g.tensor_tensor(H[:, k * T:(k + 1) * T], t1[:], mr_bc[:],
                            ALU.subtract).then_inc(hb, 1)
        g.dma_start(out=sid[:], in_=ident[:, :]).then_inc(ldp, 16)
        g.dma_start(out=Alh[:].rearrange("p (k e) -> p k e", k=32),
                    in_=Abk[:, :, :]).then_inc(ldp, 16)
        g.dma_start(out=Wgl[:].rearrange("p (k e) -> p k e", k=32),
                    in_=Wgb[:, :, :]).then_inc(ldp, 16)
        g.wait_ge(vre, NTT)
        for r in range(16):
            dst = AP(rexp, r * T, [[16 * T, 8], [1, T]])
            g.dma_start(out=dst, in_=rt_fm[:, :]).then_inc(gp, 16)
        for m in range(KC):
            if m == 0:
                g.wait_ge(hb, KC)
            elif m >= 2:
                g.wait_ge(mmg, m - 1)
            g.dma_start(out=ws[:, (m % 2) * 4096:(m % 2 + 1) * 4096],
                        in_=wb[m, :, :, :]).then_inc(ldw, 16)
            g.dma_start(out=bs[:, (m % 2) * 128:(m % 2 + 1) * 128],
                        in_=Bbk[m, :, :]).then_inc(ldw, 16)

    def SEC_scalar(a):
        for k in range(KC):
            a.wait_ge(ld, 16 * (k + 1))
            if k >= 1:
                a.wait_ge(pes, k)
            a.activation(H[:, k * T:(k + 1) * T],
                         xf[:, (k % 2) * T:(k % 2 + 1) * T], AF.Copy)
            a.activation(sq[:, :], xf[:, (k % 2) * T:(k % 2 + 1) * T],
                         AF.Square).then_inc(acts, 1)
        a.wait_ge(vstat, 1)
        a.wait_ge(gini, 1)
        a.activation(rows_r[0:1, :], srb[0:1, :], AF.Sqrt,
                     bias=epsc[0:1, 0:1]).then_inc(aa, 4)
        a.wait_ge(lda, 16)
        a.mul(hbcol[:], bcol[:], 0.5).then_inc(aa, 1)
        for tt in range(NTT):
            a.wait_ge(vsel, tt + 1)
            a.activation(rt_tm[:, tt * 8:(tt + 1) * 8],
                         lg_tm[:, tt * 8:(tt + 1) * 8], AF.Exp,
                         bias=rcols[:, 7:8]).then_inc(ab, 1)

    def SEC_tensor(te):
        te.wait_ge(gini, 1)
        for k in range(KC):
            te.wait_ge(acts, k + 1)
            st, sp = (k == 0), (k == KC - 1)
            for tg in range(4):
                te.matmul(ps[tg][0:1, :], onesb[:],
                          H[:, k * T + tg * 512:k * T + (tg + 1) * 512],
                          start=st, stop=sp)
            for tg in range(4):
                ins = te.matmul(ps[4 + tg][0:1, :], onesb[:],
                                sq[:, tg * 512:(tg + 1) * 512],
                                start=st, stop=sp)
            ins.then_inc(pes, 1)
        te.wait_ge(vstat, 2)
        for tg in range(4):
            te.matmul(ps[tg][:, :], onesfr[:], rows_r[0:1, tg * 512:(tg + 1) * 512],
                      start=True, stop=True)
            te.matmul(ps[4 + tg][:, :], onesfr[:],
                      rows_mr[0:1, tg * 512:(tg + 1) * 512],
                      start=True, stop=True).then_inc(bb, 1)
        te.wait_ge(vstat, 3)
        te.wait_ge(ldp, 48)
        for k in range(KC):
            te.wait_ge(hb, k + 1)
            st, sp = (k == 0), (k == KC - 1)
            for tg in range(4):
                te.matmul(ps[tg][:, :], Alh[:, k * 128:(k + 1) * 128],
                          H[:, k * T + tg * 512:k * T + (tg + 1) * 512],
                          start=st, stop=sp)
            for tg in range(4):
                ins = te.matmul(ps[4 + tg][0:8, :], Wgl[:, k * 8:(k + 1) * 8],
                                H[:, k * T + tg * 512:k * T + (tg + 1) * 512],
                                start=st, stop=sp)
        ins.then_inc(mmp, 1)
        for tt in range(NTT):
            te.wait_ge(vp, 1)
            if tt >= 2:
                te.wait_ge(vlg, tt - 1)
            te.transpose(ps[4 + tt % 2][:, 0:8],
                         lg_fm[:, tt * 128:(tt + 1) * 128],
                         sid[0:8, 0:8]).then_inc(mmp, 1)
        for tt in range(NTT):
            te.wait_ge(vrt, tt + 1)
            if tt >= 2:
                te.wait_ge(vre, tt - 1)
            te.transpose(ps[6 + tt % 2][0:8, 0:128],
                         rt_tm[:, tt * 8:(tt + 1) * 8],
                         sid[:]).then_inc(mmp, 1)
        for m in range(KC):
            te.wait_ge(ldw, 32 * (m + 1))
            if m == 0:
                te.wait_ge(vp, 1)
                te.wait_ge(vhw, 1)
            else:
                te.wait_ge(evu, m)
            for k in range(KC):
                st = (k == 0)
                for tg in range(4):
                    te.matmul(
                        ps[tg][:, :],
                        ws[:, (m % 2) * 4096 + k * 128:(m % 2) * 4096 + (k + 1) * 128],
                        H[:, k * T + tg * 512:k * T + (tg + 1) * 512],
                        start=st, stop=False)
            for tg in range(4):
                ins = te.matmul(ps[tg][:, :],
                                bs[:, (m % 2) * 128:(m % 2 + 1) * 128],
                                hw[:, tg * 512:(tg + 1) * 512],
                                start=False, stop=True)
            ins.then_inc(mmg, 1)

    def SEC_vector(v):
        nvv = [0]

        def chain(ins):
            nvv[0] += 1
            ins.then_inc(vv, 1)
            v.wait_ge(vv, nvv[0])

        v.wait_ge(pes, KC)
        for tg in range(4):
            ins = v.tensor_copy(sra[0:1, tg * 512:(tg + 1) * 512], ps[tg][0:1, :])
        for tg in range(4):
            ins = v.tensor_copy(srb[0:1, tg * 512:(tg + 1) * 512],
                                ps[4 + tg][0:1, :])
        chain(ins)
        ins = v.tensor_scalar(sra[0:1, :], sra[0:1, :], 1.0 / D, None, ALU.mult)
        ins = v.tensor_scalar(srb[0:1, :], srb[0:1, :], 1.0 / D, None, ALU.mult)
        chain(ins)
        ins = v.tensor_tensor(rows_mr[0:1, :], sra[0:1, :], sra[0:1, :], ALU.mult)
        chain(ins)
        v.tensor_tensor(srb[0:1, :], srb[0:1, :], rows_mr[0:1, :],
                        ALU.subtract).then_inc(vstat, 1)
        v.wait_ge(aa, 4)
        ins = v.reciprocal(srb[0:1, :], rows_r[0:1, :])
        chain(ins)
        ins = v.tensor_tensor(rows_mr[0:1, :], sra[0:1, :], srb[0:1, :], ALU.mult)
        ins = v.tensor_copy(rows_r[0:1, :], srb[0:1, :])
        ins.then_inc(vstat, 1)
        for tg in range(4):
            v.wait_ge(bb, tg + 1)
            v.tensor_copy(r_bc[:, tg * 512:(tg + 1) * 512], ps[tg][:, :])
            ins = v.tensor_copy(mr_bc[:, tg * 512:(tg + 1) * 512], ps[4 + tg][:, :])
        ins.then_inc(vstat, 1)
        v.wait_ge(vstat, 3)
        for k in range(KC):
            v.wait_ge(ld, 16 * (KC + k + 1))
            if k >= 1:
                v.wait_ge(hb, k)
            v.tensor_tensor(t1[:], xf[:, (k % 2) * T:(k % 2 + 1) * T],
                            r_bc[:], ALU.mult).then_inc(vb, 1)

        v.wait_ge(mmp, 1)
        for tg in range(4):
            ins = v.tensor_copy(hA[:, tg * 512:(tg + 1) * 512], ps[tg][:, :])
        for tg in range(4):
            ins = v.tensor_copy(lg_fm[0:8, tg * 512:(tg + 1) * 512],
                                ps[4 + tg][0:8, :])
        ins.then_inc(vp, 1)
        for tt in range(NTT):
            v.wait_ge(mmp, 2 + tt)
            v.tensor_copy(lg_tm[:, tt * 8:(tt + 1) * 8],
                          ps[4 + tt % 2][:, 0:8]).then_inc(vlg, 1)
        for tt in range(NTT):
            lg = lg_tm[:, tt * 8:(tt + 1) * 8]
            rt = rt_tm[:, tt * 8:(tt + 1) * 8]
            if tt == 0:
                v.wait_ge(vlg, 1)
            ins = v.reduce_max(rcols[:, 0:1], lg, axis=AX.X)
            chain(ins)
            ins = v.tensor_scalar(rcols[:, 7:8], rcols[:, 0:1], -1.0, -BIG,
                                  ALU.mult, ALU.add)
            ins = v.tensor_scalar(rt, lg, rcols[:, 0:1], None, ALU.is_ge)
            chain(ins)
            ins = v.scalar_tensor_tensor(rt, rt, -BIG, lg, ALU.mult, ALU.add)
            chain(ins)
            ins = v.reduce_max(rcols[:, 2:3], rt, axis=AX.X)
            chain(ins)
            ins = v.tensor_scalar(rt, lg, rcols[:, 2:3], None, ALU.is_ge)
            chain(ins)
            ins = v.scalar_tensor_tensor(rt, rt, BIG, lg, ALU.mult, ALU.add)
            chain(ins)
            v.tensor_copy(lg_tm[:, tt * 8:(tt + 1) * 8], rt).then_inc(vsel, 1)
            v.wait_ge(ab, tt + 1)
            ins = v.reduce_sum(rcols[:, 3:4], rt, axis=AX.X)
            chain(ins)
            ins = v.reciprocal(rcols[:, 4:5], rcols[:, 3:4])
            chain(ins)
            v.tensor_scalar(rt, rt, rcols[:, 4:5], None,
                            ALU.mult).then_inc(vrt, 1)
            if tt >= 2:
                v.wait_ge(mmp, 18 + tt - 2)
                v.tensor_copy(rt_fm[0:8, (tt - 2) * 128:(tt - 1) * 128],
                              ps[6 + tt % 2][0:8, 0:128]).then_inc(vre, 1)
        for tt in (NTT - 2, NTT - 1):
            v.wait_ge(mmp, 18 + tt)
            v.tensor_copy(rt_fm[0:8, tt * 128:(tt + 1) * 128],
                          ps[6 + tt % 2][0:8, 0:128]).then_inc(vre, 1)
        v.wait_ge(gp, 16 * 16)
        v.tensor_tensor(hw[:, :], hA[:, :], rexp[:, :],
                        ALU.mult).then_inc(vhw, 1)

        def update(m):
            v.wait_ge(mmg, m + 1)
            v.wait_ge(ld, 16 * (2 * KC + m + 1))
            if m >= 1:
                v.wait_ge(stm, 16 * m)
            v.wait_ge(aa, 5)
            for tg in range(4):
                ins = v.scalar_tensor_tensor(
                    xn[:, tg * 512:(tg + 1) * 512], ps[tg][:, :], 0.5,
                    xf[:, (m % 2) * T + tg * 512:(m % 2) * T + (tg + 1) * 512],
                    ALU.mult, ALU.add)
            chain(ins)
            v.tensor_scalar(xn[:, :], xn[:, :], hbcol[:, m:m + 1], None,
                            ALU.add).then_inc(evu, 1)

        for m in range(KC):
            update(m)


    return ctx.finish("molora", dict(sync=SEC_sync, gpsimd=SEC_gpsimd, scalar=SEC_scalar, tensor=SEC_tensor, vector=SEC_vector),
                      finals=[(stm, 16 * KC)])




def build_llama(dbg=False, fx=None):
    KIND = "ExternalOutput" if dbg else "Internal"
    ctx = fx if fx is not None else Ctx()
    nc = ctx.nc
    x_in = ctx.dram("x_fm", [KC, 128, T], F32, kind="ExternalInput")
    x_out = ctx.dram("x_fm_o", [KC, 128, T], F32, kind="ExternalOutput")
    q1b = ctx.dram("q1_blk", [32, 128, 32, 16], I32, kind="ExternalInput")
    o1b = ctx.dram("o1_blk", [32, 128, 32, 16], I32, kind="ExternalInput")
    gub = ctx.dram("gu_blk", [172, 128, 32, 16], I32, kind="ExternalInput")
    dnb = ctx.dram("dn_blk", [32, 128, 86, 16], I32, kind="ExternalInput")
    q1s = ctx.dram("q1_sblk", [32, 4096], F32, kind="ExternalInput")
    o1s = ctx.dram("o1_sblk", [32, 4096], F32, kind="ExternalInput")
    gus = ctx.dram("gu_sblk", [172, 4096], F32, kind="ExternalInput")
    dns = ctx.dram("dn_sblk", [32, 11008], F32, kind="ExternalInput")
    zq1 = ctx.dram("zs_q1", [32, 4096], F32, kind="ExternalInput")
    zo1 = ctx.dram("zs_o1", [32, 4096], F32, kind="ExternalInput")
    zgu = ctx.dram("zs_gu", [32, 22016], F32, kind="ExternalInput")
    zdn = ctx.dram("zs_dn", [86, 4096], F32, kind="ExternalInput")
    shl1 = ctx.dram("shift_l1", [128, 8], I32, kind="ExternalInput")
    ln1c = ctx.dram("ln1_col", [128, 32], F32, kind="ExternalInput")
    ln2c = ctx.dram("ln2_col", [128, 32], F32, kind="ExternalInput")
    w1deq = ctx.dram("w1deq", [32, 128, 4096], BF16, kind=KIND)
    wodeq = ctx.dram("wodeq", [32, 128, 4096], BF16, kind=KIND)
    gudeq = ctx.dram("gudeq", [172, 128, 4096], BF16, kind=KIND)
    dndeq = ctx.dram("dndeq", [32, 128, 11008], BF16, kind=KIND)
    traw = ctx.dram("traw", [32, 128, T], BF16, kind=KIND)

    EPS2 = 1e-6
    NG, GT = 4, 512

    onesb = ctx.sbuf("onesb", [128, 1], BF16)
    onesbr = ctx.sbuf("onesbr", [1, 128], BF16)
    onesfr = ctx.sbuf("onesfr", [1, 128], F32)
    epsc = ctx.sbuf("epsc", [128, 1], F32)
    eps2c = ctx.sbuf("eps2c", [128, 1], F32)
    shc = ctx.sbuf("shc", [128, 8], I32)
    l1col = ctx.sbuf("l1col", [128, 32], F32)
    l2col = ctx.sbuf("l2col", [128, 32], F32)
    r_bc = ctx.sbuf("r_bc", [128, T], BF16)
    mr_bc = ctx.sbuf("mr_bc", [128, T], BF16)
    r2_bc = ctx.sbuf("r2_bc", [128, T], BF16)
    r3bc = ctx.sbuf("r3bc", [128, GT], F32)
    r3row = ctx.sbuf("r3row", [1, GT], F32)
    inds = ctx.sbuf("inds", [128, 64], BF16)
    zst = ctx.sbuf("zst", [128, 256], BF16)
    H = ctx.sbuf("H", [128, KC * T], BF16)
    xf = ctx.sbuf("xf", [128, 2 * T], F32)
    sq = ctx.sbuf("sq", [128, T], BF16)
    sra = ctx.sbuf("sra", [1, T], F32)
    srb = ctx.sbuf("srb", [1, T], F32)
    rows_r = ctx.sbuf("rows_r", [1, T], F32)
    rows_mr = ctx.sbuf("rows_mr", [1, T], F32)
    t1 = ctx.sbuf("t1", [128, T], F32)
    ctx.free_top("xf")
    colq1 = ctx.sbuf("colq1", [128, 2048], BF16)
    colq1b = ctx.sbuf("colq1b", [128, 2048], BF16)
    zstA = ctx.sbuf("zstA", [128, 256], BF16)
    swq = ctx.sbuf("swq", [128, 2 * 1376], I32)
    qint = ctx.sbuf("qint", [128, 2048], I32)
    srow = ctx.sbuf("srow", [1, 4096], BF16)
    wsB = ctx.sbuf("wsB", [128, 2 * 4096], BF16)
    ctx.free_top("swq")
    wsA = ctx.sbuf("wsA", [128, 4096], BF16)
    tstag = ctx.sbuf("tstag", [128, T], BF16)
    ctx.free_top("wsA")
    ctx.free_top("H")
    colg = ctx.sbuf("colg", [128, GT], BF16)
    colgb = ctx.sbuf("colgb", [128, GT], BF16)
    xf2 = ctx.sbuf("xf2", [128, 6 * GT], F32)
    h2g = ctx.sbuf("h2g", [128, KC * GT], BF16)
    H2s = ctx.sbuf("H2s", [128, KC * GT], BF16)
    ws = ctx.sbuf("ws", [128, 2 * 4096], BF16)
    sg = ctx.sbuf("sg", [128, GT], BF16)
    sqg = ctx.sbuf("sqg", [128, GT], BF16)
    xnst = ctx.sbuf("xnst", [128, 2 * GT], F32)
    gsc = ctx.sbuf("gsc", [128, 2 * GT], F32)
    trg = ctx.sbuf("trg", [128, KC * GT], BF16)
    ctx.free_top("trg")
    gbuf = ctx.sbuf("gbuf", [128, 86 * GT], BF16)

    ps = [ctx.psum(f"ps{i}", [128, 512], F32) for i in range(8)]

    gini = ctx.sem("gini")
    lda = ctx.sem("lda")
    ld = ctx.sem("ld")
    acts = ctx.sem("acts")
    pes = ctx.sem("pes")
    vstat = ctx.sem("vstat")
    aa = ctx.sem("aa")
    bb = ctx.sem("bb")
    vbe = ctx.sem("vbe")
    vb = ctx.sem("vb")
    hb = ctx.sem("hb")
    ldq = ctx.sem("ldq")
    mmb = ctx.sem("mmb")
    vd = ctx.sem("vd")
    stw = ctx.sem("stw")
    ldw = ctx.sem("ldw")
    mmg = ctx.sem("mmg")
    vte = ctx.sem("vte")
    vev = ctx.sem("vev")
    ldt = ctx.sem("ldt")
    ldx = ctx.sem("ldx")
    aag = ctx.sem("aag")
    pst = ctx.sem("pst")
    vr3 = ctx.sem("vr3")
    vgu = ctx.sem("vgu")
    vh2 = ctx.sem("vh2")
    stm = ctx.sem("stm")
    stt2 = ctx.sem("stt2")
    vv = ctx.sem("vv")
    csa = ctx.sem("csa")
    csb = ctx.sem("csb")

    DQ = [
        ("q1", q1b, q1s, zq1, 4096, w1deq, 32, [(0, 32)], 32),
        ("o1", o1b, o1s, zo1, 4096, wodeq, 32, [(0, 32)], 32),
        ("gu", gub, gus, zgu, 22016, gudeq, 172, [(0, 32)], 32),
        ("dn", dnb, dns, zdn, 4096, dndeq, 32, [(0, 32), (32, 64), (64, 86)], 86),
    ]

    def slices_of(k0, k1):
        cols = (k1 - k0) * 128
        out = []
        c = 0
        while c < cols:
            w = min(512, cols - c)
            out.append((c, w))
            c += w
        return out

    PIECES = []
    for (nm, qb, sb, zt, zN, dq, ns, pcs, G) in DQ:
        for m in range(ns):
            for (k0, k1) in pcs:
                PIECES.append(dict(nm=nm, qb=qb, sb=sb, zt=zt, zN=zN, dq=dq,
                                   m=m, k0=k0, k1=k1, G=G,
                                   sl=slices_of(k0, k1)))
    SLB = [0]
    for p in PIECES:
        SLB.append(SLB[-1] + len(p["sl"]))
    NPC = len(PIECES)

    def Jo(g, m):
        return 32 + 300 * g + m + 1

    def Jgu(g, i, up):
        return 32 + 300 * g + 32 + 2 * i + (2 if up else 1)

    def Jdn(g, mp, pi):
        return 32 + 300 * g + 204 + 3 * mp + pi + 1

    DNP = [(0, 4096), (4096, 8192), (8192, 11008)]

    def SEC_sync(s):
        for k in range(KC):
            if k >= 2:
                s.wait_ge(acts, k - 1)
            s.dma_start(out=xf[:, (k % 2) * T:(k % 2 + 1) * T],
                        in_=x_in[k, :, :]).then_inc(ld, 16)
        for k in range(KC):
            if k == 0:
                s.wait_ge(acts, KC - 1)
            elif k == 1:
                s.wait_ge(acts, KC)
            else:
                s.wait_ge(vb, k - 1)
            s.dma_start(out=xf[:, (k % 2) * T:(k % 2 + 1) * T],
                        in_=x_in[k, :, :]).then_inc(ld, 16)
        for pc, p in enumerate(PIECES):
            s.wait_ge(vd, SLB[pc + 1])
            s.dma_start(
                out=p["dq"][p["m"], :, p["k0"] * 128:p["k1"] * 128],
                in_=wsB[:, (pc % 2) * 4096:(pc % 2) * 4096 + (p["k1"] - p["k0"]) * 128]
                ).then_inc(stw, 16)
        for m in range(KC):
            s.wait_ge(vte, m + 1)
            s.dma_start(out=traw[m, :, :], in_=tstag[:]).then_inc(stt2, 16)
        s.wait_ge(stt2, 16 * KC)
        for g in range(NG):
            if g >= 1:
                s.wait_ge(mmg, 32 + 300 * g)
            for k in range(KC):
                s.dma_start(out=trg[:, k * GT:(k + 1) * GT],
                            in_=traw[k, :, g * GT:(g + 1) * GT]).then_inc(ldt, 16)
            for m in range(KC):
                c = 64 * g + m
                if c >= 6:
                    s.wait_ge(vev, c - 6 + 1)
                s.dma_start(out=xf2[:, (c % 6) * GT:(c % 6 + 1) * GT],
                            in_=x_in[m, :, g * GT:(g + 1) * GT]).then_inc(ldx, 16)
            for m in range(KC):
                c = 64 * g + 32 + m
                if c >= 6:
                    s.wait_ge(vev, c - 6 + 1)
                s.dma_start(out=xf2[:, (c % 6) * GT:(c % 6 + 1) * GT],
                            in_=x_in[m, :, g * GT:(g + 1) * GT]).then_inc(ldx, 16)
                s.wait_ge(vev, c + 1)
                s.dma_start(out=x_out[m, :, g * GT:(g + 1) * GT],
                            in_=xnst[:, (m % 2) * GT:(m % 2 + 1) * GT]
                            ).then_inc(stm, 16)
        s.wait_ge(stm, 16 * NG * KC)
        s.wait_ge(stw, 16 * NPC)

    def SEC_gpsimd(g_):
        g_.memset(onesb[:], 1.0)
        g_.memset(onesbr[:], 1.0)
        g_.memset(onesfr[:], 1.0)
        g_.memset(epsc[:], EPS_LN)
        g_.memset(eps2c[:], EPS2)
        g_.memset(inds[:, :], 0.0)
        g_.memset(inds[:, 31:32], -1.0)
        g_.engine_nop().then_inc(gini, 1)
        g_.dma_start(out=shc[:], in_=shl1[:, :]).then_inc(lda, 16)
        g_.dma_start(out=l1col[:], in_=ln1c[:, :]).then_inc(lda, 16)
        g_.dma_start(out=l2col[:], in_=ln2c[:, :]).then_inc(lda, 16)
        for k in range(KC):
            g_.wait_ge(vb, k + 1)
            g_.wait_ge(lda, 32)
            g_.tensor_tensor(H[:, k * T:(k + 1) * T], t1[:], mr_bc[:],
                             ALU.subtract).then_inc(hb, 1)
            g_.wait_ge(hb, k + 1)
            g_.tensor_scalar(H[:, k * T:(k + 1) * T], H[:, k * T:(k + 1) * T],
                             l1col[:, k:k + 1], None, ALU.mult).then_inc(hb, 1)
        for pc, p in enumerate(PIECES):
            if pc == 0:
                g_.wait_ge(hb, 2 * KC)
            if pc >= 2:
                g_.wait_ge(vd, SLB[pc - 1])
            nw = (p["k1"] - p["k0"]) * 16
            g_.dma_start(out=swq[:, (pc % 2) * 1376:(pc % 2) * 1376 + nw],
                         in_=p["qb"][p["m"], :, p["k0"]:p["k1"], :]
                         ).then_inc(ldq, 16)
            if pc >= 1:
                g_.wait_ge(mmb, SLB[pc])
            nc_ = (p["k1"] - p["k0"]) * 128
            g_.dma_start(out=srow[0:1, 0:nc_],
                         in_=p["sb"][p["m"]:p["m"] + 1, p["k0"] * 128:p["k1"] * 128]
                         ).then_inc(ldq, 16)
        for m in range(KC):
            if m == 0:
                g_.wait_ge(stw, 16 * NPC)
            g_.wait_ge(stw, 16 * (m + 1))
            if m >= 1:
                g_.wait_ge(mmg, m)
            g_.dma_start(out=wsA[:, :],
                         in_=w1deq[m, :, :]).then_inc(ldw, 16)
            g_.dma_start(out=zstA[0:32, (m % 2) * 128:(m % 2) * 128 + 128],
                         in_=zq1[0:32, m * 128:(m + 1) * 128]).then_inc(ldw, 16)
        for g in range(NG):
            for m in range(KC):
                J = Jo(g, m)
                if J >= 35:
                    g_.wait_ge(mmg, J - 2)
                else:
                    g_.wait_ge(mmg, KC)
                if g == 0:
                    g_.wait_ge(stw, 16 * (32 + m + 1))
                g_.dma_start(out=ws[:, (J % 2) * 4096:(J % 2 + 1) * 4096],
                             in_=wodeq[m, :, :]).then_inc(ldw, 16)
                g_.dma_start(out=zst[0:32, (J % 2) * 128:(J % 2) * 128 + 128],
                             in_=zo1[0:32, m * 128:(m + 1) * 128]).then_inc(ldw, 16)
            for i in range(86):
                for up in (False, True):
                    J = Jgu(g, i, up)
                    mi = (i + 86) if up else i
                    g_.wait_ge(mmg, J - 2)
                    if g == 0:
                        g_.wait_ge(stw, 16 * (64 + mi + 1))
                    g_.dma_start(out=ws[:, (J % 2) * 4096:(J % 2 + 1) * 4096],
                                 in_=gudeq[mi, :, :]).then_inc(ldw, 16)
                    g_.dma_start(out=zst[0:32, (J % 2) * 128:(J % 2) * 128 + 128],
                                 in_=zgu[0:32, mi * 128:(mi + 1) * 128]
                                 ).then_inc(ldw, 16)
            for mp in range(KC):
                for pi, (c0, c1) in enumerate(DNP):
                    J = Jdn(g, mp, pi)
                    g_.wait_ge(mmg, J - 2)
                    if g == 0:
                        g_.wait_ge(stw, 16 * (96 + 3 * mp + pi + 1))
                    g_.dma_start(out=ws[:, (J % 2) * 4096:(J % 2) * 4096 + (c1 - c0)],
                                 in_=dndeq[mp, :, c0:c1]).then_inc(ldw, 16)
                    g_.dma_start(out=zst[0:86, (J % 2) * 128:(J % 2) * 128 + 128],
                                 in_=zdn[0:86, mp * 128:(mp + 1) * 128]
                                 ).then_inc(ldw, 16)

    def SEC_scalar(a):
        for k in range(KC):
            a.wait_ge(ld, 16 * (k + 1))
            if k >= 1:
                a.wait_ge(pes, k)
            a.activation(H[:, k * T:(k + 1) * T],
                         xf[:, (k % 2) * T:(k % 2 + 1) * T], AF.Copy)
            a.activation(sq[:, :], xf[:, (k % 2) * T:(k % 2 + 1) * T],
                         AF.Square).then_inc(acts, 1)
        a.wait_ge(vstat, 1)
        a.wait_ge(gini, 1)
        a.activation(rows_r[0:1, :], srb[0:1, :], AF.Sqrt,
                     bias=epsc[0:1, 0:1]).then_inc(aa, 4)
        a.wait_ge(vv, 1000)
        a.activation(sra[0:1, :], srb[0:1, :], AF.Sqrt,
                     bias=eps2c[0:1, 0:1]).then_inc(aa, 1)
        for g in range(NG):
            for m in range(KC):
                a.wait_ge(vev, 64 * g + m + 1)
                if g + m > 0:
                    a.wait_ge(pst, 33 * g + m)
                a.activation(sqg[:, :], h2g[:, m * GT:(m + 1) * GT],
                             AF.Square).then_inc(aag, 1)
            a.wait_ge(vr3, 3 * g + 1)
            a.activation(ps[4][64:65, :], ps[4][32:33, :], AF.Sqrt,
                         bias=eps2c[0:1, 0:1]).then_inc(aag, 1)
            for i in range(86):
                a.wait_ge(mmg, Jgu(g, i, False))
                if i >= 1:
                    a.wait_ge(vgu, 86 * g + i)
                a.activation(sg[:, :], ps[i % 2][:, :], AF.Silu).then_inc(aag, 1)

    def SEC_tensor(te):
        te.wait_ge(gini, 1)
        for k in range(KC):
            te.wait_ge(acts, k + 1)
            st, sp = (k == 0), (k == KC - 1)
            for tg in range(4):
                te.matmul(ps[tg][0:1, :], onesb[:],
                          H[:, k * T + tg * 512:k * T + (tg + 1) * 512],
                          start=st, stop=sp)
            for tg in range(4):
                ins = te.matmul(ps[4 + tg][0:1, :], onesb[:],
                                sq[:, tg * 512:(tg + 1) * 512],
                                start=st, stop=sp)
            ins.then_inc(pes, 1)
        te.wait_ge(vstat, 2)
        for tg in range(4):
            te.matmul(ps[tg][:, :], onesfr[:], rows_r[0:1, tg * 512:(tg + 1) * 512],
                      start=True, stop=True)
            te.matmul(ps[4 + tg][:, :], onesfr[:],
                      rows_mr[0:1, tg * 512:(tg + 1) * 512],
                      start=True, stop=True).then_inc(bb, 1)
        te.wait_ge(vstat, 3)
        for tg in range(4):
            te.wait_ge(vbe, tg + 1)
            te.matmul(ps[tg][:, :], onesfr[:], srb[0:1, tg * 512:(tg + 1) * 512],
                      start=True, stop=True).then_inc(bb, 1)
        for tg in range(4):
            if tg == 0:
                te.wait_ge(vbe, 8)
                te.wait_ge(hb, 2 * KC)
            else:
                te.wait_ge(csb, tg)
            for g in range(KC):
                ins = te.matmul(ps[6][0:32, :], inds[:, 31 - g:63 - g],
                                H[:, g * T + tg * 512:g * T + tg * 512 + 512],
                                start=(g == 0), stop=(g == KC - 1))
            ins.then_inc(csa, 1)
        for pc, p in enumerate(PIECES):
            te.wait_ge(ldq, 32 * (pc + 1))
            for si, (c0, w) in enumerate(p["sl"]):
                S = SLB[pc] + si
                if S == 0:
                    te.wait_ge(vbe, 8)
                if S >= 2:
                    te.wait_ge(vd, S - 1)
                te.matmul(ps[4 + S % 2][:, 0:w], onesbr[:],
                          srow[0:1, c0:c0 + w], start=True,
                          stop=True).then_inc(mmb, 1)
        for m in range(KC):
            te.wait_ge(ldw, 32 * (m + 1))
            if m == 0:
                te.wait_ge(vbe, 8)
                te.wait_ge(csb, 4)
            if m >= 1:
                te.wait_ge(vte, m)
            for k in range(KC):
                st = (k == 0)
                for tg in range(4):
                    te.matmul(
                        ps[tg][:, :],
                        wsA[:, k * 128:(k + 1) * 128],
                        H[:, k * T + tg * 512:k * T + (tg + 1) * 512],
                        start=st, stop=False)
            for tg in range(4):
                te.matmul(
                    ps[tg][:, :], zstA[0:32, (m % 2) * 128:(m % 2) * 128 + 128],
                    colq1[0:32, tg * 512:(tg + 1) * 512],
                    start=False, stop=False)
                ins = te.matmul(
                    ps[tg][:, :], zstA[0:32, (m % 2) * 128:(m % 2) * 128 + 128],
                    colq1b[0:32, tg * 512:(tg + 1) * 512],
                    start=False, stop=True)
            ins.then_inc(mmg, 1)
        for g in range(NG):
            te.wait_ge(ldt, 16 * 32 * (g + 1))
            te.wait_ge(csb, 4 + 5 * g)
            for k in range(KC):
                ins = te.matmul(ps[6][0:32, :], inds[:, 31 - k:63 - k],
                                trg[:, k * GT:(k + 1) * GT],
                                start=(k == 0), stop=(k == KC - 1))
            ins.then_inc(csa, 1)
            for m in range(KC):
                J = Jo(g, m)
                te.wait_ge(ldw, 32 * J)
                c = 64 * g + m
                if c >= 2:
                    te.wait_ge(vev, c - 1)
                if m == 0:
                    te.wait_ge(csb, 5 + 5 * g)
                for k in range(KC):
                    te.matmul(
                        ps[2 + m % 2][:, :],
                        ws[:, (J % 2) * 4096 + k * 128:(J % 2) * 4096 + (k + 1) * 128],
                        trg[:, k * GT:(k + 1) * GT],
                        start=(k == 0), stop=False)
                te.matmul(
                    ps[2 + m % 2][:, :], zst[0:32, (J % 2) * 128:(J % 2) * 128 + 128],
                    colg[0:32, :], start=False, stop=False)
                ins = te.matmul(
                    ps[2 + m % 2][:, :], zst[0:32, (J % 2) * 128:(J % 2) * 128 + 128],
                    colgb[0:32, :], start=False, stop=True)
                ins.then_inc(mmg, 1)
                if m >= 1:
                    te.wait_ge(aag, 119 * g + m)
                    te.matmul(ps[4][0:1, :], onesb[:], sqg[:, :],
                              start=(m == 1), stop=(m == KC - 1)
                              ).then_inc(pst, 1)
            te.wait_ge(aag, 119 * g + 32)
            te.matmul(ps[4][0:1, :], onesb[:], sqg[:, :],
                      start=False, stop=True).then_inc(pst, 1)
            te.wait_ge(vr3, 3 * g + 2)
            te.matmul(ps[5][:, :], onesfr[:], r3row[0:1, :],
                      start=True, stop=True).then_inc(pst, 1)
            te.wait_ge(vh2, 32 * (g + 1))
            te.wait_ge(csb, 5 + 5 * g)
            for k in range(KC):
                ins = te.matmul(ps[6][0:32, :], inds[:, 31 - k:63 - k],
                                H2s[:, k * GT:(k + 1) * GT],
                                start=(k == 0), stop=(k == KC - 1))
            ins.then_inc(csa, 1)
            for i in range(86):
                for up in (False, True):
                    J = Jgu(g, i, up)
                    te.wait_ge(ldw, 32 * J)
                    bank = (2 if up else 0) + i % 2
                    if up:
                        if i >= 2:
                            te.wait_ge(vgu, 86 * g + i - 1)
                    else:
                        if i == 0:
                            te.wait_ge(csb, 6 + 5 * g)
                        if i >= 2:
                            te.wait_ge(aag, 119 * g + 33 + i - 1)
                    for k in range(KC):
                        te.matmul(
                            ps[bank][:, :],
                            ws[:, (J % 2) * 4096 + k * 128:(J % 2) * 4096 + (k + 1) * 128],
                            H2s[:, k * GT:(k + 1) * GT],
                            start=(k == 0), stop=False)
                    te.matmul(
                        ps[bank][:, :],
                        zst[0:32, (J % 2) * 128:(J % 2) * 128 + 128],
                        colg[0:32, :], start=False, stop=False)
                    ins = te.matmul(
                        ps[bank][:, :],
                        zst[0:32, (J % 2) * 128:(J % 2) * 128 + 128],
                        colgb[0:32, :], start=False, stop=True)
                    ins.then_inc(mmg, 1)
            te.wait_ge(vgu, 86 * (g + 1))
            te.wait_ge(csb, 6 + 5 * g)
            for ci, (g0, g1) in enumerate([(0, 32), (32, 64), (64, 86)]):
                n = g1 - g0
                for j in range(n):
                    ins = te.matmul(ps[6][g0:g0 + n, :], inds[:, 31 - j:31 - j + n],
                                    gbuf[:, (g0 + j) * GT:(g0 + j + 1) * GT],
                                    start=(j == 0), stop=(j == n - 1))
                ins.then_inc(csa, 1)
            for mp in range(KC):
                if mp <= 1:
                    te.wait_ge(aag, 119 * (g + 1))
                if mp == 0:
                    te.wait_ge(csb, 9 + 5 * g)
                c = 64 * g + 32 + mp
                if mp >= 2:
                    te.wait_ge(vev, c - 1)
                kk = 0
                for pi, (c0, c1) in enumerate(DNP):
                    J = Jdn(g, mp, pi)
                    te.wait_ge(ldw, 32 * J)
                    nk = (c1 - c0) // 128
                    for k in range(nk):
                        ins = te.matmul(
                            ps[mp % 2][:, :],
                            ws[:, (J % 2) * 4096 + k * 128:(J % 2) * 4096 + (k + 1) * 128],
                            gbuf[:, (kk + k) * GT:(kk + k + 1) * GT],
                            start=(kk + k == 0), stop=False)
                    kk += nk
                    if pi < 2:
                        ins.then_inc(mmg, 1)
                Jl = Jdn(g, mp, 2)
                te.matmul(
                    ps[mp % 2][:, :],
                    zst[0:86, (Jl % 2) * 128:(Jl % 2) * 128 + 128],
                    colg[0:86, :], start=False, stop=False)
                ins = te.matmul(
                    ps[mp % 2][:, :],
                    zst[0:86, (Jl % 2) * 128:(Jl % 2) * 128 + 128],
                    colgb[0:86, :], start=False, stop=True)
                ins.then_inc(mmg, 1)

    def SEC_vector(v):
        nvv = [0]

        def chain(ins):
            nvv[0] += 1
            ins.then_inc(vv, 1)
            v.wait_ge(vv, nvv[0])

        v.wait_ge(pes, KC)
        for tg in range(4):
            ins = v.tensor_copy(sra[0:1, tg * 512:(tg + 1) * 512], ps[tg][0:1, :])
        for tg in range(4):
            ins = v.tensor_copy(srb[0:1, tg * 512:(tg + 1) * 512],
                                ps[4 + tg][0:1, :])
        chain(ins)
        ins = v.tensor_scalar(sra[0:1, :], sra[0:1, :], 1.0 / D, None, ALU.mult)
        ins = v.tensor_scalar(srb[0:1, :], srb[0:1, :], 1.0 / D, None, ALU.mult)
        chain(ins)
        ins = v.tensor_tensor(rows_mr[0:1, :], sra[0:1, :], sra[0:1, :], ALU.mult)
        chain(ins)
        v.tensor_tensor(srb[0:1, :], srb[0:1, :], rows_mr[0:1, :],
                        ALU.subtract).then_inc(vstat, 1)
        v.wait_ge(aa, 4)
        ins = v.reciprocal(rows_mr[0:1, :], rows_r[0:1, :])
        chain(ins)
        ins = v.tensor_tensor(srb[0:1, :], srb[0:1, :], rows_mr[0:1, :], ALU.mult)
        chain(ins)
        ins = v.tensor_tensor(srb[0:1, :], srb[0:1, :], rows_mr[0:1, :], ALU.mult)
        chain(ins)
        ins = v.tensor_tensor(sra[0:1, :], sra[0:1, :], rows_mr[0:1, :], ALU.mult)
        chain(ins)
        ins = v.tensor_copy(rows_r[0:1, :], rows_mr[0:1, :])
        ins = v.tensor_copy(rows_mr[0:1, :], sra[0:1, :])
        ins.then_inc(vstat, 1)
        nvv[0] += 1
        ins2 = v.engine_nop()
        ins2.then_inc(vv, 1000 - nvv[0] + 1)
        nvv[0] = 1000
        v.wait_ge(vv, 1000)
        v.wait_ge(aa, 5)
        ins = v.reciprocal(srb[0:1, :], sra[0:1, :])
        nvv[0] += 1
        ins.then_inc(vv, 1)
        v.wait_ge(vv, nvv[0])
        v.engine_nop().then_inc(vstat, 1)
        for tg in range(4):
            v.wait_ge(bb, tg + 1)
            v.tensor_copy(r_bc[:, tg * 512:(tg + 1) * 512], ps[tg][:, :])
            v.tensor_copy(mr_bc[:, tg * 512:(tg + 1) * 512],
                          ps[4 + tg][:, :]).then_inc(vbe, 1)
        for tg in range(4):
            v.wait_ge(bb, 4 + tg + 1)
            v.tensor_copy(r2_bc[:, tg * 512:(tg + 1) * 512],
                          ps[tg][:, :]).then_inc(vbe, 1)
        for k in range(KC):
            v.wait_ge(ld, 16 * (KC + k + 1))
            if k >= 1:
                v.wait_ge(hb, 2 * k)
            v.tensor_tensor(t1[:], xf[:, (k % 2) * T:(k % 2 + 1) * T],
                            r_bc[:], ALU.mult).then_inc(vb, 1)
        for tg in range(4):
            v.wait_ge(csa, tg + 1)
            ins = v.tensor_copy(colq1[0:32, tg * 512:(tg + 1) * 512],
                                ps[6][0:32, :])
            nvv[0] += 1
            ins.then_inc(vv, 1)
            v.wait_ge(vv, nvv[0])
            v.tensor_tensor(colq1b[0:32, tg * 512:(tg + 1) * 512],
                            ps[6][0:32, :],
                            colq1[0:32, tg * 512:(tg + 1) * 512],
                            ALU.subtract).then_inc(csb, 1)
        for pc, p in enumerate(PIECES):
            if pc >= 2:
                v.wait_ge(stw, 16 * (pc - 1))
            ncol = (p["k1"] - p["k0"]) * 128
            nhalf = (ncol + 2047) // 2048
            for h in range(nhalf):
                h0 = h * 2048
                h1 = min(h0 + 2048, ncol)
                k0h = p["k0"] + h0 // 128
                nk = (h1 - h0) // 128
                v.wait_ge(ldq, 32 * pc + 16)
                ins = v.tensor_tensor(
                    AP(qint, 0, [[2048, 128], [128, nk], [8, 16], [1, 8]]),
                    AP(swq, (pc % 2) * 1376 + (h0 // 128 * 16),
                       [[2 * 1376, 128], [16, nk], [1, 16], [0, 8]]),
                    AP(shc, 0, [[8, 128], [0, nk], [0, 16], [1, 8]]),
                    ALU.logical_shift_right)
                chain(ins)
                ins = v.tensor_scalar(qint[:, 0:h1 - h0], qint[:, 0:h1 - h0], 15,
                                      None, ALU.bitwise_and)
                chain(ins)
                for si, (c0, w) in enumerate(p["sl"]):
                    if not (h0 <= c0 < h1):
                        continue
                    S = SLB[pc] + si
                    v.wait_ge(mmb, S + 1)
                    off = (pc % 2) * 4096 + c0
                    v.tensor_tensor(wsB[:, off:off + w],
                                    qint[:, c0 - h0:c0 - h0 + w],
                                    ps[4 + S % 2][:, 0:w],
                                    ALU.mult).then_inc(vd, 1)
        for m in range(KC):
            v.wait_ge(mmg, m + 1)
            if m >= 1:
                v.wait_ge(stt2, 16 * m)
            for tg in range(4):
                ins = v.tensor_copy(tstag[:, tg * 512:(tg + 1) * 512], ps[tg][:, :])
            ins.then_inc(vte, 1)
        for g in range(NG):
            v.wait_ge(csa, 5 + 5 * g)
            ins = v.tensor_copy(colg[0:32, :], ps[6][0:32, :])
            nvv[0] += 1
            ins.then_inc(vv, 1)
            v.wait_ge(vv, nvv[0])
            v.tensor_tensor(colgb[0:32, :], ps[6][0:32, :], colg[0:32, :],
                            ALU.subtract).then_inc(csb, 1)
            for m in range(KC):
                c = 64 * g + m
                v.wait_ge(mmg, Jo(g, m))
                v.wait_ge(ldx, 16 * (c + 1))
                gs = slice(g * GT, (g + 1) * GT)
                ins = v.tensor_tensor(gsc[:, 0:GT],
                                      xf2[:, (c % 6) * GT:(c % 6 + 1) * GT],
                                      r_bc[:, gs], ALU.mult)
                chain(ins)
                ins = v.tensor_tensor(gsc[:, 0:GT], gsc[:, 0:GT], mr_bc[:, gs],
                                      ALU.subtract)
                chain(ins)
                ins = v.tensor_tensor(gsc[:, GT:2 * GT], ps[2 + m % 2][:, :],
                                      r2_bc[:, gs], ALU.mult)
                chain(ins)
                v.tensor_tensor(h2g[:, m * GT:(m + 1) * GT], gsc[:, 0:GT],
                                gsc[:, GT:2 * GT], ALU.add).then_inc(vev, 1)
            v.wait_ge(pst, 33 * g + 32)
            ins = v.tensor_scalar(ps[4][32:33, :], ps[4][0:1, :], 1.0 / D, None,
                                  ALU.mult)
            ins.then_inc(vr3, 1)
            v.wait_ge(aag, 119 * g + 33)
            ins = v.reciprocal(r3row[0:1, :], ps[4][64:65, :])
            ins.then_inc(vr3, 1)
            v.wait_ge(pst, 33 * g + 33)
            ins = v.tensor_copy(r3bc[:, :], ps[5][:, :])
            ins.then_inc(vr3, 1)
            nvv[0] += 0
            v.wait_ge(vr3, 3 * g + 3)
            for k in range(KC):
                ins = v.tensor_tensor(gsc[:, 0:GT], h2g[:, k * GT:(k + 1) * GT],
                                      r3bc[:, :], ALU.mult)
                chain(ins)
                v.tensor_scalar(H2s[:, k * GT:(k + 1) * GT], gsc[:, 0:GT],
                                l2col[:, k:k + 1], None,
                                ALU.mult).then_inc(vh2, 1)
            v.wait_ge(csa, 6 + 5 * g)
            ins = v.tensor_copy(colg[0:32, :], ps[6][0:32, :])
            nvv[0] += 1
            ins.then_inc(vv, 1)
            v.wait_ge(vv, nvv[0])
            v.tensor_tensor(colgb[0:32, :], ps[6][0:32, :], colg[0:32, :],
                            ALU.subtract).then_inc(csb, 1)
            for i in range(86):
                v.wait_ge(aag, 119 * g + 33 + i + 1)
                v.wait_ge(mmg, Jgu(g, i, True))
                v.tensor_tensor(gbuf[:, i * GT:(i + 1) * GT], sg[:, :],
                                ps[2 + i % 2][:, :], ALU.mult).then_inc(vgu, 1)
            for ci, (g0, g1) in enumerate([(0, 32), (32, 64), (64, 86)]):
                v.wait_ge(csa, 7 + 5 * g + ci)
                ins = v.tensor_copy(colg[g0:g1, :], ps[6][g0:g1, :])
                nvv[0] += 1
                ins.then_inc(vv, 1)
                v.wait_ge(vv, nvv[0])
                v.tensor_tensor(colgb[g0:g1, :], ps[6][g0:g1, :],
                                colg[g0:g1, :], ALU.subtract).then_inc(csb, 1)
            for mp in range(KC):
                c = 64 * g + 32 + mp
                cst = 32 * g + mp
                v.wait_ge(mmg, Jdn(g, mp, 2))
                v.wait_ge(ldx, 16 * (c + 1))
                if cst >= 2:
                    v.wait_ge(stm, 16 * (cst - 1))
                ins = v.tensor_tensor(gsc[:, 0:GT], ps[mp % 2][:, :],
                                      h2g[:, mp * GT:(mp + 1) * GT], ALU.add)
                chain(ins)
                v.tensor_tensor(xnst[:, (mp % 2) * GT:(mp % 2 + 1) * GT],
                                gsc[:, 0:GT],
                                xf2[:, (c % 6) * GT:(c % 6 + 1) * GT],
                                ALU.add).then_inc(vev, 1)


    return ctx.finish("llama", dict(sync=SEC_sync, gpsimd=SEC_gpsimd, scalar=SEC_scalar, tensor=SEC_tensor, vector=SEC_vector),
                      finals=[(stm, 16 * NG * KC), (stw, 16 * NPC)])




def build_attn(stop_after='full', sub=9, dump=False, fx=None):
    PH = {'p1': 1, 'p2': 2, 'p3': 3, 'full': 4}[stop_after]
    ctx = fx if fx is not None else Ctx()
    nc = ctx.nc
    x_in = ctx.dram("x_fm", [KC, 128, T], F32, kind="ExternalInput")
    x_out = ctx.dram("x_fm_o", [KC, 128, T], F32, kind="ExternalOutput")
    qkvb = ctx.dram("qkv_blk", [96, 128, 32, 128], F32, kind="ExternalInput")
    gateb = ctx.dram("gate_blk", [32, 128, 32, 128], F32, kind="ExternalInput")
    outb = ctx.dram("out_blk", [32, 128, 32, 128], F32, kind="ExternalInput")
    nwc = ctx.dram("normw_col", [128, 32], F32, kind="ExternalInput")
    qdb = ctx.dram("qd_bc", [32, 128, 256], F32, kind="ExternalInput")
    kdb = ctx.dram("kd_bc", [32, 128, 256], F32, kind="ExternalInput")
    ddp = ctx.dram("dd_T", [32, 128, 2, 256], F32, kind="ExternalInput")
    cwd = ctx.dram("cw_d", [128, 128], F32, kind="ExternalInput")
    bdp = ctx.dram("bd_pows", [128, 32 * 9], F32, kind="ExternalInput")
    gate_d = ctx.dram("gate_d", [KC, 128, T], BF16, kind="Internal")
    attn_d = ctx.dram("attn_d", [KC, 128, T], F32, kind="Internal")
    q_d = ctx.dram("q_d", [32, 128, T], BF16, kind="Internal")
    sc_d = ctx.dram("sc_d", [32 * 128, 128], F32, kind="Internal")
    scg_d = ctx.dram("scg_d", [4 * 32 * 128, 128], F32, kind="Internal")
    if dump:
        scgo = ctx.dram("scg_o", [4 * 32 * 128, 128], F32, kind="ExternalOutput")
        attno = ctx.dram("attn_o", [KC, 128, T], F32, kind="ExternalOutput")

    EPSA = 1e-5
    NH, NB, BS = 32, 8, 256

    onesb = ctx.sbuf("onesb", [128, 1], BF16)
    onesfr = ctx.sbuf("onesfr", [1, 128], F32)
    epsc = ctx.sbuf("epsc", [128, 1], F32)
    identb = ctx.sbuf("identb", [128, 128], BF16)
    nwcol = ctx.sbuf("nwcol", [128, 32], F32)
    cw = ctx.sbuf("cw", [128, 128], F32)
    bdp_s = ctx.sbuf("bdp_s", [128, 32 * 9], F32)
    H = ctx.sbuf("H", [128, KC * T], BF16)
    xf = ctx.sbuf("xf", [128, 2 * T], F32)
    sq = ctx.sbuf("sq", [128, T], BF16)
    r_bc = ctx.sbuf("r_bc", [128, T], F32)
    mr_bc = ctx.sbuf("mr_bc", [128, T], F32)
    sra = ctx.sbuf("sra", [1, T], F32)
    srb = ctx.sbuf("srb", [1, T], F32)
    rows_r = ctx.sbuf("rows_r", [1, T], F32)
    rows_mr = ctx.sbuf("rows_mr", [1, T], F32)
    t1 = ctx.sbuf("t1", [128, T], F32)
    ctx.free_top("xf")
    ws = ctx.sbuf("ws", [128, 2 * 4096], BF16)
    qh = ctx.sbuf("qh", [128, T], BF16)
    kh = ctx.sbuf("kh", [128, T], BF16)
    vh = ctx.sbuf("vh", [128, T], BF16)
    qsc = ctx.sbuf("qsc", [128, T], BF16)
    ksc = ctx.sbuf("ksc", [128, T], BF16)
    kT = ctx.sbuf("kT", [128, T], BF16)
    vT = ctx.sbuf("vT", [128, T], BF16)
    sdec = ctx.sbuf("sdec", [128, 512], BF16)
    astage = ctx.sbuf("astage", [128, T], F32)
    kv = ctx.sbuf("kv", [128, 128], F32)
    kvbf = ctx.sbuf("kvbf", [128, 128], BF16)
    kvin = ctx.sbuf("kvin", [128, 128], F32)
    kvinb = ctx.sbuf("kvinb", [128, 128], BF16)
    ddh = ctx.sbuf("ddh", [128, 512], BF16)
    qdbc = ctx.sbuf("qdbc", [128, 256], BF16)
    kdbc = ctx.sbuf("kdbc", [128, 256], BF16)
    sgb = ctx.sbuf("sgb", [128, 512], F32)
    qs2 = ctx.sbuf("qs2", [128, T], BF16)
    ctx.free_top("ws")
    rstd_bc = ctx.sbuf("rstd_bc", [128, T], BF16)
    sqb = ctx.sbuf("sqb", [128, T], BF16)
    gb = ctx.sbuf("gb", [128, T], BF16)
    af = ctx.sbuf("af", [128, T], F32)
    ws2 = ctx.sbuf("ws2", [128, 2 * 4096], BF16)
    xn = ctx.sbuf("xn", [128, T], F32)
    xfp = ctx.sbuf("xfp", [128, 2 * T], F32)
    srowA = ctx.sbuf("srowA", [1, T], F32)

    ps = [ctx.psum(f"ps{i}", [128, 512], F32) for i in range(7)]
    p7b = ctx.psum("p7b", [128, 1024], BF16)

    gini = ctx.sem("gini")
    lda = ctx.sem("lda")
    ld = ctx.sem("ld")
    acts = ctx.sem("acts")
    pes = ctx.sem("pes")
    vstat = ctx.sem("vstat")
    aa = ctx.sem("aa")
    bb = ctx.sem("bb")
    vb = ctx.sem("vb")
    hb = ctx.sem("hb")
    ldw = ctx.sem("ldw")
    mmg = ctx.sem("mmg")
    ea = ctx.sem("ea")
    gstm = ctx.sem("gstm")
    qstm = ctx.sem("qstm")
    ldh = ctx.sem("ldh")
    vq = ctx.sem("vq")
    mmt = ctx.sem("mmt")
    vt = ctx.sem("vt")
    ps1 = ctx.sem("ps1")
    vsd = ctx.sem("vsd")
    pi = ctx.sem("pi")
    pu = ctx.sem("pu")
    vkv = ctx.sem("vkv")
    ek = ctx.sem("ek")
    eb = ctx.sem("eb")
    astm = ctx.sem("astm")
    scst = ctx.sem("scst")
    cc = ctx.sem("cc")
    ldsg = ctx.sem("ldsg")
    vci = ctx.sem("vci")
    eki = ctx.sem("eki")
    vq2 = ctx.sem("vq2")
    mmc = ctx.sem("mmc")
    vce = ctx.sem("vce")
    vbe = ctx.sem("vbe")
    dmp = ctx.sem("dmp")
    ldz = ctx.sem("ldz")
    acz = ctx.sem("acz")
    pz = ctx.sem("pz")
    vz = ctx.sem("vz")
    evu = ctx.sem("evu")
    stm = ctx.sem("stm")
    vv = ctx.sem("vv")

    def SEC_sync(s):
        for k in range(KC):
            if k >= 2:
                s.wait_ge(acts, k - 1)
            s.dma_start(out=xf[:, (k % 2) * T:(k % 2 + 1) * T],
                        in_=x_in[k, :, :]).then_inc(ld, 16)
        for k in range(KC):
            if k == 0:
                s.wait_ge(acts, KC - 1)
            elif k == 1:
                s.wait_ge(acts, KC)
            else:
                s.wait_ge(vb, k - 1)
            s.dma_start(out=xf[:, (k % 2) * T:(k % 2 + 1) * T],
                        in_=x_in[k, :, :]).then_inc(ld, 16)
        for m in range(KC):
            s.wait_ge(ea, m + 1)
            s.dma_start(out=gate_d[m, :, :], in_=qh[:]).then_inc(gstm, 16)
        for h in range(NH if PH >= 2 else 0):
            s.wait_ge(ea, KC + 3 * h + 1)
            s.dma_start(out=q_d[h, :, :], in_=qh[:]).then_inc(qstm, 16)
            if sub >= 3:
                s.wait_ge(eb, NB * (h + 1))
                s.dma_start(out=attn_d[h, :, :], in_=astage[:]).then_inc(astm, 16)
                s.wait_ge(vkv, NB * (h + 1))
                s.dma_start(out=sc_d[128 * h:128 * (h + 1), :], in_=kv[:]
                            ).then_inc(scst, 16)
        if PH >= 3:
            s.wait_ge(cc, 1)
            s.wait_ge(qstm, 16 * NH)
        for h in range(NH if PH >= 3 else 0):
            if h >= 1:
                s.wait_ge(vci, h)
            for cp in range(4):
                s.dma_start(out=sgb[:, cp * 128:(cp + 1) * 128],
                            in_=scg_d[cp * 4096 + 128 * h:cp * 4096 + 128 * (h + 1), :]
                            ).then_inc(ldsg, 16)
            if h >= 1:
                s.wait_ge(vq2, h)
            s.dma_start(out=qh[:], in_=q_d[h, :, :]).then_inc(ldsg, 16)
        if PH >= 4:
            s.wait_ge(astm, 32 * NH)
        for k in range(KC if PH >= 4 else 0):
            if k >= 1:
                s.wait_ge(acz, k)
            s.dma_start(out=af[:, :],
                        in_=attn_d[k, :, :]).then_inc(ldz, 16)
        for k in range(KC if (PH >= 4 and sub >= 6) else 0):
            if k == 0:
                s.wait_ge(acz, KC)
            if k >= 1:
                s.wait_ge(vz, k + 2)
            s.dma_start(out=af[:, :],
                        in_=attn_d[k, :, :]).then_inc(ldz, 16)
            s.dma_start(out=gb[:], in_=gate_d[k, :, :]).then_inc(ldz, 16)
        for m in range(KC if (PH >= 4 and sub >= 6) else 0):
            if m >= 2:
                s.wait_ge(evu, m - 1)
            s.dma_start(out=xfp[:, (m % 2) * T:(m % 2 + 1) * T],
                        in_=x_in[m, :, :]).then_inc(ldz, 16)
            s.wait_ge(evu, m + 1)
            s.dma_start(out=x_out[m, :, :], in_=xn[:]).then_inc(stm, 16)
        if PH >= 4 and sub == 4.2:
            s.wait_ge(acz, KC)
        elif PH >= 4 and sub == 4.5:
            s.wait_ge(vz, 1)
        elif PH >= 4 and sub == 4.7:
            s.wait_ge(vz, 2)
        elif PH >= 4 and sub == 4.8:
            s.wait_ge(pz, KC + 4)
        elif PH >= 4 and sub == 5:
            s.wait_ge(acz, KC + 1)
        elif PH >= 4:
            s.wait_ge(stm, 16 * KC)
            if dump:
                s.dma_start(out=scgo[:, :], in_=scg_d[:, :]).then_inc(dmp, 16)
                for k in range(KC):
                    s.dma_start(out=attno[k, :, :],
                                in_=attn_d[k, :, :]).then_inc(dmp, 16)
                s.wait_ge(dmp, 16 * (KC + 1))
        elif PH >= 2:
            if sub >= 3:
                s.wait_ge(astm, 16 * NH)
                s.wait_ge(scst, 16 * NH)
            else:
                s.wait_ge(qstm, 16 * NH)
        else:
            s.wait_ge(gstm, 16 * KC)

    def SEC_gpsimd(g):
        g.memset(onesb[:], 1.0)
        g.memset(onesfr[:], 1.0)
        g.memset(epsc[:], EPSA)
        g.iota(identb[:], [[1, 128]], channel_multiplier=-1,
               allow_small_or_imprecise_dtypes=True)
        g.tensor_single_scalar(identb[:], identb[:], 0.0, ALU.is_equal)
        g.engine_nop().then_inc(gini, 1)
        g.dma_start(out=nwcol[:], in_=nwc[:, :]).then_inc(lda, 16)
        g.dma_start(out=cw[:], in_=cwd[:, :]).then_inc(lda, 16)
        g.dma_start(out=bdp_s[:], in_=bdp[:, :]).then_inc(lda, 16)
        for k in range(KC):
            g.wait_ge(vb, k + 1)
            g.tensor_tensor(H[:, k * T:(k + 1) * T], t1[:], mr_bc[:],
                            ALU.subtract).then_inc(hb, 1)
        def head_consts(h):
            if h >= 1 and sub >= 2:
                g.wait_ge(vq, 2 * h)
                if sub >= 3:
                    g.wait_ge(vsd, NB * h)
            g.dma_start(out=qdbc[:], in_=qdb[h, :, :]).then_inc(ldh, 16)
            g.dma_start(out=kdbc[:], in_=kdb[h, :, :]).then_inc(ldh, 16)
            g.dma_start(out=ddh[:].rearrange("p (a n) -> p a n", a=2),
                        in_=ddp[h, :, :, :]).then_inc(ldh, 16)
            if sub >= 3:
                g.wait_ge(scst, 16 * h)
            g.memset(kv[:], 0.0)
            g.memset(kvbf[:], 0.0)
            g.engine_nop().then_inc(ldh, 1)

        for sidx in range(KC + 96 if PH >= 2 else KC):
            if sidx == 0:
                g.wait_ge(hb, KC)
            elif sidx >= 2:
                g.wait_ge(mmg, sidx - 1)
            src = gateb[sidx, :, :, :] if sidx < KC else qkvb[sidx - KC, :, :, :]
            g.dma_start(out=ws[:, (sidx % 2) * 4096:(sidx % 2 + 1) * 4096],
                        in_=src).then_inc(ldw, 16)
            if sidx >= KC and (sidx - KC) % 3 == 0:
                head_consts((sidx - KC) // 3)
        if PH >= 3:
            g.wait_ge(scst, 16 * NH)
        if PH >= 3:
            g.collective_compute(
            "AllGather", ALU.bypass,
                replica_groups=[[0, 1, 2, 3], [4, 5, 6, 7]],
                ins=[sc_d.ap().opt()], outs=[scg_d.ap().opt()]).then_inc(cc, 1)
        for h in range(NH if PH >= 3 else 0):
            if h == 0:
                g.wait_ge(vq, 2 * NH)
            else:
                g.wait_ge(vq2, h)
            g.dma_start(out=qdbc[:], in_=qdb[h, :, :]).then_inc(ldh, 16)
            g.wait_ge(vce, h + 1)
            g.dma_start(out=attn_d[h, :, :], in_=astage[:],
                        accum_op=ALU.add).then_inc(astm, 16)
        for m in range(KC if (PH >= 4 and sub >= 6) else 0):
            if m == 0:
                g.wait_ge(astm, 32 * NH)
            if m >= 2:
                g.wait_ge(mmg, KC + 96 + m - 1)
            g.dma_start(out=ws2[:, (m % 2) * 4096:(m % 2 + 1) * 4096],
                        in_=outb[m, :, :, :]).then_inc(ldw, 16)

    def SEC_scalar(a):
        for k in range(KC):
            a.wait_ge(ld, 16 * (k + 1))
            if k >= 1:
                a.wait_ge(pes, k)
            a.activation(H[:, k * T:(k + 1) * T],
                         xf[:, (k % 2) * T:(k % 2 + 1) * T], AF.Copy)
            a.activation(sq[:, :], xf[:, (k % 2) * T:(k % 2 + 1) * T],
                         AF.Square).then_inc(acts, 1)
        a.wait_ge(vstat, 1)
        a.wait_ge(gini, 1)
        a.activation(rows_r[0:1, :], srb[0:1, :], AF.Sqrt,
                     bias=epsc[0:1, 0:1]).then_inc(aa, 4)
        for m in range(KC):
            a.wait_ge(mmg, m + 1)
            if m >= 1:
                a.wait_ge(gstm, 16 * m)
            for tg in range(4):
                ins = a.activation(qh[:, tg * 512:(tg + 1) * 512], ps[tg][:, :],
                                   AF.Sigmoid)
            ins.then_inc(ea, 1)
        for h in range(NH if PH >= 2 else 0):
            for ci in range(3):
                m2 = 3 * h + ci
                a.wait_ge(mmg, KC + m2 + 1)
                if ci == 0:
                    if h == 0:
                        a.wait_ge(gstm, 16 * KC)
                    a.wait_ge(qstm, 16 * h)
                dst = (qh, kh, vh)[ci]
                for tg in range(4):
                    ins = a.activation(dst[:, tg * 512:(tg + 1) * 512],
                                       ps[tg][:, :], AF.Silu)
                ins.then_inc(ea, 1)
            for b_ in range(NB if sub >= 3 else 0):
                BA = NB * h + b_
                a.wait_ge(pi, BA + 1)
                if b_ == 0:
                    a.wait_ge(astm, 16 * h)
                a.activation(astage[:, b_ * 256:(b_ + 1) * 256],
                             ps[5][:, 0:256], AF.Copy).then_inc(eb, 1)
                a.wait_ge(vkv, BA + 1)
                a.activation(kvbf[:], kv[:], AF.Copy).then_inc(ek, 1)
        for h in range(NH if PH >= 3 else 0):
            a.wait_ge(vci, h + 1)
            if h >= 1:
                a.wait_ge(mmc, h)
            a.activation(kvinb[:], kvin[:], AF.Copy).then_inc(eki, 1)
            a.wait_ge(mmc, h + 1)
            a.wait_ge(astm, 16 * NH + 16 * h)
            for tg in range(4):
                ins = a.activation(astage[:, tg * 512:(tg + 1) * 512],
                                   ps[tg][:, :], AF.Copy)
            ins.then_inc(vce, 1)
        for k in range(KC if PH >= 4 else 0):
            a.wait_ge(ldz, 16 * (k + 1))
            if k >= 1 and sub >= 4.5:
                a.wait_ge(pz, k)
            a.activation(sqb[:, :], af[:, :],
                         AF.Square).then_inc(acz, 1)
        if PH < 4 or sub < 4.7:
            return
        a.wait_ge(vz, 1)
        a.activation(xn[0:1, :], srowA[0:1, :], AF.Sqrt,
                     bias=epsc[0:1, 0:1]).then_inc(acz, 1)

    def SEC_tensor(te):
        te.wait_ge(gini, 1)
        for k in range(KC):
            te.wait_ge(acts, k + 1)
            st, sp = (k == 0), (k == KC - 1)
            for tg in range(4):
                te.matmul(ps[tg][0:1, :], onesb[:],
                          H[:, k * T + tg * 512:k * T + (tg + 1) * 512],
                          start=st, stop=sp)
            for tg in range(4):
                dst = ps[4 + tg][0:1, :] if tg < 3 else ps[6][32:33, :]
                st_ = st and tg != 3
                sp_ = sp and tg != 2
                ins = te.matmul(dst, onesb[:], sq[:, tg * 512:(tg + 1) * 512],
                                start=st_, stop=sp_)
            ins.then_inc(pes, 1)
        te.wait_ge(vstat, 2)
        for tg in range(4):
            te.matmul(ps[tg][:, :], onesfr[:], rows_r[0:1, tg * 512:(tg + 1) * 512],
                      start=True, stop=True)
            if tg == 3:
                te.wait_ge(vbe, 1)
            te.matmul(ps[4 + tg][:, :] if tg < 3 else ps[6][:, :], onesfr[:],
                      rows_mr[0:1, tg * 512:(tg + 1) * 512],
                      start=True, stop=True).then_inc(bb, 1)
        def gemm(sidx, wbuf):
            te.wait_ge(ldw, 16 * (sidx + 1))
            if sidx == 0:
                te.wait_ge(vstat, 3)
            else:
                te.wait_ge(ea, sidx)
            for k in range(KC):
                st, sp = (k == 0), (k == KC - 1)
                for tg in range(4):
                    ins = te.matmul(
                        ps[tg][:, :],
                        wbuf[:, (sidx % 2) * 4096 + k * 128:(sidx % 2) * 4096 + (k + 1) * 128],
                        H[:, k * T + tg * 512:k * T + (tg + 1) * 512],
                        start=st, stop=sp)
            ins.then_inc(mmg, 1)

        for m in range(KC):
            gemm(m, ws)
        for h in range(NH if PH >= 2 else 0):
            for ci in range(3):
                gemm(KC + 3 * h + ci, ws)
            if sub >= 2.5:
                te.wait_ge(vq, 2 * h + 2)
                te.wait_ge(ea, KC + 3 * h + 3)
            for tt in range(32 if sub >= 2.5 else 0):
                src = ksc if tt < 16 else vh
                t0 = tt % 16
                TA = 32 * h + tt
                if TA >= 1:
                    te.wait_ge(vt, TA)
                te.transpose(p7b[:, (tt % 8) * 128:(tt % 8 + 1) * 128],
                             src[:, t0 * 128:(t0 + 1) * 128],
                             identb[:]).then_inc(mmt, 1)
            if sub >= 3:
                te.wait_ge(vt, 32 * (h + 1))
            for b_ in range(NB if sub >= 3 else 0):
                BA = NB * h + b_
                if BA >= 1:
                    te.wait_ge(vsd, BA)
                for jm in range(2):
                    ins = te.matmul(ps[4][:, jm * 256:(jm + 1) * 256],
                                    kh[:, b_ * 256 + jm * 128:b_ * 256 + (jm + 1) * 128],
                                    qh[:, b_ * 256:(b_ + 1) * 256],
                                    start=True, stop=True)
                ins.then_inc(ps1, 1)
                if sub == 3:
                    continue
                te.wait_ge(vsd, BA + 1)
                if BA >= 1:
                    te.wait_ge(eb, BA)
                if b_ == 0:
                    te.wait_ge(ldh, 49 * h + 49)
                for jm in range(2):
                    te.matmul(ps[5][:, 0:256],
                              vT[:, (b_ * 2 + jm) * 128:(b_ * 2 + jm + 1) * 128],
                              sdec[:, jm * 256:(jm + 1) * 256],
                              start=(jm == 0), stop=False)
                te.wait_ge(ek, BA) if b_ > 0 else None
                te.matmul(ps[5][:, 0:256], kvbf[:],
                          qsc[:, b_ * 256:(b_ + 1) * 256],
                          start=False, stop=True).then_inc(pi, 1)
                if BA >= 1:
                    te.wait_ge(vkv, BA)
                for jm in range(2):
                    ins = te.matmul(ps[6][:, 0:128],
                                    kT[:, (b_ * 2 + jm) * 128:(b_ * 2 + jm + 1) * 128],
                                    vT[:, (b_ * 2 + jm) * 128:(b_ * 2 + jm + 1) * 128],
                                    start=(jm == 0), stop=(jm == 1))
                ins.then_inc(pu, 1)
        for h in range(NH if PH >= 3 else 0):
            te.wait_ge(eki, h + 1)
            te.wait_ge(vq2, h + 1)
            if h >= 1:
                te.wait_ge(vce, h)
            for tg in range(4):
                ins = te.matmul(ps[tg][:, :], kvinb[:],
                                qs2[:, tg * 512:(tg + 1) * 512],
                                start=True, stop=True)
            ins.then_inc(mmc, 1)
        for k in range(KC if (PH >= 4 and sub >= 4.5) else 0):
            te.wait_ge(acz, k + 1)
            st, sp = (k == 0), (k == KC - 1)
            for tg in range(4):
                dst = ps[4 + tg][0:1, :] if tg < 3 else ps[6][32:33, :]
                st_ = st and tg != 3
                sp_ = sp and tg != 2
                ins = te.matmul(dst, onesb[:], sqb[:, tg * 512:(tg + 1) * 512],
                                start=st_, stop=sp_)
            ins.then_inc(pz, 1)
        if PH < 4 or sub < 4.8:
            return
        te.wait_ge(acz, KC + 1)
        te.wait_ge(vz, 2)
        for tg in range(4):
            te.matmul(ps[tg][:, :], onesfr[:],
                      srowA[0:1, tg * 512:(tg + 1) * 512],
                      start=True, stop=True).then_inc(pz, 1)
        for m in range(KC if sub >= 6 else 0):
            sidx = KC + 96 + m
            te.wait_ge(ldw, 16 * (sidx + 1))
            te.wait_ge(vz, KC + 2)
            if m >= 1:
                te.wait_ge(evu, m)
            for k in range(KC):
                st, sp = (k == 0), (k == KC - 1)
                for tg in range(4):
                    ins = te.matmul(
                        ps[tg][:, :],
                        ws2[:, (m % 2) * 4096 + k * 128:(m % 2) * 4096 + (k + 1) * 128],
                        H[:, k * T + tg * 512:k * T + (tg + 1) * 512],
                        start=st, stop=sp)
            ins.then_inc(mmg, 1)

    def SEC_vector(v):
        nvv = [0]

        def chain(ins):
            nvv[0] += 1
            ins.then_inc(vv, 1)
            v.wait_ge(vv, nvv[0])

        v.wait_ge(pes, KC)
        for tg in range(4):
            ins = v.tensor_copy(sra[0:1, tg * 512:(tg + 1) * 512], ps[tg][0:1, :])
        for tg in range(4):
            ins = v.tensor_copy(srb[0:1, tg * 512:(tg + 1) * 512],
                                ps[4 + tg][0:1, :] if tg < 3 else ps[6][32:33, :])
        chain(ins)
        ins = v.tensor_scalar(sra[0:1, :], sra[0:1, :], 1.0 / D, None, ALU.mult)
        ins = v.tensor_scalar(srb[0:1, :], srb[0:1, :], 1.0 / D, None, ALU.mult)
        chain(ins)
        ins = v.tensor_tensor(rows_mr[0:1, :], sra[0:1, :], sra[0:1, :], ALU.mult)
        chain(ins)
        v.tensor_tensor(srb[0:1, :], srb[0:1, :], rows_mr[0:1, :],
                        ALU.subtract).then_inc(vstat, 1)
        v.wait_ge(aa, 4)
        ins = v.reciprocal(rows_mr[0:1, :], rows_r[0:1, :])
        chain(ins)
        ins = v.tensor_tensor(sra[0:1, :], sra[0:1, :], rows_mr[0:1, :], ALU.mult)
        chain(ins)
        ins = v.tensor_copy(rows_r[0:1, :], rows_mr[0:1, :])
        ins = v.tensor_copy(rows_mr[0:1, :], sra[0:1, :])
        ins.then_inc(vstat, 1)
        for tg in range(4):
            v.wait_ge(bb, tg + 1)
            v.tensor_copy(r_bc[:, tg * 512:(tg + 1) * 512], ps[tg][:, :])
            ins = v.tensor_copy(mr_bc[:, tg * 512:(tg + 1) * 512],
                                ps[4 + tg][:, :] if tg < 3 else ps[6][:, :])
            if tg == 2:
                ins.then_inc(vbe, 1)
        ins.then_inc(vstat, 1)
        v.wait_ge(vstat, 3)
        for k in range(KC):
            v.wait_ge(ld, 16 * (KC + k + 1))
            if k >= 1:
                v.wait_ge(hb, k)
            v.tensor_tensor(t1[:], xf[:, (k % 2) * T:(k % 2 + 1) * T],
                            r_bc[:], ALU.mult).then_inc(vb, 1)
        for h in range((NH if PH >= 2 else 0) if sub >= 2 else 0):
            v.wait_ge(ea, KC + 3 * h + 2)
            v.wait_ge(ldh, 49 * h + 48)
            if h >= 1:
                if sub >= 3:
                    v.wait_ge(pi, NB * h)
                if sub >= 2.5:
                    v.wait_ge(mmt, 32 * h)
            ins = v.tensor_tensor(
                qsc[:, :], qh[:, :],
                AP(qdbc, 0, [[256, 128], [0, 8], [1, 256]]),
                ALU.mult)
            ins.then_inc(vq, 1)
            ins = v.tensor_tensor(
                ksc[:, :], kh[:, :],
                AP(kdbc, 0, [[256, 128], [0, 8], [1, 256]]),
                ALU.mult)
            ins.then_inc(vq, 1)
            for tt in range(32 if sub >= 2.5 else 0):
                TA = 32 * h + tt
                v.wait_ge(mmt, TA + 1)
                dst = kT if tt < 16 else vT
                t0 = tt % 16
                v.tensor_copy(dst[:, t0 * 128:(t0 + 1) * 128],
                              p7b[:, (tt % 8) * 128:(tt % 8 + 1) * 128]
                              ).then_inc(vt, 1)
            for b_ in range(NB if sub >= 3 else 0):
                BA = NB * h + b_
                v.wait_ge(ps1, BA + 1)
                ins = v.tensor_tensor(sdec[:, 0:256], ps[4][:, 0:256],
                                      ddh[:, 0:256], ALU.mult)
                ins = v.tensor_tensor(sdec[:, 256:512], ps[4][:, 256:512],
                                      ddh[:, 256:512], ALU.mult)
                ins.then_inc(vsd, 1)
                if sub == 3:
                    continue
                v.wait_ge(pu, BA + 1)
                v.scalar_tensor_tensor(
                    kv[:], kv[:], bdp_s[:, 9 * h + 1:9 * h + 2], ps[6][:, 0:128],
                    ALU.mult, ALU.add).then_inc(vkv, 1)
        for h in range(NH if PH >= 3 else 0):
            v.wait_ge(ldsg, 80 * h + 64)
            if h >= 1:
                v.wait_ge(eki, h)
            ins = v.tensor_scalar(kvin[:], sgb[:, 0:128], cw[:, 4 * h:4 * h + 1],
                                  None, ALU.mult)
            chain(ins)
            for cp in range(1, 4):
                ins = v.scalar_tensor_tensor(
                    kvin[:], sgb[:, cp * 128:(cp + 1) * 128],
                    cw[:, 4 * h + cp:4 * h + cp + 1], kvin[:],
                    ALU.mult, ALU.add)
                if cp < 3:
                    chain(ins)
            nvv[0] += 1
            ins.then_inc(vv, 1)
            v.wait_ge(vv, nvv[0])
            v.engine_nop().then_inc(vci, 1)
            v.wait_ge(ldsg, 80 * h + 80)
            v.wait_ge(ldh, 49 * NH + 16 * (h + 1))
            if h >= 1:
                v.wait_ge(mmc, h)
            for b_ in range(NB):
                ins = v.scalar_tensor_tensor(
                    qs2[:, b_ * 256:(b_ + 1) * 256],
                    qh[:, b_ * 256:(b_ + 1) * 256],
                    bdp_s[:, 9 * h + b_:9 * h + b_ + 1],
                    AP(qdbc, 0, [[256, 128], [1, 256]]),
                    ALU.mult, ALU.mult)
            ins.then_inc(vq2, 1)
        if PH < 4 or sub < 4.5:
            return
        v.wait_ge(pz, KC)
        for tg in range(4):
            ins = v.tensor_copy(srowA[0:1, tg * 512:(tg + 1) * 512],
                                ps[4 + tg][0:1, :] if tg < 3 else ps[6][32:33, :])
        chain(ins)
        v.tensor_scalar(srowA[0:1, :], srowA[0:1, :], 1.0 / D, None,
                        ALU.mult).then_inc(vz, 1)
        if sub < 4.7:
            return
        v.wait_ge(acz, KC + 1)
        ins = v.reciprocal(srowA[0:1, :], xn[0:1, :])
        chain(ins)
        v.engine_nop().then_inc(vz, 1)
        if sub < 5:
            return
        for tg in range(4):
            v.wait_ge(pz, KC + tg + 1)
            ins = v.tensor_copy(rstd_bc[:, tg * 512:(tg + 1) * 512],
                                ps[tg][:, :])
        chain(ins)
        for k in range(KC if sub >= 6 else 0):
            v.wait_ge(ldz, 16 * KC + 32 * (k + 1))
            ins = v.tensor_tensor(xn[:],
                                  af[:, :],
                                  rstd_bc[:], ALU.mult)
            chain(ins)
            ins = v.tensor_scalar(xn[:], xn[:], nwcol[:, k:k + 1], None, ALU.mult)
            chain(ins)
            v.tensor_tensor(H[:, k * T:(k + 1) * T], xn[:], gb[:],
                            ALU.mult).then_inc(vz, 1)
        for m in range(KC if sub >= 6 else 0):
            v.wait_ge(mmg, KC + 96 + m + 1)
            v.wait_ge(ldz, 16 * KC + 32 * KC + 16 * (m + 1))
            if m >= 1:
                v.wait_ge(stm, 16 * m)
            for tg in range(4):
                ins = v.scalar_tensor_tensor(
                    xn[:, tg * 512:(tg + 1) * 512], ps[tg][:, :], 1.0,
                    xfp[:, (m % 2) * T + tg * 512:(m % 2) * T + (tg + 1) * 512],
                    ALU.mult, ALU.add)
            chain(ins)
            v.engine_nop().then_inc(evu, 1)


    return ctx.finish("attn", dict(sync=SEC_sync, gpsimd=SEC_gpsimd, scalar=SEC_scalar, tensor=SEC_tensor, vector=SEC_vector),
                      finals=[(stm, 16 * KC)])



def build_prepl1(fx=None):
    ctx = fx if fx is not None else Ctx()
    nc = ctx.nc
    qzs = {
        "q1": (ctx.dram("q1_qz", [32, 512], I32, kind="ExternalInput"),
               ctx.dram("q1_sc", [32, 4096], F32, kind="ExternalInput"),
               ctx.dram("zs_q1", [32, 4096], F32, kind="ExternalOutput"), 32, 4096),
        "o1": (ctx.dram("o1_qz", [32, 512], I32, kind="ExternalInput"),
               ctx.dram("o1_sc", [32, 4096], F32, kind="ExternalInput"),
               ctx.dram("zs_o1", [32, 4096], F32, kind="ExternalOutput"), 32, 4096),
        "gu": (ctx.dram("gu_qz", [32, 2752], I32, kind="ExternalInput"),
               ctx.dram("gu_sc", [32, 22016], F32, kind="ExternalInput"),
               ctx.dram("zs_gu", [32, 22016], F32, kind="ExternalOutput"), 32, 22016),
        "dn": (ctx.dram("dn_qz", [86, 512], I32, kind="ExternalInput"),
               ctx.dram("dn_sc", [86, 4096], F32, kind="ExternalInput"),
               ctx.dram("zs_dn", [86, 4096], F32, kind="ExternalOutput"), 86, 4096),
    }
    sh1 = ctx.dram("shift_l1z", [86, 8], I32, kind="ExternalInput")

    zrow = ctx.sbuf("zrow", [86, 2752], I32)
    zint = ctx.sbuf("zint", [86, 4096], I32)
    zf = ctx.sbuf("zf", [86, 4096], F32)
    snat = ctx.sbuf("snat", [86, 4096], F32)
    shz = ctx.sbuf("shz", [86, 8], I32)
    ld = ctx.sem("ld")
    vz = ctx.sem("vz")
    stm = ctx.sem("stm")

    def SEC_gpsimd(g):
        g.dma_start(out=shz[:], in_=sh1[:, :]).then_inc(ld, 16)
        nvz = 0
        for name, (qz, sc, zs, G, N) in qzs.items():
            for j in range((N + 4095) // 4096):
                n0, n1 = j * 4096, min((j + 1) * 4096, N)
                w0, w1 = n0 // 8, n1 // 8
                g.wait_ge(vz, nvz)
                g.dma_start(out=zrow[0:G, 0:w1 - w0], in_=qz[0:G, w0:w1]
                            ).then_inc(ld, 16)
                g.dma_start(out=snat[0:G, 0:n1 - n0], in_=sc[0:G, n0:n1]
                            ).then_inc(ld, 16)
                nvz += 3

    def SEC_vector(v):
        cld = 16
        cvz = 0
        for name, (qz, sc, zs, G, N) in qzs.items():
            for j in range((N + 4095) // 4096):
                n0, n1 = j * 4096, min((j + 1) * 4096, N)
                nw = (n1 - n0) // 8
                cld += 32
                v.wait_ge(ld, cld)
                v.tensor_tensor(
                    AP(zint, 0, [[4096, G], [8, nw], [1, 8]]),
                    AP(zrow, 0, [[2752, G], [1, nw], [0, 8]]),
                    AP(shz, 0, [[8, G], [0, nw], [1, 8]]),
                    ALU.logical_shift_right).then_inc(vz, 1)
                cvz += 1
                v.wait_ge(vz, cvz)
                v.tensor_scalar(zint[0:G, 0:n1 - n0], zint[0:G, 0:n1 - n0], 15,
                                None, ALU.bitwise_and).then_inc(vz, 1)
                cvz += 1
                v.wait_ge(vz, cvz)
                v.tensor_tensor(zf[0:G, 0:n1 - n0], zint[0:G, 0:n1 - n0],
                                snat[0:G, 0:n1 - n0], ALU.mult).then_inc(vz, 1)
                cvz += 1

    def SEC_sync(s):
        cvz = 0
        cst = 0
        for name, (qz, sc, zs, G, N) in qzs.items():
            for j in range((N + 4095) // 4096):
                n0, n1 = j * 4096, min((j + 1) * 4096, N)
                cvz += 3
                s.wait_ge(vz, cvz)
                s.dma_start(out=zs[0:G, n0:n1], in_=zf[0:G, 0:n1 - n0]
                            ).then_inc(stm, 16)
                cst += 16
        s.wait_ge(stm, cst)


    return ctx.finish("prepl1", dict(gpsimd=SEC_gpsimd, vector=SEC_vector, sync=SEC_sync),
                      finals=[(stm, 16 * sum((N + 4095) // 4096 for (_, _, _, _, N) in qzs.values()))])



_CACHE = {}


def _get(name, fn, *args):
    key = (name,) + args
    if key not in _CACHE:
        _CACHE[key] = fn(*args)
    return _CACHE[key]


def kernel_unfused(**inputs):
    x = np.asarray(inputs["L0_attn_hidden_states"])
    c = host_prepare(inputs)
    xs = shard_x(x)
    cores = list(range(NCORES))

    nc = _get("init", build_init)
    res = run_bass_kernel_spmd(nc, [dict(
        x_tm=xs[i], ident=c["ident"], qz0=c["qz0"], s0_nat=c["s0_nat"],
        shift_l0z=c["shift_l0z"]) for i in cores], core_ids=cores)
    xfm = [r["x_fm"] for r in res.results]
    zs0 = [r["zs0"] for r in res.results]

    nc = _get("s1", build_ffn0, True)
    res = run_bass_kernel_spmd(nc, [dict(
        x_fm=xfm[i], bias0_col=c["bias0_col"], w0q_blk=c["w0q_blk"],
        s0_blk=c["s0_blk"], zs0=zs0[i], shift_l0=c["shift_l0"]) for i in cores],
        core_ids=cores)
    xfm = [r["x_fm_o"] for r in res.results]
    w0deq = [r["w0deq"] for r in res.results]

    nc = _get("s2", build_attn)
    res = run_bass_kernel_spmd(nc, [dict(
        x_fm=xfm[i], qkv_blk=c["qkv_blk"], gate_blk=c["gate_blk"],
        out_blk=c["out_blk"], normw_col=c["normw_col"], qd_bc=c["qd_bc"],
        kd_bc=c["kd_bc"], dd_T=c["dd_T"], cw_d=c["cw_percore"][i],
        bd_pows=c["bd_pows"]) for i in cores], core_ids=cores)
    xfm = [r["x_fm_o"] for r in res.results]

    nc = _get("s3", build_ffn0, False)
    res = run_bass_kernel_spmd(nc, [dict(
        x_fm=xfm[i], bias0_col=c["bias0_col"], w0deq=w0deq[i]) for i in cores],
        core_ids=cores)
    xfm = [r["x_fm_o"] for r in res.results]

    nc = _get("s4", build_molora)
    res = run_bass_kernel_spmd(nc, [dict(
        x_fm=xfm[i], wb_blk=c["wb_blk"], bb_col=c["bb_col"], A_blk2=c["A_blk2"],
        Wg_blk2=c["Wg_blk2"], B_blk=c["B_blk"], ident=c["ident"]) for i in cores],
        core_ids=cores)
    xfm = [r["x_fm_o"] for r in res.results]

    nc = _get("prepl1", build_prepl1)
    res = run_bass_kernel_spmd(nc, [dict(
        q1_qz=c["q1_qz"], q1_sc=c["q1_sc"], o1_qz=c["o1_qz"], o1_sc=c["o1_sc"],
        gu_qz=c["gu_qz"], gu_sc=c["gu_sc"], dn_qz=c["dn_qz"], dn_sc=c["dn_sc"],
        shift_l1z=c["shift_l1z"]) for i in cores], core_ids=cores)
    zsd = [{k: res.results[i][k] for k in ("zs_q1", "zs_o1", "zs_gu", "zs_dn")}
           for i in cores]

    nc = _get("s5", build_llama)
    res = run_bass_kernel_spmd(nc, [dict(
        x_fm=xfm[i], q1_blk=c["q1_blk"], o1_blk=c["o1_blk"], gu_blk=c["gu_blk"],
        dn_blk=c["dn_blk"], q1_sblk=c["q1_sblk"], o1_sblk=c["o1_sblk"],
        gu_sblk=c["gu_sblk"], dn_sblk=c["dn_sblk"], zs_q1=zsd[i]["zs_q1"],
        zs_o1=zsd[i]["zs_o1"], zs_gu=zsd[i]["zs_gu"], zs_dn=zsd[i]["zs_dn"],
        shift_l1=c["shift_l1"], ln1_col=c["ln1_col"], ln2_col=c["ln2_col"])
        for i in cores], core_ids=cores)
    xfm = [r["x_fm_o"] for r in res.results]

    nc = _get("s6", build_molora)
    res = run_bass_kernel_spmd(nc, [dict(
        x_fm=xfm[i], wb_blk=c["wb_blk"], bb_col=c["bb_col"], A_blk2=c["A_blk2"],
        Wg_blk2=c["Wg_blk2"], B_blk=c["B_blk"], ident=c["ident"]) for i in cores],
        core_ids=cores)
    xfm = [r["x_fm_o"] for r in res.results]

    nc = _get("fin", build_fin)
    res = run_bass_kernel_spmd(nc, [dict(
        x_fm=xfm[i], ident=c["ident"]) for i in cores], core_ids=cores)
    shards = [r["x_tm"] for r in res.results]

    out = np.empty((2, 8192, D), dtype=np.float32)
    for cid in range(NCORES):
        b, q = cid // 4, cid % 4
        out[b, q * T:(q + 1) * T, :] = shards[cid]
    return out


def build_fused():
    fx = FusedCtx()
    build_init(fx=fx)
    build_ffn0(True, fx=fx)
    build_attn(fx=fx)
    build_ffn0(False, fx=fx)
    build_molora(fx=fx)
    build_prepl1(fx=fx)
    build_llama(fx=fx)
    build_molora(fx=fx)
    build_fin(fx=fx)
    return fx.build()


def kernel(**inputs):
    x = np.asarray(inputs["L0_attn_hidden_states"])
    c = host_prepare(inputs)
    xs = shard_x(x)
    cores = list(range(NCORES))
    nc = _get("fused", build_fused)
    res = run_bass_kernel_spmd(nc, [dict(
        x_tm=xs[i], ident=c["ident"], qz0=c["qz0"], s0_nat=c["s0_nat"],
        shift_l0z=c["shift_l0z"], bias0_col=c["bias0_col"],
        w0q_blk=c["w0q_blk"], s0_blk=c["s0_blk"], shift_l0=c["shift_l0"],
        qkv_blk=c["qkv_blk"], gate_blk=c["gate_blk"], out_blk=c["out_blk"],
        normw_col=c["normw_col"], qd_bc=c["qd_bc"], kd_bc=c["kd_bc"],
        dd_T=c["dd_T"], cw_d=c["cw_percore"][i], bd_pows=c["bd_pows"],
        wb_blk=c["wb_blk"], bb_col=c["bb_col"], A_blk2=c["A_blk2"],
        Wg_blk2=c["Wg_blk2"], B_blk=c["B_blk"],
        q1_qz=c["q1_qz"], q1_sc=c["q1_sc"], o1_qz=c["o1_qz"], o1_sc=c["o1_sc"],
        gu_qz=c["gu_qz"], gu_sc=c["gu_sc"], dn_qz=c["dn_qz"], dn_sc=c["dn_sc"],
        shift_l1z=c["shift_l1z"], q1_blk=c["q1_blk"], o1_blk=c["o1_blk"],
        gu_blk=c["gu_blk"], dn_blk=c["dn_blk"], q1_sblk=c["q1_sblk"],
        o1_sblk=c["o1_sblk"], gu_sblk=c["gu_sblk"], dn_sblk=c["dn_sblk"],
        shift_l1=c["shift_l1"], ln1_col=c["ln1_col"], ln2_col=c["ln2_col"])
        for i in cores], core_ids=cores)
    shards = [r["x_out_tm"] for r in res.results]
    out = np.empty((2, 8192, D), dtype=np.float32)
    for cid in range(NCORES):
        b, q = cid // 4, cid % 4
        out[b, q * T:(q + 1) * T, :] = shards[cid]
    return out
